# version 07
# baseline (speedup 1.0000x reference)
import sys
sys.path.insert(0, "/opt/trn_rl_repo")
from contextlib import ExitStack

import numpy as np
import ml_dtypes

import concourse.bass as bass
import concourse.mybir as mybir
from concourse.bass_utils import run_bass_kernel_spmd

F32 = mybir.dt.float32
BF16 = mybir.dt.bfloat16
AF = mybir.ActivationFunctionType
ALU = mybir.AluOpType
AX = mybir.AxisListType.X

NC = 8
S = 8192
D = 2048
T = S // NC
C = D // 128
H = 16
HD = 128
HALO = 256
TH = T + HALO
INTER = 5632
IC = INTER // 128
ATT_SCALE = float(HD) ** -0.5
MB = -25.0
BIG = 1e9
LN_EPS = 1e-5


class Ctr:
    __slots__ = ("h", "n")

    def __init__(self, h):
        self.h = h
        self.n = 0

    def bump(self, inst, k=1):
        inst.then_inc(self.h, k)
        self.n += k
        return self.n

    def wait(self, eng, val=None):
        eng.wait_ge(self.h, self.n if val is None else val)


class KB:

    def __init__(self, name):
        self.nc = bass.Bass("TRN2", target_bir_lowering=False, name=name)
        self.ctx = ExitStack()
        self.stack = [self.ctx]
        self.sem_track = [[]]
        self.prefix = ""

    def sb(self, name, shape, dt, **kw):
        return self.stack[-1].enter_context(
            self.nc.sbuf_tensor(self.prefix + name, shape, dt, **kw))

    def ps(self, name, shape, dt=F32):
        return self.stack[-1].enter_context(
            self.nc.psum_tensor(self.prefix + name, shape, dt))

    def sem(self, name):
        h = self.stack[-1].enter_context(self.nc.semaphore(self.prefix + name))
        c = Ctr(h)
        self.sem_track[-1].append(c)
        return c

    def sem_outer(self, name):
        h = self.ctx.enter_context(self.nc.semaphore(self.prefix + name))
        c = Ctr(h)
        self.sem_track[0].append(c)
        return c

    def dram(self, name, shape, dt, kind="Internal"):
        return self.nc.dram_tensor(name, shape, dt, kind=kind).ap()

    def phase(self, tag):
        from contextlib import contextmanager

        @contextmanager
        def _p():
            es = ExitStack()
            self.stack.append(es)
            self.sem_track.append([])
            old = self.prefix
            self.prefix = tag + "_"
            try:
                yield
            finally:
                es.close()
                self.stack.pop()
                self.sem_track.pop()
                self.prefix = old
        return _p()

    def block(self):
        return self.ctx.enter_context(self.nc.Block())


def cbf(a):
    return np.ascontiguousarray(a).astype(ml_dtypes.bfloat16)


def shard_T(x2d):
    return [np.ascontiguousarray(x2d[i * T:(i + 1) * T]) for i in range(NC)]



def emit_identity(g, ident, s_gp):
    s_gp.bump(g.memset(ident[:, :], 0.0))
    g.wait_ge(s_gp.h, s_gp.n)
    s_gp.bump(g.affine_select(out=ident[:, :], in_=ident[:, :],
                              compare_op=ALU.not_equal, fill=1.0, base=0,
                              pattern=[[-1, ident.shape[1]]], channel_multiplier=1))


def emit_ln(blk, kb, xts, xnbf, Tn, s_x, s_xn, psA, psB, psC, psD,
            xsq, lnrows, negMb, Rb, lnscr, ones_bf, onesrow_bf, epsb,
            var_row_out=None, tag="", s_gp=None, pre_bcast_waits=()):
    nc = kb.nc
    s_sq = kb.sem("lnsq" + tag)
    s_t2 = kb.sem("lnt2" + tag)
    s_vx = kb.sem("lnvx" + tag)
    s_sc = kb.sem("lnsc" + tag)
    nhalf = (Tn + 511) // 512
    xbh = xsq

    @blk.scalar
    def _(sc):
        s_x.wait(sc)
        for c in range(C):
            if c >= 1:
                s_t2.wait(sc, c)
            s_sq.bump(sc.activation(out=xbh[:, 0:Tn],
                                    in_=xts[:, c * Tn:(c + 1) * Tn], func=AF.Identity))
            s_sq.bump(sc.activation(out=xbh[:, Tn:2 * Tn],
                                    in_=xts[:, c * Tn:(c + 1) * Tn], func=AF.Square))

    @blk.tensor
    def _(t):
        if s_gp is not None:
            s_gp.wait(t)
        for c in range(C):
            s_sq.wait(t, 2 * (c + 1))
            for hf in range(nhalf):
                w = min(512, Tn - hf * 512)
                t.matmul(psA[0:1, hf * 512:hf * 512 + w], lhsT=ones_bf[:, :],
                         rhs=xbh[:, hf * 512:hf * 512 + w],
                         start=(c == 0), stop=(c == C - 1))
            for hf in range(nhalf):
                w = min(512, Tn - hf * 512)
                inst = t.matmul(psB[0:1, hf * 512:hf * 512 + w], lhsT=ones_bf[:, :],
                                rhs=xbh[:, Tn + hf * 512:Tn + hf * 512 + w],
                                start=(c == 0), stop=(c == C - 1))
                if hf == nhalf - 1:
                    s_t2.bump(inst)

    NM, MS, VR, SD = 0, Tn, 2 * Tn, 3 * Tn
    HB = 4 * Tn
    lnrows_bf = lnrows[0:1, HB:HB + 2 * Tn].bitcast(BF16)

    @blk.vector
    def _(v):
        s_t2.wait(v, C)
        s_vx.bump(v.tensor_scalar(lnrows[0:1, NM:NM + Tn], psA[0:1, 0:Tn],
                                  -1.0 / D, None, ALU.mult))
        s_vx.bump(v.tensor_scalar(lnrows[0:1, MS:MS + Tn], psB[0:1, 0:Tn],
                                  1.0 / D, None, ALU.mult))
        v.wait_ge(s_vx.h, s_vx.n)
        s_vx.bump(v.tensor_tensor(lnrows[0:1, VR:VR + Tn], lnrows[0:1, NM:NM + Tn],
                                  lnrows[0:1, NM:NM + Tn], ALU.mult))
        v.wait_ge(s_vx.h, s_vx.n)
        s_vx.bump(v.tensor_tensor(lnrows[0:1, VR:VR + Tn], lnrows[0:1, MS:MS + Tn],
                                  lnrows[0:1, VR:VR + Tn], ALU.subtract))
        if var_row_out is not None:
            v.wait_ge(s_vx.h, s_vx.n)
            s_vx.bump(v.tensor_copy(out=var_row_out[0:1, 0:Tn],
                                    in_=lnrows[0:1, VR:VR + Tn]))

    @blk.scalar
    def _(sc):
        s_vx.wait(sc)
        s_sc.bump(sc.activation(out=lnrows[0:1, SD:SD + Tn], in_=lnrows[0:1, VR:VR + Tn],
                                func=AF.Sqrt, bias=epsb[0:1, 0:1]))

    @blk.vector
    def _(v):
        s_sc.wait(v)
        s_vx.bump(v.reciprocal(lnrows[0:1, VR:VR + Tn], lnrows[0:1, SD:SD + Tn]))
        for i, src in enumerate([NM, VR]):
            v.wait_ge(s_vx.h, s_vx.n)
            s_vx.bump(v.tensor_copy(out=lnrows_bf[0:1, (2 * i) * Tn:(2 * i) * Tn + Tn],
                                    in_=lnrows[0:1, src:src + Tn]))
            v.wait_ge(s_vx.h, s_vx.n)
            s_vx.bump(v.tensor_tensor(lnrows_bf[0:1, (2 * i + 1) * Tn:(2 * i + 2) * Tn],
                                      lnrows[0:1, src:src + Tn],
                                      lnrows_bf[0:1, (2 * i) * Tn:(2 * i) * Tn + Tn],
                                      ALU.subtract))

    s_bc = kb.sem("lnbc" + tag)
    s_bv = kb.sem("lnbv" + tag)

    @blk.tensor
    def _(t):
        s_vx.wait(t)
        for w_ in pre_bcast_waits:
            w_.wait(t)
        for i, dstT in enumerate([negMb, Rb]):
            if i == 1:
                s_bv.wait(t, nhalf)
            for hf in range(nhalf):
                w = min(512, Tn - hf * 512)
                dst = psC if hf % 2 == 0 else psD
                t.matmul(dst[:, 0:w], lhsT=onesrow_bf[0:1, :],
                         rhs=lnrows_bf[0:1, (2 * i) * Tn + hf * 512:(2 * i) * Tn + hf * 512 + w],
                         start=True, stop=False)
                s_bc.bump(t.matmul(dst[:, 0:w], lhsT=onesrow_bf[0:1, :],
                                   rhs=lnrows_bf[0:1, (2 * i + 1) * Tn + hf * 512:(2 * i + 1) * Tn + hf * 512 + w],
                                   start=False, stop=True))

    @blk.vector
    def _(v):
        for i, dstT in enumerate([negMb, Rb]):
            for hf in range(nhalf):
                w = min(512, Tn - hf * 512)
                src = psC if hf % 2 == 0 else psD
                s_bc.wait(v, i * nhalf + hf + 1)
                s_bv.bump(v.tensor_copy(out=dstT[:, hf * 512:hf * 512 + w], in_=src[:, 0:w]))
        v.wait_ge(s_bv.h, s_bv.n)
        s_x.wait(v)
        for c in range(C):
            sl = (c % 2) * Tn
            s_vx.bump(v.tensor_tensor(lnscr[:, sl:sl + Tn], xts[:, c * Tn:(c + 1) * Tn],
                            negMb[:, 0:Tn], ALU.add))
            v.wait_ge(s_vx.h, s_vx.n)
            s_xn.bump(v.tensor_tensor(xnbf[:, c * Tn:(c + 1) * Tn],
                                      lnscr[:, sl:sl + Tn], Rb[:, 0:Tn], ALU.mult))
            v.wait_ge(s_xn.h, s_xn.n)
    return {"s_bv": s_bv}


def emit_bias_load_transpose(blk, kb, bias_aps, btmp, bT, ident, psM, s_gp, tag=""):
    s_bl = kb.sem("bl" + tag)
    s_bt = kb.sem("bt" + tag)
    s_be = kb.sem("be" + tag)
    nch = [ap.shape[0] // 128 for ap in bias_aps]

    @blk.sync
    def _(sy):
        for i, ap in enumerate(bias_aps):
            n = nch[i]
            s_bl.bump(sy.dma_start(out=btmp[0:n, i * 128:i * 128 + 128],
                                   in_=ap.rearrange("(c f) -> c f", f=128)), 16)

    @blk.tensor
    def _(t):
        s_bl.wait(t)
        s_gp.wait(t)
        for i, ap in enumerate(bias_aps):
            n = nch[i]
            if i > 0:
                s_be.wait(t, i)
            s_bt.bump(t.transpose(psM[:, 0:n], btmp[0:n, i * 128:i * 128 + 128],
                                  ident[0:n, 0:n]))

    @blk.vector
    def _(v):
        col = 0
        for i, ap in enumerate(bias_aps):
            n = nch[i]
            s_bt.wait(v, i + 1)
            s_be.bump(v.tensor_copy(out=bT[:, col:col + n], in_=psM[:, 0:n]))
            col += n

    return s_be



def build_l0pre(stage=100, F=None):
    if F is None:
        kb = KB("l0pre")
        xT = kb.dram("xT", [D, T], F32, "ExternalInput")
        Ws = {n: kb.dram("W" + n, [D, D], F32, "ExternalInput") for n in "qkv"}
        bs = {n: kb.dram("b" + n, [D], F32, "ExternalInput") for n in "qkv"}
        Bs = {n: kb.dram("B" + n, [16, D], F32, "ExternalInput") for n in "qkv"}
        As = {n: kb.dram("A" + n, [16, D], F32, "ExternalInput") for n in "qkv"}
        outs = {n: kb.dram(n + "T", [D, T], BF16, "ExternalOutput") for n in "qkv"}
        xnT = kb.dram("xnT", [D, T], BF16, "ExternalOutput")
    else:
        kb = F["kb"]
        ext = F["ext"]
        xT = ext["xT"]
        Ws = {n: ext["W" + n] for n in "qkv"}
        bs = {n: ext["b" + n] for n in "qkv"}
        Bs = {n: ext["B" + n] for n in "qkv"}
        As = {n: ext["A" + n] for n in "qkv"}
        outs = {n: ext[n + "T"] for n in "qkv"}
        xnT = ext["xnT"]
    nc = kb.nc
    dbgkind = "ExternalOutput" if F is None else "Internal"
    dbg = kb.dram("dbg", [1, 4 * T], F32, dbgkind)
    dbg2 = kb.dram("dbg2", [128, 2 * T], F32, dbgkind)
    dbg3 = kb.dram("dbg3", [16, 3 * T], BF16, dbgkind)
    dbg4 = kb.dram("dbg4", [128, 48], F32, dbgkind)
    dbg5 = kb.dram("dbg5", [16, 3 * D], BF16, dbgkind)
    Wbf = {n: kb.dram("W" + n + "_bf", [D, D], BF16) for n in "qkv"}

    xts = kb.sb("xts", [128, C * T], F32)
    xnbf = kb.sb("xnbf", [128, C * T], BF16)
    xsq = kb.sb("xsq", [128, 2 * T], BF16)
    lnrows = kb.sb("lnrows", [1, 6 * T], F32)
    negMb = kb.sb("negMb", [128, T], F32)
    Rb = kb.sb("Rb", [128, T], F32)
    lnscr = kb.sb("lnscr", [128, 2 * T], F32)
    ones = kb.sb("ones", [128, 1], BF16)
    onesrow = kb.sb("onesrow", [1, 128], BF16)
    epsb = kb.sb("epsb", [1, 1], F32)
    ident = kb.sb("ident", [128, 128], F32)
    btmp = kb.sb("btmp", [16, 3 * 128], F32)
    bT = kb.sb("bT", [128, 3 * 16], F32)
    Ahat = kb.sb("Ahat", [16, 3 * D], BF16)
    btile = kb.sb("btile", [128, 3 * C * 16], BF16, is_transpose_dest=True)
    Bsb16 = kb.sb("Bsb16", [16, 3 * D], BF16)
    ident_bf = kb.sb("ident_bf", [128, 128], BF16)
    ubf = kb.sb("ubf", [16, 3 * T], BF16)
    wbuf = kb.sb("wbuf", [128, C * 512], BF16, is_transpose_dest=True)
    ost = kb.sb("ost", [128, 2 * 512], BF16)

    psA = kb.ps("psA", [128, 1024])
    psB = kb.ps("psB", [128, 1024])
    psC = kb.ps("psC", [128, 512])
    psD = kb.ps("psD", [128, 512])
    psU = kb.ps("psU", [16, 512])
    psM = kb.ps("psM", [128, 128])

    s_x = kb.sem("s_x")
    s_gp = kb.sem("s_gp")
    s_castW = {n: kb.sem("s_castW" + n) for n in "qkv"}
    s_castB = {n: kb.sem("s_castB" + n) for n in "qkv"}
    s_xn = kb.sem("s_xn")
    s_ae = kb.sem("s_ae")
    s_bt = kb.sem("s_btl")
    s_btp = kb.sem("s_btp")
    s_ul = kb.sem("s_ul")
    s_ue = kb.sem("s_ue")
    s_wl = kb.sem("s_wl")
    s_wl2 = kb.sem("s_wl2")
    s_mm = kb.sem("s_mm")
    s_ev = kb.sem("s_ev")
    s_outp = [kb.sem("s_out0"), kb.sem("s_out1")]
    s_out = kb.sem("s_out")

    blk = kb.block() if F is None else F["blk"]

    @blk.gpsimd
    def _(g):
        s_gp.bump(g.memset(ones[:, :], 1.0))
        s_gp.bump(g.memset(onesrow[:, :], 1.0))
        s_gp.bump(g.memset(epsb[:, :], LN_EPS))
        emit_identity(g, ident, s_gp)
        for n in "qkv":
            s_castW[n].bump(g.dma_start(out=Wbf[n], in_=Ws[n]), 16)
        for i, n in enumerate("qkv"):
            s_castB[n].bump(g.dma_start(out=Bsb16[0:16, i * D:(i + 1) * D],
                                        in_=Bs[n]), 16)
        for i, n in enumerate("qkv"):
            s_ae.bump(g.dma_start(out=Ahat[0:16, i * D:(i + 1) * D], in_=As[n]), 16)

    @blk.sync
    def _(sy):
        s_x.bump(sy.dma_start(out=xts[:, :].rearrange("p (c t) -> p c t", t=T),
                              in_=xT.rearrange("(c p) t -> p c t", p=128)), 16)

    if stage >= 2:
        emit_ln(blk, kb, xts, xnbf, T, s_x, s_xn, psA, psB, psC, psD,
                xsq, lnrows, negMb, Rb, lnscr, ones, onesrow, epsb, tag="l0",
                s_gp=s_gp)


    if stage < 25:
        @blk.sync
        def _(sy):
            if stage >= 2:
                s_xn.wait(sy)
                s_out.bump(sy.dma_start(out=xnT.rearrange("(c p) t -> p c t", p=128),
                                        in_=xnbf[:, :].rearrange("p (c t) -> p c t", t=T)), 16)
                s_out.bump(sy.dma_start(out=dbg, in_=lnrows[0:1, 0:4*T]), 16)
                s_out.bump(sy.dma_start(out=dbg2[:, 0:T], in_=negMb[:, :]), 16)
                s_out.bump(sy.dma_start(out=dbg2[:, T:2*T], in_=Rb[:, :]), 16)
            s_castW["q"].wait(sy)
            s_castW["k"].wait(sy)
            s_castW["v"].wait(sy)
            s_castB["q"].wait(sy)
            s_castB["k"].wait(sy)
            s_castB["v"].wait(sy)
            sy.wait_ge(s_out.h, s_out.n)
        kb.ctx.close()
        return kb.nc

    if stage >= 25:
        s_be = emit_bias_load_transpose(blk, kb, [bs[n] for n in "qkv"],
                                        btmp, bT, ident, psM, s_gp, tag="l0")
    if stage == 25:
        @blk.sync
        def _(sy):
            s_be.wait(sy)
            s_out.bump(sy.dma_start(out=dbg4, in_=bT[:, :]), 16)
            sy.wait_ge(s_out.h, s_out.n)
        kb.ctx.close()
        return kb.nc

    if stage == 26:
        @blk.sync
        def _(sy):
            s_ae.wait(sy)
            s_out.bump(sy.dma_start(out=dbg5, in_=Ahat[:, :]), 16)
            sy.wait_ge(s_out.h, s_out.n)
        kb.ctx.close()
        return kb.nc

    @blk.vector
    def _(v):
        s_gp.wait(v)
        s_gp.bump(v.tensor_copy(out=ident_bf[:, :], in_=ident[:, :]))

    @blk.tensor
    def _(t):
        s_gp.wait(t)
        s_be.wait(t)
        for i, n in enumerate("qkv"):
            s_castB[n].wait(t)
            for c in range(C):
                k = i * C + c
                if k >= 1:
                    s_bt.wait(t, k)
                s_btp.bump(t.transpose(
                    psM[:, 0:128].bitcast(BF16)[0:128, 0:16],
                    Bsb16[0:16, i * D + c * 128:i * D + c * 128 + 128],
                    ident_bf[0:16, 0:16]))

    @blk.scalar
    def _(sc):
        for k in range(3 * C):
            s_btp.wait(sc, k + 1)
            s_bt.bump(sc.activation(
                out=btile[:, k * 16:k * 16 + 16],
                in_=psM[:, 0:128].bitcast(BF16)[0:128, 0:16],
                func=AF.Identity))

    if stage == 27:
        @blk.sync
        def _(sy):
            s_bt.wait(sy)
            s_out.bump(sy.dma_start(out=dbg3[0:16, 0:768].bitcast(BF16) if False else dbg3[0:16, 0:768], in_=btile[:, :][0:16, 0:768]), 16)
            sy.wait_ge(s_out.h, s_out.n)
        kb.ctx.close()
        return kb.nc

    @blk.tensor
    def _(t):
        s_xn.wait(t)
        s_bt.wait(t)
        s_be.wait(t)
        k = 0
        for i in range(3):
            for tp in range(2):
                if k >= 1:
                    s_ue.wait(t, k)
                for c in range(C):
                    inst = t.matmul(psU[:, 0:512],
                             lhsT=btile[:, (i * C + c) * 16:(i * C + c) * 16 + 16],
                             rhs=xnbf[:, c * T + tp * 512:c * T + tp * 512 + 512],
                             start=(c == 0), stop=(c == C - 1))
                s_ul.bump(inst)
                k += 1

    @blk.vector
    def _(v):
        k = 0
        for i in range(3):
            for tp in range(2):
                s_ul.wait(v, k + 1)
                s_ue.bump(v.tensor_scalar(ubf[0:16, i * T + tp * 512:i * T + tp * 512 + 512],
                                          psU[:, 0:512], 0.5, None, ALU.mult))
                k += 1

    if stage == 3:
        @blk.sync
        def _(sy):
            s_ue.wait(sy)
            s_be.wait(sy)
            s_ae.wait(sy)
            s_out.bump(sy.dma_start(out=dbg3, in_=ubf[:, :]), 16)
            s_out.bump(sy.dma_start(out=dbg4, in_=bT[:, :]), 16)
            s_out.bump(sy.dma_start(out=dbg5, in_=Ahat[:, :]), 16)
            sy.wait_ge(s_out.h, s_out.n)
        kb.ctx.close()
        return kb.nc

    PORD = "kvq" if F is not None else "qkv"

    @blk.sync
    def _(sy):
        for ip, n in enumerate(PORD):
            s_castW[n].wait(sy)
            for og in range(4):
                gidx = ip * 4 + og
                if gidx >= 1:
                    sy.wait_ge(s_mm.h, gidx * 8)
                for c in range(0, C, 2):
                    s_wl.bump(sy.dma_start(
                        out=wbuf[:, c * 512:c * 512 + 512],
                        in_=Wbf[n][c * 128:(c + 1) * 128, og * 512:og * 512 + 512]), 16)

    @blk.gpsimd
    def _(g):
        for ip, n in enumerate(PORD):
            s_castW[n].wait(g)
            for og in range(4):
                gidx = ip * 4 + og
                if gidx >= 1:
                    g.wait_ge(s_mm.h, gidx * 8)
                for c in range(1, C, 2):
                    s_wl2.bump(g.dma_start(
                        out=wbuf[:, c * 512:c * 512 + 512],
                        in_=Wbf[n][c * 128:(c + 1) * 128, og * 512:og * 512 + 512]), 16)

    @blk.tensor
    def _(t):
        s_ue.wait(t)
        s_ae.wait(t)
        s_be.wait(t)
        k = 0
        for ip, n in enumerate(PORD):
            io = "qkv".index(n)
            for og in range(4):
                gidx = ip * 4 + og
                t.wait_ge(s_wl.h, (gidx + 1) * 8 * 16)
                t.wait_ge(s_wl2.h, (gidx + 1) * 8 * 16)
                wb = wbuf
                for oc4 in range(4):
                    oc = og * 4 + oc4
                    for tp in range(2):
                        ps = psC if k % 2 == 0 else psD
                        if k >= 2:
                            s_ev.wait(t, k - 1)
                        for c in range(C):
                            t.matmul(ps[:, 0:512],
                                     lhsT=wb[:, c * 512 + oc4 * 128:c * 512 + oc4 * 128 + 128],
                                     rhs=xnbf[:, c * T + tp * 512:c * T + tp * 512 + 512],
                                     start=(c == 0), stop=False)
                        s_mm.bump(t.matmul(ps[:, 0:512],
                                 lhsT=Ahat[:, io * D + oc * 128:io * D + oc * 128 + 128],
                                 rhs=ubf[0:16, io * T + tp * 512:io * T + tp * 512 + 512],
                                 start=False, stop=True))
                        k += 1

    @blk.scalar
    def _(sc):
        k = 0
        for ip, n in enumerate(PORD):
            io = "qkv".index(n)
            for og in range(4):
                for oc4 in range(4):
                    oc = og * 4 + oc4
                    for tp in range(2):
                        ps = psC if k % 2 == 0 else psD
                        s_mm.wait(sc, k + 1)
                        if k >= 2:
                            sc.wait_ge(s_outp[k % 2].h, (k // 2) * 16)
                        s_ev.bump(sc.activation(
                            out=ost[:, (k % 2) * 512:(k % 2) * 512 + 512],
                            in_=ps[:, 0:512], func=AF.Identity,
                            bias=bT[:, io * 16 + oc:io * 16 + oc + 1]))
                        sc.wait_ge(s_ev.h, k + 1)
                        s_outp[k % 2].bump(sc.dma_start(
                            out=outs[n][oc * 128:(oc + 1) * 128, tp * 512:tp * 512 + 512],
                            in_=ost[:, (k % 2) * 512:(k % 2) * 512 + 512]), 16)
                        k += 1

    @blk.sync
    def _(sy):
        s_xn.wait(sy)
        s_out.bump(sy.dma_start(out=xnT.rearrange("(c p) t -> p c t", p=128),
                                in_=xnbf[:, :].rearrange("p (c t) -> p c t", t=T)), 16)
        sy.wait_ge(s_outp[0].h, s_outp[0].n)
        sy.wait_ge(s_outp[1].h, s_outp[1].n)
        sy.wait_ge(s_out.h, s_out.n)

    if F is not None:
        F["sems"]["l0pre_outp"] = s_outp
        F["sems"]["l0pre_xn"] = s_out
        return None
    kb.ctx.close()
    return kb.nc



def build_attn0(F=None):
    if F is None:
        kb = KB("attn0")
        qT = kb.dram("qT", [D, T], BF16, "ExternalInput")
        kTf = kb.dram("kTf", [D, S], BF16, "ExternalInput")
        vTf = kb.dram("vTf", [D, S], BF16, "ExternalInput")
        sT = kb.dram("sT", [D, T], BF16, "ExternalOutput")

        def kb_ap(h, r):
            return kTf[h * 128:(h + 1) * 128, r * 1024:(r + 1) * 1024]

        def vt_ap(h, kt):
            return vTf[h * 128:(h + 1) * 128, kt * 128:(kt + 1) * 128]
    else:
        kb = F["kb"]
        ext = F["ext"]
        qT = ext["qT"]
        kTg = ext["kTg"]
        vTg = ext["vTg"]
        sT = ext["sT"]

        def kb_ap(h, r):
            return kTg[r * D + h * 128:r * D + h * 128 + 128, 0:1024]

        def vt_ap(h, kt):
            r, j = kt // 8, kt % 8
            return vTg[r * D + h * 128:r * D + h * 128 + 128, j * 128:(j + 1) * 128]
    nc = kb.nc

    qsb = kb.sb("qsb", [128, H * T], BF16)
    kbuf = kb.sb("kbuf", [128, 3 * 1024], BF16)
    vbuf = kb.sb("vbuf", [128, 4 * 128], BF16, is_transpose_dest=True)
    Pt = kb.sb("Pt", [128, 2 * 1024], BF16)
    rows = kb.sb("rows", [1, 2 * 1024], F32)
    rbrows = kb.sb("rbrows", [1, 2 * 1024], BF16)
    rb = kb.sb("rb", [128, 1024], F32)
    stbuf = kb.sb("stbuf", [128, 2 * 1024], BF16)
    ones_bf = kb.sb("ones_bf", [128, 1], BF16)
    onesrow_bf = kb.sb("onesrow_bf", [1, 128], BF16)
    mbias = kb.sb("mbias", [128, 1], F32)

    psS = [kb.ps("psS0", [128, 1024]), kb.ps("psS1", [128, 1024])]
    psO = kb.ps("psO", [128, 1024])
    psD = kb.ps("psD", [128, 1024])

    s_q = kb.sem("s_q")
    s_gp = kb.sem("s_gp")
    s_kb = [kb.sem(f"s_kb{i}") for i in range(3)]
    s_vt = [kb.sem(f"s_vt{i}") for i in range(4)]
    s_sT = kb.sem("s_sT")
    s_exp = kb.sem("s_exp")
    s_pav = kb.sem("s_pav")
    s_dnr = kb.sem("s_dnr")
    s_bc = kb.sem("s_bc")
    s_rbe = kb.sem("s_rbe")
    s_dne = kb.sem("s_dne")
    s_se = kb.sem("s_se")
    s_vx = kb.sem("s_vx")
    s_outp = [kb.sem("s_out0"), kb.sem("s_out1")]

    blk = kb.block() if F is None else F["blk"]

    @blk.gpsimd
    def _(g):
        s_gp.bump(g.memset(ones_bf[:, :], 1.0))
        s_gp.bump(g.memset(onesrow_bf[:, :], 1.0))
        s_gp.bump(g.memset(mbias[:, :], MB))

    @blk.sync
    def _(sy):
        if F is not None:
            F["sems"]["l0pre_outp"][0].wait(sy)
            F["sems"]["l0pre_outp"][1].wait(sy)
        s_q.bump(sy.dma_start(out=qsb[:, :].rearrange("p (h t) -> p h t", t=T),
                              in_=qT.rearrange("(h p) t -> p h t", p=128)), 16)
        if F is not None:
            F["sems"]["cc_kv"].wait(sy)
        for h in range(H):
            for kt in range(64):
                if kt % 8 == 0:
                    r = kt // 8
                    i = h * 8 + r
                    if i >= 3:
                        rel = i - 3
                        sy.wait_ge(s_exp.h, (rel // 8) * 64 + ((rel % 8) + 1) * 8)
                    s_kb[i % 3].bump(sy.dma_start(
                        out=kbuf[:, (i % 3) * 1024:(i % 3) * 1024 + 1024],
                        in_=kb_ap(h, r)), 16)
                i = h * 64 + kt
                if i >= 4:
                    sy.wait_ge(s_pav.h, i - 3)
                s_vt[i % 4].bump(sy.dma_start(
                    out=vbuf[:, (i % 4) * 128:(i % 4) * 128 + 128],
                    in_=vt_ap(h, kt),
                    transpose=True), 16)

    @blk.tensor
    def _(t):
        s_q.wait(t)
        s_gp.wait(t)
        for h in range(H):
            for kt in range(65):
                g = h * 64 + kt
                if kt < 64:
                    r = kt // 8
                    if g >= 2:
                        t.wait_ge(s_exp.h, g - 1)
                    i_blk = h * 8 + r
                    t.wait_ge(s_kb[i_blk % 3].h, (i_blk // 3 + 1) * 16)
                    ko = ((h * 8 + r) % 3) * 1024 + (kt % 8) * 128
                    t.matmul(psS[g % 2][:, 0:512], lhsT=kbuf[:, ko:ko + 128],
                             rhs=qsb[:, h * T:h * T + 512], start=True, stop=True)
                    s_sT.bump(t.matmul(psS[g % 2][:, 512:1024], lhsT=kbuf[:, ko:ko + 128],
                             rhs=qsb[:, h * T + 512:h * T + 1024], start=True, stop=True))
                if kt >= 1:
                    ga = g - 1
                    kta = kt - 1
                    t.wait_ge(s_exp.h, ga + 1)
                    t.wait_ge(s_vt[ga % 4].h, (ga // 4 + 1) * 16)
                    if kta == 0 and h > 0:
                        t.wait_ge(s_se.h, 2 * h)
                        t.wait_ge(s_rbe.h, h)
                    st = (kta == 0)
                    sp = (kta == 63)
                    vsl = (ga % 4) * 128
                    psl = (ga % 2) * 1024
                    t.matmul(psO[:, 0:512], lhsT=vbuf[:, vsl:vsl + 128],
                             rhs=Pt[:, psl:psl + 512], start=st, stop=sp)
                    t.matmul(psO[:, 512:1024], lhsT=vbuf[:, vsl:vsl + 128],
                             rhs=Pt[:, psl + 512:psl + 1024], start=st, stop=sp)
                    t.matmul(psD[0:1, 0:512], lhsT=ones_bf[:, :],
                             rhs=Pt[:, psl:psl + 512], start=st, stop=sp)
                    s_pav.bump(t.matmul(psD[0:1, 512:1024], lhsT=ones_bf[:, :],
                             rhs=Pt[:, psl + 512:psl + 1024], start=st, stop=sp))
            t.wait_ge(s_dnr.h, h + 1)
            for hf in range(2):
                t.matmul(psD[:, hf * 512:hf * 512 + 512], lhsT=onesrow_bf[0:1, :],
                         rhs=rbrows[0:1, hf * 512:hf * 512 + 512], start=True, stop=False)
                s_bc.bump(t.matmul(psD[:, hf * 512:hf * 512 + 512], lhsT=onesrow_bf[0:1, :],
                         rhs=rbrows[0:1, 1024 + hf * 512:1024 + hf * 512 + 512],
                         start=False, stop=True))

    @blk.scalar
    def _(sc):
        for h in range(H):
            for kt in range(64):
                g = h * 64 + kt
                sc.wait_ge(s_sT.h, g + 1)
                if g >= 2:
                    sc.wait_ge(s_pav.h, g - 1)
                psl = (g % 2) * 1024
                s_exp.bump(sc.activation(out=Pt[:, psl:psl + 1024],
                                         in_=psS[g % 2][:, :], func=AF.Exp,
                                         bias=mbias[:, :], scale=ATT_SCALE))
            if h >= 1:
                sc.wait_ge(s_se.h, 2 * h)
                s_outp[(h - 1) % 2].bump(sc.dma_start(
                    out=sT[(h - 1) * 128:h * 128, :],
                    in_=stbuf[:, ((h - 1) % 2) * 1024:((h - 1) % 2) * 1024 + 1024]), 16)

    @blk.vector
    def _(v):
        for h in range(H):
            v.wait_ge(s_pav.h, (h + 1) * 64)
            s_dne.bump(v.tensor_copy(out=rows[0:1, 0:1024], in_=psD[0:1, :]))
            v.wait_ge(s_dne.h, h + 1)
            s_vx.bump(v.reciprocal(rows[0:1, 1024:2048], rows[0:1, 0:1024]))
            v.wait_ge(s_vx.h, s_vx.n)
            s_vx.bump(v.tensor_copy(out=rbrows[0:1, 0:1024], in_=rows[0:1, 1024:2048]))
            v.wait_ge(s_vx.h, s_vx.n)
            s_dnr.bump(v.tensor_tensor(rbrows[0:1, 1024:2048], rows[0:1, 1024:2048],
                                       rbrows[0:1, 0:1024], ALU.subtract))
            v.wait_ge(s_bc.h, (h + 1) * 2)
            s_vx.bump(v.tensor_copy(out=rb[:, 0:512], in_=psD[:, 0:512]))
            s_rbe.bump(v.tensor_copy(out=rb[:, 512:1024], in_=psD[:, 512:1024]))
            v.wait_ge(s_rbe.h, h + 1)
            if h >= 2:
                v.wait_ge(s_outp[h % 2].h, (h // 2) * 16)
            ssl = (h % 2) * 1024
            s_se.bump(v.tensor_tensor(stbuf[:, ssl:ssl + 512], psO[:, 0:512],
                                      rb[:, 0:512], ALU.mult))
            s_se.bump(v.tensor_tensor(stbuf[:, ssl + 512:ssl + 1024], psO[:, 512:1024],
                                      rb[:, 512:1024], ALU.mult))

    @blk.scalar
    def _(sc):
        sc.wait_ge(s_se.h, 2 * H)
        s_outp[(H - 1) % 2].bump(sc.dma_start(
            out=sT[(H - 1) * 128:H * 128, :],
            in_=stbuf[:, ((H - 1) % 2) * 1024:((H - 1) % 2) * 1024 + 1024]), 16)
        sc.wait_ge(s_outp[0].h, s_outp[0].n)
        sc.wait_ge(s_outp[1].h, s_outp[1].n)

    if F is not None:
        F["sems"]["attn0_outp"] = s_outp
        return None
    kb.ctx.close()
    return kb.nc





def build_l0post(F=None):
    if F is None:
        kb = KB("l0post")
        xT = kb.dram("xT", [D, T], F32, "ExternalInput")
        xnT = kb.dram("xnT", [D, T], BF16, "ExternalInput")
        sT = kb.dram("sT", [D, T], BF16, "ExternalInput")
        Wo = kb.dram("Wo", [D, D], F32, "ExternalInput")
        bo = kb.dram("bo", [D], F32, "ExternalInput")
        Bo = kb.dram("Bo", [16, D], F32, "ExternalInput")
        Ao = kb.dram("Ao", [16, D], F32, "ExternalInput")
        gate_w = kb.dram("gate_w", [D, INTER], F32, "ExternalInput")
        up_w = kb.dram("up_w", [D, INTER], F32, "ExternalInput")
        down_w = kb.dram("down_w", [INTER, D], F32, "ExternalInput")
        gate_s = kb.dram("gate_s", [INTER], F32, "ExternalInput")
        up_s = kb.dram("up_s", [INTER], F32, "ExternalInput")
        down_s = kb.dram("down_s", [D], F32, "ExternalInput")
        x1T = kb.dram("x1T", [D, T], F32, "ExternalOutput")
    else:
        kb = F["kb"]
        ext = F["ext"]
        xT = ext["xT"]
        xnT = ext["xnT"]
        sT = ext["sT"]
        Wo, bo, Bo, Ao = ext["Wo"], ext["bo"], ext["Bo"], ext["Ao"]
        gate_w, up_w, down_w = ext["gate_w"], ext["up_w"], ext["down_w"]
        gate_s, up_s, down_s = ext["gate_s"], ext["up_s"], ext["down_s"]
        x1T = ext["x1T"]
    nc = kb.nc
    if F is None:
        Wo_bf = kb.dram("Wo_bf", [D, D], BF16)
        Bo_bf = None
        gate_bf = kb.dram("gate_bf", [D, INTER], BF16)
        up_bf = kb.dram("up_bf", [D, INTER], BF16)
        down_bf = kb.dram("down_bf", [INTER, D], BF16)
    else:
        Wo_bf, Bo_bf = ext["Wo_bf"], None
        gate_bf, up_bf, down_bf = ext["gate_bf"], ext["up_bf"], ext["down_bf"]

    xnbf = kb.sb("xnbf", [128, C * T], BF16)
    ssb = kb.sb("ssb", [128, C * T], BF16)
    htb = kb.sb("htb", [128, IC * 512], BF16)
    wdbuf = kb.sb("wdbuf", [128, IC * 512], BF16, is_transpose_dest=True)
    wobuf = kb.sb("wobuf", [128, C * 512], BF16, is_transpose_dest=True)
    gs_scr = kb.sb("gs_scr", [128, 4 * 512], F32)
    gscr = kb.sb("gscr", [128, 512], F32)
    oscr = kb.sb("oscr", [128, 512], F32)
    x1sb = kb.sb("x1sb", [128, 2 * 512], F32)
    xstage = kb.sb("xstage", [128, 4 * 512], F32)
    ident = kb.sb("ident", [128, 128], F32)
    btmp = kb.sb("btmp", [44, 4 * 128], F32)
    bT = kb.sb("bT", [128, 120], F32)
    Aohat = kb.sb("Aohat", [16, D], BF16)
    botile = kb.sb("botile", [128, C * 16], BF16, is_transpose_dest=True)
    Bosb16 = kb.sb("Bosb16", [16, D], BF16)
    ident_bf = kb.sb("ident_bf", [128, 128], BF16)
    uo_bf = kb.sb("uo_bf", [16, T], BF16)

    psF = [kb.ps("psF%d" % j, [128, 512]) for j in range(4)]
    psX = kb.ps("psX", [128, 512])
    psY = kb.ps("psY", [128, 512])
    psU = kb.ps("psU", [16, 512])
    psM = kb.ps("psM", [128, 128])

    s_ld = kb.sem("s_ld")
    s_gp = kb.sem("s_gp")
    s_ae = kb.sem("s_ae")
    s_bt = kb.sem("s_btl")
    s_btp = kb.sem("s_btp")
    s_ul = kb.sem("s_ul")
    s_ue = kb.sem("s_ue")
    if F is None:
        cast = {n: kb.sem("cast_" + n) for n in ["Wo", "Bo", "gate", "up", "down"]}
    else:
        cast = {n: F["cast"][n] for n in ["Wo", "Bo", "gate", "up", "down"]}
    s_wf = [kb.sem("s_wf%d" % j) for j in range(4)]
    s_wu = [kb.sem("s_wu%d" % j) for j in range(4)]
    s_wd = [kb.sem("s_wd%d" % j) for j in range(4)]
    s_wdn = [kb.sem("s_wdn%d" % j) for j in range(4)]
    s_gm = kb.sem("s_gm")
    s_um = kb.sem("s_um")
    s_gv = kb.sem("s_gv")
    s_hv = kb.sem("s_hv")
    s_dm = kb.sem("s_dm")
    s_om = kb.sem("s_om")
    s_ge = kb.sem("s_ge")
    s_oe = kb.sem("s_oe")
    s_t1 = kb.sem("s_t1")
    s_x1 = kb.sem("s_x1")
    s_xl = kb.sem("s_xl")
    s_outp = [kb.sem("s_out0"), kb.sem("s_out1")]

    blk = kb.block() if F is None else F["blk"]
    NG = IC // 4

    @blk.gpsimd
    def _(g):
        emit_identity(g, ident, s_gp)
        if F is None:
            cast["Wo"].bump(g.dma_start(out=Wo_bf, in_=Wo), 16)
            cast["gate"].bump(g.dma_start(out=gate_bf, in_=gate_w), 16)
            cast["up"].bump(g.dma_start(out=up_bf, in_=up_w), 16)
            cast["down"].bump(g.dma_start(out=down_bf, in_=down_w), 16)
        cast["Bo"].bump(g.dma_start(out=Bosb16[0:16, :], in_=Bo), 16)
        s_ae.bump(g.dma_start(out=Aohat[0:16, :], in_=Ao), 16)

    @blk.sync
    def _(sy):
        if F is not None:
            F["sems"]["l0pre_xn"].wait(sy)
            F["sems"]["attn0_outp"][0].wait(sy)
            F["sems"]["attn0_outp"][1].wait(sy)
        s_ld.bump(sy.dma_start(out=xnbf[:, :].rearrange("p (c t) -> p c t", t=T),
                               in_=xnT.rearrange("(c p) t -> p c t", p=128)), 16)
        s_ld.bump(sy.dma_start(out=ssb[:, :].rearrange("p (c t) -> p c t", t=T),
                               in_=sT.rearrange("(c p) t -> p c t", p=128)), 16)

    s_be = emit_bias_load_transpose(blk, kb, [bo, down_s, gate_s, up_s],
                                    btmp, bT, ident, psM, s_gp, tag="l0p")

    @blk.vector
    def _(v):
        s_gp.wait(v)
        s_gp.bump(v.tensor_copy(out=ident_bf[:, :], in_=ident[:, :]))

    @blk.tensor
    def _(t):
        s_gp.wait(t)
        s_be.wait(t)
        cast["Bo"].wait(t)
        for c in range(C):
            if c >= 1:
                s_bt.wait(t, c)
            s_btp.bump(t.transpose(
                psM[:, 0:128].bitcast(BF16)[0:128, 0:16],
                Bosb16[0:16, c * 128:c * 128 + 128],
                ident_bf[0:16, 0:16]))

    @blk.scalar
    def _(sc):
        for c in range(C):
            s_btp.wait(sc, c + 1)
            s_bt.bump(sc.activation(
                out=botile[:, c * 16:c * 16 + 16],
                in_=psM[:, 0:128].bitcast(BF16)[0:128, 0:16],
                func=AF.Identity))

    @blk.tensor
    def _(t):
        s_ld.wait(t)
        s_bt.wait(t)
        s_ae.wait(t)
        s_be.wait(t)
        for tp in range(2):
            if tp >= 1:
                s_ue.wait(t, tp)
            for c in range(C):
                inst = t.matmul(psU[:, 0:512], lhsT=botile[:, c * 16:c * 16 + 16],
                                rhs=ssb[:, c * T + tp * 512:c * T + tp * 512 + 512],
                                start=(c == 0), stop=(c == C - 1))
            s_ul.bump(inst)

    @blk.vector
    def _(v):
        for tp in range(2):
            s_ul.wait(v, tp + 1)
            s_ue.bump(v.tensor_scalar(uo_bf[0:16, tp * 512:tp * 512 + 512],
                                      psU[:, 0:512], 0.5, None, ALU.mult))

    @blk.sync
    def _(sy):
        cast["gate"].wait(sy)
        cast["up"].wait(sy)
        cast["down"].wait(sy)
        cast["Wo"].wait(sy)
        for tb in range(2):
            for ig2 in range(2 * NG):
                gi2 = tb * 2 * NG + ig2
                if gi2 >= 2:
                    sy.wait_ge(s_gm.h, 2 * gi2 - 2)
                if tb == 1 and ig2 <= 1:
                    sy.wait_ge(s_dm.h, 16)
                for dc in range(C):
                    s_wf[dc // 4].bump(sy.dma_start(
                        out=wdbuf[:, (gi2 % 2) * 4096 + dc * 256:(gi2 % 2) * 4096 + dc * 256 + 256],
                        in_=gate_bf[dc * 128:(dc + 1) * 128, ig2 * 256:ig2 * 256 + 256]), 16)
            for og in range(4):
                ogi = tb * 4 + og
                if ogi == 0:
                    sy.wait_ge(s_hv.h, IC)
                elif og == 0:
                    sy.wait_ge(s_hv.h, 2 * IC)
                if ogi >= 1:
                    sy.wait_ge(s_dm.h, ogi * 4)
                    sy.wait_ge(s_om.h, ogi * 4)
                for ic in range(1, IC, 2):
                    s_wdn[1].bump(sy.dma_start(
                        out=wdbuf[:, ic * 512:ic * 512 + 512],
                        in_=down_bf[ic * 128:(ic + 1) * 128, og * 512:og * 512 + 512]), 16)
                for dc in range(C):
                    s_wd[dc // 4].bump(sy.dma_start(
                        out=wobuf[:, dc * 512:dc * 512 + 512],
                        in_=Wo_bf[dc * 128:(dc + 1) * 128, og * 512:og * 512 + 512]), 16)
                if ogi >= 1:
                    sy.wait_ge(s_x1.h, ogi * 4)
                for oc4 in range(4):
                    oc = og * 4 + oc4
                    s_xl.bump(sy.dma_start(
                        out=xstage[:, oc4 * 512:oc4 * 512 + 512],
                        in_=xT[oc * 128:oc * 128 + 128, tb * 512:tb * 512 + 512]), 16)

    @blk.gpsimd
    def _(g):
        for tb in range(2):
            for ig2 in range(2 * NG):
                gi2 = tb * 2 * NG + ig2
                if gi2 >= 2:
                    g.wait_ge(s_um.h, 2 * gi2 - 2)
                if tb == 1 and ig2 <= 1:
                    g.wait_ge(s_dm.h, 16)
                for dc in range(C):
                    s_wu[dc // 4].bump(g.dma_start(
                        out=wdbuf[:, 8192 + (gi2 % 2) * 4096 + dc * 256:8192 + (gi2 % 2) * 4096 + dc * 256 + 256],
                        in_=up_bf[dc * 128:(dc + 1) * 128, ig2 * 256:ig2 * 256 + 256]), 16)
            for og in range(4):
                ogi = tb * 4 + og
                if ogi == 0:
                    g.wait_ge(s_hv.h, IC)
                elif og == 0:
                    g.wait_ge(s_hv.h, 2 * IC)
                if ogi >= 1:
                    g.wait_ge(s_dm.h, ogi * 4)
                    g.wait_ge(s_om.h, ogi * 4)
                for ic in range(0, IC, 2):
                    s_wdn[0].bump(g.dma_start(
                        out=wdbuf[:, ic * 512:ic * 512 + 512],
                        in_=down_bf[ic * 128:(ic + 1) * 128, og * 512:og * 512 + 512]), 16)

    @blk.tensor
    def _(t):
        s_ue.wait(t)
        for tb in range(2):
            t0 = tb * 512
            for ig2 in range(2 * NG):
                gi2 = tb * 2 * NG + ig2
                for ic2 in range(2):
                    icg = tb * 44 + ig2 * 2 + ic2
                    bg = (2 * icg) % 4
                    bu = (2 * icg + 1) % 4
                    if icg >= 2:
                        t.wait_ge(s_gv.h, icg - 1)
                    if ic2 == 0:
                        for j in range(4):
                            t.wait_ge(s_wf[j].h, (gi2 + 1) * 64)
                    for dc in range(C):
                        inst = t.matmul(psF[bg][:, :],
                                        lhsT=wdbuf[:, (gi2 % 2) * 4096 + dc * 256 + ic2 * 128:(gi2 % 2) * 4096 + dc * 256 + ic2 * 128 + 128],
                                        rhs=xnbf[:, dc * T + t0:dc * T + t0 + 512],
                                        start=(dc == 0), stop=(dc == C - 1))
                    s_gm.bump(inst)
                    if icg >= 2:
                        t.wait_ge(s_hv.h, icg - 1)
                    if ic2 == 0:
                        for j in range(4):
                            t.wait_ge(s_wu[j].h, (gi2 + 1) * 64)
                    for dc in range(C):
                        inst = t.matmul(psF[bu][:, :],
                                        lhsT=wdbuf[:, 8192 + (gi2 % 2) * 4096 + dc * 256 + ic2 * 128:8192 + (gi2 % 2) * 4096 + dc * 256 + ic2 * 128 + 128],
                                        rhs=xnbf[:, dc * T + t0:dc * T + t0 + 512],
                                        start=(dc == 0), stop=(dc == C - 1))
                    s_um.bump(inst)
            for og in range(4):
                ogi = tb * 4 + og
                for oc4 in range(4):
                    ocg = ogi * 4 + oc4
                    oc = og * 4 + oc4
                    if ocg >= 1:
                        t.wait_ge(s_oe.h, ocg)
                    if oc4 == 0:
                        for j in range(4):
                            t.wait_ge(s_wd[j].h, (ogi + 1) * 64)
                    for dc in range(C):
                        t.matmul(psY[:, :],
                                 lhsT=wobuf[:, dc * 512 + oc4 * 128:dc * 512 + oc4 * 128 + 128],
                                 rhs=ssb[:, dc * T + t0:dc * T + t0 + 512],
                                 start=(dc == 0), stop=False)
                    s_om.bump(t.matmul(psY[:, :],
                             lhsT=Aohat[:, oc * 128:oc * 128 + 128],
                             rhs=uo_bf[0:16, t0:t0 + 512],
                             start=False, stop=True))
                    if ocg >= 1:
                        t.wait_ge(s_ge.h, ocg)
                    if oc4 == 0:
                        t.wait_ge(s_wdn[0].h, (ogi + 1) * 22 * 16)
                        t.wait_ge(s_wdn[1].h, (ogi + 1) * 22 * 16)
                    for ic in range(IC):
                        inst = t.matmul(psX[:, :],
                                        lhsT=wdbuf[:, ic * 512 + oc4 * 128:ic * 512 + oc4 * 128 + 128],
                                        rhs=htb[:, ic * 512:(ic + 1) * 512],
                                        start=(ic == 0), stop=(ic == IC - 1))
                    s_dm.bump(inst)

    @blk.scalar
    def _(sc):
        for tb in range(2):
            for ig in range(NG):
                for ic4 in range(4):
                    icg = (tb * NG + ig) * 4 + ic4
                    ic = ig * 4 + ic4
                    sc.wait_ge(s_gm.h, icg + 1)
                    if icg >= 4:
                        sc.wait_ge(s_hv.h, icg - 3)
                    s_gv.bump(sc.activation(
                        out=gs_scr[:, (icg % 4) * 512:(icg % 4) * 512 + 512],
                        in_=psF[(2 * icg) % 4][:, :], func=AF.Silu,
                        scale=bT[:, 32 + ic:32 + ic + 1]))
            for og in range(4):
                for oc4 in range(4):
                    ocg = (tb * 4 + og) * 4 + oc4
                    oc = og * 4 + oc4
                    sc.wait_ge(s_dm.h, ocg + 1)
                    if ocg >= 1:
                        sc.wait_ge(s_t1.h, ocg)
                    s_ge.bump(sc.activation(out=gscr[:, :], in_=psX[:, :],
                                            func=AF.Silu,
                                            scale=bT[:, 16 + oc:16 + oc + 1]))
                    sc.wait_ge(s_om.h, ocg + 1)
                    s_oe.bump(sc.activation(out=oscr[:, :], in_=psY[:, :],
                                            func=AF.Identity,
                                            bias=bT[:, oc:oc + 1]))
                    sc.wait_ge(s_x1.h, ocg)
                    if ocg >= 1:
                        po = ocg - 1
                        ptb, poc = (po // 16), (po % 16)
                        s_outp[po % 2].bump(sc.dma_start(
                            out=x1T[poc * 128:poc * 128 + 128, ptb * 512:ptb * 512 + 512],
                            in_=x1sb[:, (po % 2) * 512:(po % 2) * 512 + 512]), 16)

    @blk.vector
    def _(v):
        for tb in range(2):
            for ig in range(NG):
                for ic4 in range(4):
                    icg = (tb * NG + ig) * 4 + ic4
                    ic = ig * 4 + ic4
                    v.wait_ge(s_um.h, icg + 1)
                    v.wait_ge(s_gv.h, icg + 1)
                    if tb == 1 and ig == 0 and ic4 == 0:
                        v.wait_ge(s_dm.h, 16)
                    s_hv.bump(v.scalar_tensor_tensor(
                        htb[:, ic * 512:(ic + 1) * 512],
                        psF[(2 * icg + 1) % 4][:, :],
                        bT[:, 76 + ic:76 + ic + 1],
                        gs_scr[:, (icg % 4) * 512:(icg % 4) * 512 + 512],
                        ALU.mult, ALU.mult))
            for og in range(4):
                ogi = tb * 4 + og
                v.wait_ge(s_xl.h, (ogi + 1) * 64)
                for oc4 in range(4):
                    ocg = ogi * 4 + oc4
                    v.wait_ge(s_ge.h, ocg + 1)
                    v.wait_ge(s_oe.h, ocg + 1)
                    if ocg >= 1:
                        v.wait_ge(s_x1.h, ocg)
                    s_t1.bump(v.tensor_tensor(oscr[:, :], oscr[:, :], gscr[:, :],
                                              ALU.mult))
                    v.wait_ge(s_t1.h, s_t1.n)
                    if ocg >= 2:
                        v.wait_ge(s_outp[ocg % 2].h, (ocg // 2) * 16)
                    s_x1.bump(v.tensor_tensor(
                        x1sb[:, (ocg % 2) * 512:(ocg % 2) * 512 + 512],
                        xstage[:, oc4 * 512:oc4 * 512 + 512], oscr[:, :], ALU.add))

    @blk.scalar
    def _(sc):
        sc.wait_ge(s_x1.h, 32)
        po = 31
        s_outp[po % 2].bump(sc.dma_start(
            out=x1T[15 * 128:16 * 128, 512:1024],
            in_=x1sb[:, (po % 2) * 512:(po % 2) * 512 + 512]), 16)
        sc.wait_ge(s_outp[0].h, s_outp[0].n)
        sc.wait_ge(s_outp[1].h, s_outp[1].n)

    if F is not None:
        F["sems"]["l0post_outp"] = s_outp
        return None
    kb.ctx.close()
    return kb.nc





PI2 = 6.283185307179586
PI2_HI = 6.28125
PI2_LO = PI2 - PI2_HI
MAGIC = 12582912.0


def emit_ln_stream(blk, kb, xbf, x_dram, xnbf, Tn, s_xb, s_xn, psA, psB, psC, psD,
                   xsq, lnrows, negMb, Rb, lnscr, ones_bf, onesrow_bf, epsb,
                   tag="", s_gp=None, pre_bcast_waits=(), x_srcs=None, pre_x_waits=()):
    s_sq = kb.sem("lnsq" + tag)
    s_t2 = kb.sem("lnt2" + tag)
    s_vx = kb.sem("lnvx" + tag)
    s_sc = kb.sem("lnsc" + tag)
    s_tt = kb.sem("lntt" + tag)
    s_xg = [kb.sem("lnxg%d" % i + tag) for i in range(2)]
    nhalf = (Tn + 511) // 512

    @blk.scalar
    def _(sc):
        s_xb.wait(sc)
        for c in range(C):
            if c >= 2:
                s_t2.wait(sc, c - 1)
            s_sq.bump(sc.activation(out=xsq[:, (c % 2) * Tn:(c % 2) * Tn + Tn],
                                    in_=xbf[:, c * Tn:(c + 1) * Tn], func=AF.Square))

    @blk.tensor
    def _(t):
        if s_gp is not None:
            s_gp.wait(t)
        s_xb.wait(t)
        for c in range(C):
            for hf in range(nhalf):
                w = min(512, Tn - hf * 512)
                t.matmul(psA[0:1, hf * 512:hf * 512 + w], lhsT=ones_bf[:, :],
                         rhs=xbf[:, c * Tn + hf * 512:c * Tn + hf * 512 + w],
                         start=(c == 0), stop=(c == C - 1))
            s_sq.wait(t, c + 1)
            for hf in range(nhalf):
                w = min(512, Tn - hf * 512)
                inst = t.matmul(psB[0:1, hf * 512:hf * 512 + w], lhsT=ones_bf[:, :],
                                rhs=xsq[:, (c % 2) * Tn + hf * 512:(c % 2) * Tn + hf * 512 + w],
                                start=(c == 0), stop=(c == C - 1))
                if hf == nhalf - 1:
                    s_t2.bump(inst)

    NM, MS, VR, SD = 0, Tn, 2 * Tn, 3 * Tn
    HB = 4 * Tn
    lnrows_bf = lnrows[0:1, HB:HB + 2 * Tn].bitcast(BF16)

    @blk.vector
    def _(v):
        s_t2.wait(v, C)
        s_vx.bump(v.tensor_scalar(lnrows[0:1, NM:NM + Tn], psA[0:1, 0:Tn],
                                  -1.0 / D, None, ALU.mult))
        s_vx.bump(v.tensor_scalar(lnrows[0:1, MS:MS + Tn], psB[0:1, 0:Tn],
                                  1.0 / D, None, ALU.mult))
        v.wait_ge(s_vx.h, s_vx.n)
        s_vx.bump(v.tensor_tensor(lnrows[0:1, VR:VR + Tn], lnrows[0:1, NM:NM + Tn],
                                  lnrows[0:1, NM:NM + Tn], ALU.mult))
        v.wait_ge(s_vx.h, s_vx.n)
        s_vx.bump(v.tensor_tensor(lnrows[0:1, VR:VR + Tn], lnrows[0:1, MS:MS + Tn],
                                  lnrows[0:1, VR:VR + Tn], ALU.subtract))

    @blk.scalar
    def _(sc):
        s_vx.wait(sc)
        s_sc.bump(sc.activation(out=lnrows[0:1, SD:SD + Tn], in_=lnrows[0:1, VR:VR + Tn],
                                func=AF.Sqrt, bias=epsb[0:1, 0:1]))

    @blk.vector
    def _(v):
        s_sc.wait(v)
        s_vx.bump(v.reciprocal(lnrows[0:1, VR:VR + Tn], lnrows[0:1, SD:SD + Tn]))
        for i, src in enumerate([NM, VR]):
            v.wait_ge(s_vx.h, s_vx.n)
            s_vx.bump(v.tensor_copy(out=lnrows_bf[0:1, (2 * i) * Tn:(2 * i) * Tn + Tn],
                                    in_=lnrows[0:1, src:src + Tn]))
            v.wait_ge(s_vx.h, s_vx.n)
            s_vx.bump(v.tensor_tensor(lnrows_bf[0:1, (2 * i + 1) * Tn:(2 * i + 2) * Tn],
                                      lnrows[0:1, src:src + Tn],
                                      lnrows_bf[0:1, (2 * i) * Tn:(2 * i) * Tn + Tn],
                                      ALU.subtract))

    s_bc = kb.sem("lnbc" + tag)
    s_bv = kb.sem("lnbv" + tag)

    @blk.tensor
    def _(t):
        s_vx.wait(t)
        for w_ in pre_bcast_waits:
            w_.wait(t)
        b = 0
        for i in range(2):
            for hf in range(nhalf):
                w = min(512, Tn - hf * 512)
                dst = psC if hf % 2 == 0 else psD
                if b >= 1:
                    s_bv.wait(t, b)
                t.matmul(dst[:, 0:w], lhsT=onesrow_bf[0:1, :],
                         rhs=lnrows_bf[0:1, (2 * i) * Tn + hf * 512:(2 * i) * Tn + hf * 512 + w],
                         start=True, stop=False)
                s_bc.bump(t.matmul(dst[:, 0:w], lhsT=onesrow_bf[0:1, :],
                                   rhs=lnrows_bf[0:1, (2 * i + 1) * Tn + hf * 512:(2 * i + 1) * Tn + hf * 512 + w],
                                   start=False, stop=True))
                b += 1

    nxd = 1 if x_srcs is None else len(x_srcs)

    @blk.sync
    def _(sy):
        for w_ in pre_x_waits:
            w_.wait(sy)
        for c in range(C):
            if c >= 2:
                sy.wait_ge(s_xn.h, c - 1)
            if x_srcs is None:
                s_xg[c % 2].bump(sy.dma_start(
                    out=lnscr[:, (c % 2) * Tn:(c % 2) * Tn + Tn],
                    in_=x_dram[c * 128:(c + 1) * 128, :]), 16)
            else:
                for (col0, wdt, apf) in x_srcs:
                    s_xg[c % 2].bump(sy.dma_start(
                        out=lnscr[:, (c % 2) * Tn + col0:(c % 2) * Tn + col0 + wdt],
                        in_=apf(c)), 16)

    @blk.vector
    def _(v):
        for i, dstT in enumerate([negMb, Rb]):
            for hf in range(nhalf):
                w = min(512, Tn - hf * 512)
                src = psC if hf % 2 == 0 else psD
                s_bc.wait(v, i * nhalf + hf + 1)
                s_bv.bump(v.tensor_copy(out=dstT[:, hf * 512:hf * 512 + w], in_=src[:, 0:w]))
        v.wait_ge(s_bv.h, s_bv.n)
        for c in range(C):
            sl = (c % 2) * Tn
            v.wait_ge(s_xg[c % 2].h, (c // 2 + 1) * 16 * nxd)
            s_tt.bump(v.tensor_tensor(lnscr[:, sl:sl + Tn], lnscr[:, sl:sl + Tn],
                                      negMb[:, 0:Tn], ALU.add))
            v.wait_ge(s_tt.h, s_tt.n)
            s_xn.bump(v.tensor_tensor(xnbf[:, c * Tn:(c + 1) * Tn],
                                      lnscr[:, sl:sl + Tn], Rb[:, 0:Tn], ALU.mult))
            v.wait_ge(s_xn.h, s_xn.n)
    return {"s_bv": s_bv, "s_tt": s_tt}



def build_l1pre(F=None):
    if F is None:
        kb = KB("l1pre")
        x1Th = kb.dram("x1Th", [D, TH], F32, "ExternalInput")
        w_qkv = kb.dram("w_qkv", [D, 3 * D], F32, "ExternalInput")
        meta = kb.dram("meta", [1, 4], F32, "ExternalInput")
        q1T = kb.dram("q1T", [D, T], BF16, "ExternalOutput")
        k1T = kb.dram("k1T", [D, TH], BF16, "ExternalOutput")
        v1T = kb.dram("v1T", [D, TH], BF16, "ExternalOutput")
        xn1T = kb.dram("xn1T", [D, TH], BF16, "ExternalOutput")
        x1own = None
        x1halo = None
    else:
        kb = F["kb"]
        ext = F["ext"]
        x1Th = None
        x1own = ext["x1T"]
        x1halo = ext["x1halo"]
        w_qkv = ext["w_qkv"]
        meta = ext["meta"]
        q1T, k1T, v1T, xn1T = ext["q1T"], ext["k1T"], ext["v1T"], ext["xn1T"]
    nc = kb.nc
    wq_bf = kb.dram("wq_bf", [D, 3 * D], BF16) if F is None else ext["wq_bf"]

    xbf = kb.sb("xbf", [128, C * TH], BF16)
    xnbf = kb.sb("xnbf", [128, C * TH], BF16)
    xsq = kb.sb("xsq", [128, 2 * TH], BF16)
    lnrows = kb.sb("lnrows", [1, 6 * TH], F32)
    negMb = kb.sb("negMb", [128, TH], F32)
    Rb = kb.sb("Rb", [128, TH], F32)
    lnscr = kb.sb("lnscr", [128, 2 * TH], F32)
    ones_bf = kb.sb("ones_bf", [128, 1], BF16)
    onesrow_bf = kb.sb("onesrow_bf", [1, 128], BF16)
    epsb = kb.sb("epsb", [1, 1], F32)
    msb = kb.sb("msb", [1, 4], F32)
    mrow_bf = kb.sb("mrow_bf", [1, 1], BF16)
    t0b = kb.sb("t0b", [128, 1], F32)
    iota_f = kb.sb("iota_f", [128, TH], F32)
    iota_p = kb.sb("iota_p", [128, 1], F32)
    srow = kb.sb("srow", [128, 8], F32)
    scrA = kb.sb("scrA", [128, 4 * TH], F32)
    sraw = kb.sb("sraw", [128, TH], F32)
    craw = kb.sb("craw", [128, TH], F32)
    wbuf = kb.sb("wbuf", [128, C * 512], BF16, is_transpose_dest=True)
    rotst = kb.sb("rotst", [128, TH], BF16)
    vstage = kb.sb("vstage", [128, TH], BF16)

    tg = scrA[:, 0:TH]
    uu = scrA[:, TH:2 * TH]
    rr = scrA[:, 2 * TH:3 * TH]
    qs = scrA[:, 0:TH]
    qsw = scrA[:, TH:2 * TH]
    rs1 = scrA[:, 2 * TH:3 * TH]
    rs2 = scrA[:, 3 * TH:4 * TH]
    Cf = craw
    Sf = sraw

    psA = kb.ps("psA", [128, 1536])
    psB = kb.ps("psB", [128, 1536])
    psC = kb.ps("psC", [128, 512])
    psD = kb.ps("psD", [128, 512])

    s_xb = kb.sem("s_xb")
    s_gp = kb.sem("s_gp")
    s_xn = kb.sem("s_xn")
    s_mt = kb.sem("s_mt")
    s_tb = kb.sem("s_tb")
    s_tab = kb.sem("s_tab")
    s_sin = kb.sem("s_sin")
    s_cast = kb.sem("s_cast") if F is None else F["cast"]["wq"]
    s_wl = kb.sem("s_wl")
    s_mm = kb.sem("s_mm")
    s_ev = kb.sem("s_ev")
    s_rp = kb.sem("s_rp")
    s_outp = [kb.sem("s_outq0"), kb.sem("s_outq1")]
    s_oxn = kb.sem("s_oxn")

    blk = kb.block() if F is None else F["blk"]

    @blk.gpsimd
    def _(g):
        s_gp.bump(g.memset(ones_bf[:, :], 1.0))
        s_gp.bump(g.memset(onesrow_bf[:, :], 1.0))
        s_gp.bump(g.memset(epsb[:, :], LN_EPS))
        s_gp.bump(g.iota(iota_f[:, :], pattern=[[1, TH]], base=0, channel_multiplier=0,
                         allow_small_or_imprecise_dtypes=True))
        s_gp.bump(g.iota(iota_p[:, :], pattern=[[0, 1]], base=0, channel_multiplier=1,
                         allow_small_or_imprecise_dtypes=True))
        if F is None:
            s_cast.bump(g.dma_start(out=wq_bf, in_=w_qkv), 16)
        if F is None:
            s_xb.bump(g.dma_start(out=xbf[:, :].rearrange("p (c t) -> p c t", t=TH),
                                  in_=x1Th.rearrange("(c p) t -> p c t", p=128)), 16)
        else:
            F["sems"]["cc_halo"].wait(g)
            F["sems"]["l0post_outp"][0].wait(g)
            F["sems"]["l0post_outp"][1].wait(g)
            s_xb.bump(g.dma_start(
                out=xbf[:, :].rearrange("p (c t) -> p c t", t=TH)[:, :, 0:HALO],
                in_=x1halo.rearrange("(c p) t -> p c t", p=128)), 16)
            s_xb.bump(g.dma_start(
                out=xbf[:, :].rearrange("p (c t) -> p c t", t=TH)[:, :, HALO:TH],
                in_=x1own.rearrange("(c p) t -> p c t", p=128)), 16)

    @blk.sync
    def _(sy):
        s_mt.bump(sy.dma_start(out=msb[:, :], in_=meta), 16)

    @blk.vector
    def _(v):
        s_mt.wait(v)
        s_tb.bump(v.tensor_copy(out=mrow_bf[:, :], in_=msb[0:1, 0:1]))

    @blk.tensor
    def _(t):
        s_tb.wait(t)
        s_gp.wait(t)
        s_tb.bump(t.matmul(psC[:, 0:1], lhsT=onesrow_bf[0:1, :], rhs=mrow_bf[0:1, 0:1],
                           start=True, stop=True))

    @blk.vector
    def _(v):
        s_tb.wait(v)
        s_tb.bump(v.tensor_copy(out=t0b[:, :], in_=psC[:, 0:1]))
        v.wait_ge(s_tb.h, s_tb.n)
        s_tab.bump(v.tensor_scalar(srow[:, 0:1], iota_p[:, :], 1.0 / 64, -0.4921875,
                                   ALU.mult, ALU.add))
        v.wait_ge(s_tab.h, s_tab.n)
        s_tab.bump(v.tensor_scalar(srow[:, 1:2], srow[:, 0:1], MAGIC, None, ALU.add))
        v.wait_ge(s_tab.h, s_tab.n)
        s_tab.bump(v.tensor_scalar(srow[:, 2:3], srow[:, 1:2], -MAGIC, None, ALU.add))
        v.wait_ge(s_tab.h, s_tab.n)
        s_tab.bump(v.scalar_tensor_tensor(srow[:, 3:4], srow[:, 2:3], -64.0,
                                          iota_p[:, :], ALU.mult, ALU.add))
        v.wait_ge(s_tab.h, s_tab.n)
        s_tab.bump(v.tensor_scalar(srow[:, 4:5], srow[:, 3:4], -0.14391156831212787,
                                   None, ALU.mult))
        v.wait_ge(s_tab.h, s_tab.n)
        s_tab.bump(v.tensor_scalar(srow[:, 5:6], iota_p[:, :], 0.5, -0.25,
                                   ALU.mult, ALU.add))
        v.wait_ge(s_tab.h, s_tab.n)
        s_tab.bump(v.tensor_scalar(srow[:, 5:6], srow[:, 5:6], MAGIC, None, ALU.add))
        v.wait_ge(s_tab.h, s_tab.n)
        s_tab.bump(v.tensor_scalar(srow[:, 5:6], srow[:, 5:6], -MAGIC, None, ALU.add))
        v.wait_ge(s_tab.h, s_tab.n)
        s_tab.bump(v.scalar_tensor_tensor(srow[:, 6:7], srow[:, 5:6], -2.0,
                                          iota_p[:, :], ALU.mult, ALU.add))
        v.wait_ge(s_tab.h, s_tab.n)
        s_tab.bump(v.tensor_scalar(srow[:, 7:8], srow[:, 6:7], -2.0, 1.0,
                                   ALU.mult, ALU.add))

    @blk.scalar
    def _(sc):
        s_tab.wait(sc)
        s_sin.bump(sc.activation(out=srow[:, 4:5], in_=srow[:, 4:5], func=AF.Exp))

    def rope_reduce(v, src, dst, cnt):
        cnt.bump(v.tensor_scalar(rr[:, :], src[:, :], 1.0 / PI2, -0.5,
                                 ALU.mult, ALU.add))
        v.wait_ge(cnt.h, cnt.n)
        cnt.bump(v.tensor_scalar(rr[:, :], rr[:, :], MAGIC, None, ALU.add))
        v.wait_ge(cnt.h, cnt.n)
        cnt.bump(v.tensor_scalar(rr[:, :], rr[:, :], -MAGIC, None, ALU.add))
        v.wait_ge(cnt.h, cnt.n)
        cnt.bump(v.scalar_tensor_tensor(dst[:, :], rr[:, :], -PI2_HI, src[:, :],
                                        ALU.mult, ALU.add))
        v.wait_ge(cnt.h, cnt.n)
        cnt.bump(v.scalar_tensor_tensor(dst[:, :], rr[:, :], -PI2_LO, dst[:, :],
                                        ALU.mult, ALU.add))
        v.wait_ge(cnt.h, cnt.n)
        cnt.bump(v.tensor_scalar(dst[:, :], dst[:, :], -3.141592653589793, None,
                                 ALU.add))
        v.wait_ge(cnt.h, cnt.n)

    @blk.vector
    def _(v):
        s_sin.wait(v)
        s_gp.wait(v)
        s_tab.bump(v.tensor_scalar(tg[:, :], iota_f[:, :], t0b[:, :], None, ALU.add))
        v.wait_ge(s_tab.h, s_tab.n)
        s_tab.bump(v.tensor_scalar(uu[:, :], tg[:, :], srow[:, 4:5], None, ALU.mult))
        v.wait_ge(s_tab.h, s_tab.n)
        rope_reduce(v, uu, sraw, s_tab)
        s_tab.bump(v.tensor_scalar(tg[:, :], uu[:, :], 1.5707963267948966, None,
                                   ALU.add))
        v.wait_ge(s_tab.h, s_tab.n)
        rope_reduce(v, tg, craw, s_tab)

    @blk.scalar
    def _(sc):
        s_tab.wait(sc)
        s_sin.bump(sc.activation(out=sraw[:, :], in_=sraw[:, :], func=AF.Sin))
        sc.wait_ge(s_sin.h, s_sin.n)
        s_sin.bump(sc.activation(out=craw[:, :], in_=craw[:, :], func=AF.Sin))

    @blk.vector
    def _(v):
        s_sin.wait(v)
        s_tab.bump(v.tensor_scalar(Cf[:, :], craw[:, :], -1.0, None, ALU.mult))
        s_tab.bump(v.tensor_scalar(Sf[:, :], sraw[:, :], srow[:, 7:8], None, ALU.mult))
        v.wait_ge(s_tab.h, s_tab.n)
    s_tabdone = s_tab.n
    s_sindone = s_sin.n

    if F is None:
        lns = emit_ln_stream(blk, kb, xbf, x1Th, xnbf, TH, s_xb, s_xn,
                             psA, psB, psC, psD,
                             xsq, lnrows, negMb, Rb, lnscr, ones_bf, onesrow_bf, epsb,
                             tag="l1", s_gp=s_gp, pre_bcast_waits=(s_tb,))
    else:
        x_srcs = [(0, HALO, lambda c: x1halo[c * 128:(c + 1) * 128, :]),
                  (HALO, T, lambda c: x1own[c * 128:(c + 1) * 128, :])]
        lns = emit_ln_stream(blk, kb, xbf, None, xnbf, TH, s_xb, s_xn,
                             psA, psB, psC, psD,
                             xsq, lnrows, negMb, Rb, lnscr, ones_bf, onesrow_bf, epsb,
                             tag="l1", s_gp=s_gp, pre_bcast_waits=(s_tb,),
                             x_srcs=x_srcs,
                             pre_x_waits=(F["sems"]["cc_halo"],
                                          F["sems"]["l0post_outp"][0],
                                          F["sems"]["l0post_outp"][1]))
    s_lnbv = lns["s_bv"]

    passes_kv = [(0, 512), (512, 512), (1024, 256)]
    passes_q = [(256, 512), (768, 512)]

    def oc_passes(oc):
        return passes_q if oc < 16 else passes_kv

    mm_before = [0]
    for oc in range(48):
        mm_before.append(mm_before[-1] + len(oc_passes(oc)))

    @blk.sync
    def _(sy):
        s_cast.wait(sy)
        for wg in range(12):
            if wg >= 1:
                sy.wait_ge(s_mm.h, mm_before[4 * wg])
            for dc in range(C):
                s_wl.bump(sy.dma_start(
                    out=wbuf[:, dc * 512:dc * 512 + 512],
                    in_=wq_bf[dc * 128:(dc + 1) * 128, wg * 512:wg * 512 + 512]), 16)

    @blk.tensor
    def _(t):
        s_xn.wait(t)
        s_lnbv.wait(t)
        kq = 0
        for wg in range(12):
            t.wait_ge(s_wl.h, (wg + 1) * C * 16)
            for oc4 in range(4):
                oc = wg * 4 + oc4
                for (p0, pw) in oc_passes(oc):
                    ps = psC if kq % 2 == 0 else psD
                    if kq >= 2:
                        t.wait_ge(s_ev.h, kq - 1)
                    for dc in range(C):
                        inst = t.matmul(ps[:, 0:pw],
                                        lhsT=wbuf[:, dc * 512 + oc4 * 128:dc * 512 + oc4 * 128 + 128],
                                        rhs=xnbf[:, dc * TH + p0:dc * TH + p0 + pw],
                                        start=(dc == 0), stop=(dc == C - 1))
                    s_mm.bump(inst)
                    kq += 1

    @blk.scalar
    def _(sc):
        sc.wait_ge(s_tab.h, s_tabdone)
        kq = 0
        for wg in range(12):
            for oc4 in range(4):
                oc = wg * 4 + oc4
                sec = oc // 16
                h = oc % 16
                for ip, (p0, pw) in enumerate(oc_passes(oc)):
                    ps = psC if kq % 2 == 0 else psD
                    sc.wait_ge(s_mm.h, kq + 1)
                    if sec < 2:
                        if ip == 0 and oc >= 1:
                            sc.wait_ge(s_rp.h, 4 * oc)
                        s_ev.bump(sc.activation(out=qs[:, p0:p0 + pw], in_=ps[:, 0:pw],
                                                func=AF.Identity))
                    else:
                        if ip == 0 and oc >= 33:
                            po = oc - 1
                            sc.wait_ge(s_outp[po % 2].h, (po // 2 + 1) * 16)
                        s_ev.bump(sc.activation(out=vstage[:, p0:p0 + pw],
                                                in_=ps[:, 0:pw], func=AF.Identity))
                    kq += 1
                if sec < 2:
                    sc.wait_ge(s_rp.h, 4 * (oc + 1))
                    if sec == 0:
                        s_outp[oc % 2].bump(sc.dma_start(
                            out=q1T[h * 128:(h + 1) * 128, :],
                            in_=rotst[:, 256:TH]), 16)
                    else:
                        s_outp[oc % 2].bump(sc.dma_start(
                            out=k1T[h * 128:(h + 1) * 128, :],
                            in_=rotst[:, :]), 16)
                else:
                    sc.wait_ge(s_ev.h, kq)
                    s_outp[oc % 2].bump(sc.dma_start(
                        out=v1T[h * 128:(h + 1) * 128, :],
                        in_=vstage[:, :]), 16)

    @blk.vector
    def _(v):
        mask = []
        for i in range(0, 32, 2):
            mask += [i + 1, i]
        for oc in range(32):
            sec = oc // 16
            lo = 256 if sec == 0 else 0
            v.wait_ge(s_ev.h, mm_before[oc + 1])
            if oc >= 1:
                po = oc - 1
                v.wait_ge(s_outp[po % 2].h, (po // 2 + 1) * 16)
            s_rp.bump(v.stream_shuffle(qsw[:, lo:TH], qs[:, lo:TH], mask))
            v.wait_ge(s_rp.h, s_rp.n)
            s_rp.bump(v.tensor_tensor(rs1[:, lo:TH], qs[:, lo:TH], Cf[:, lo:TH],
                                      ALU.mult))
            v.wait_ge(s_rp.h, s_rp.n)
            s_rp.bump(v.tensor_tensor(rs2[:, lo:TH], qsw[:, lo:TH], Sf[:, lo:TH],
                                      ALU.mult))
            v.wait_ge(s_rp.h, s_rp.n)
            s_rp.bump(v.tensor_tensor(rotst[:, lo:TH], rs1[:, lo:TH], rs2[:, lo:TH],
                                      ALU.add))
            v.wait_ge(s_rp.h, s_rp.n)

    @blk.sync
    def _(sy):
        s_xn.wait(sy)
        s_oxn.bump(sy.dma_start(out=xn1T.rearrange("(c p) t -> p c t", p=128),
                                in_=xnbf[:, :].rearrange("p (c t) -> p c t", t=TH)), 16)
        sy.wait_ge(s_outp[0].h, s_outp[0].n)
        sy.wait_ge(s_outp[1].h, s_outp[1].n)
        sy.wait_ge(s_oxn.h, s_oxn.n)

    if F is not None:
        F["sems"]["l1pre_outp"] = s_outp
        F["sems"]["l1pre_xn"] = s_oxn
        return None
    kb.ctx.close()
    return kb.nc





def build_attn1(F=None):
    if F is None:
        kb = KB("attn1")
        q1T = kb.dram("q1T", [D, T], BF16, "ExternalInput")
        k1T = kb.dram("k1T", [D, TH], BF16, "ExternalInput")
        v1T = kb.dram("v1T", [D, TH], BF16, "ExternalInput")
        xn1f = kb.dram("xn1f", [D, S], BF16, "ExternalInput")
        w_c1 = kb.dram("w_c1", [D, 512], F32, "ExternalInput")
        w_c2 = kb.dram("w_c2", [1, 512], F32, "ExternalInput")
        meta = kb.dram("meta", [1, 4], F32, "ExternalInput")
        s1T = kb.dram("s1T", [D, T], BF16, "ExternalOutput")
        WS = S

        def stats_src(c):
            return xn1f[c * 128:(c + 1) * 128, :]
    else:
        kb = F["kb"]
        ext = F["ext"]
        q1T, k1T, v1T = ext["q1T"], ext["k1T"], ext["v1T"]
        xn1T_st = ext["xn1T"]
        w_c1, w_c2, meta = ext["w_c1"], ext["w_c2"], ext["meta"]
        s1T = ext["s1T"]
        cc_in = ext["cc_stats_in"]
        cc_out = ext["cc_stats_out"]
        WS = T

        def stats_src(c):
            return xn1T_st[c * 128:(c + 1) * 128, HALO:TH]
    nc = kb.nc
    if F is None:
        wc1_bf = kb.dram("wc1_bf", [D, 512], BF16)
        wc2_bf = kb.dram("wc2_bf", [1, 512], BF16)
    else:
        wc1_bf, wc2_bf = ext["wc1_bf"], ext["wc2_bf"]

    q1sb = kb.sb("q1sb", [128, H * T], BF16)
    k1sb = kb.sb("k1sb", [128, H * TH], BF16)
    vbuf = kb.sb("vbuf", [128, 2 * 10 * 128], BF16, is_transpose_dest=True)
    stg = kb.sb("stg", [128, 2 * WS], BF16)
    sqscr = kb.sb("sqscr", [128, WS], BF16)
    fsum = kb.sb("fsum", [128, 20], F32)
    fsb = kb.sb("fsb", [128, 32], BF16)
    xmb = kb.sb("xmb", [128, 16], BF16)
    wcbuf = kb.sb("wcbuf", [128, C * 512], BF16, is_transpose_dest=True)
    hid = kb.sb("hid", [1, 512], BF16)
    hT = kb.sb("hT", [128, 4], BF16)
    w2sb = kb.sb("w2sb", [1, 512], BF16)
    w2T = kb.sb("w2T", [128, 4], BF16)
    rows1 = kb.sb("rows1", [1, 16], F32)
    rows1b = kb.sb("rows1b", [1, 8], BF16)
    fb = kb.sb("fb", [128, 1], F32)
    hvb = kb.sb("hvb", [128, 1], F32)
    grow = kb.sb("grow", [128, 4], F32)
    dprime = kb.sb("dprime", [128, 128], F32)
    zer = kb.sb("zer", [128, 128], F32)
    pieces = kb.sb("pieces", [128, 6 * 128], F32)
    amask = kb.sb("amask", [128, 3 * 384], F32)
    smask = kb.sb("smask", [128, 2 * 384], F32)
    Pt = kb.sb("Pt", [128, 2 * 384], BF16)
    dnr = kb.sb("dnr", [1, 2 * 128], F32)
    dnb = kb.sb("dnb", [1, 2 * 128], BF16)
    rbq = kb.sb("rbq", [128, 128], F32)
    stbuf = kb.sb("stbuf", [128, 2 * T], BF16)
    ones_bf = kb.sb("ones_bf", [128, 1], BF16)
    onesrow_bf = kb.sb("onesrow_bf", [1, 128], BF16)
    mbias = kb.sb("mbias", [128, 1], F32)
    m5b = kb.sb("m5b", [1, 1], F32)
    msb = kb.sb("msb", [1, 4], F32)
    mrow_bf = kb.sb("mrow_bf", [1, 2], BF16)
    ident = kb.sb("ident", [128, 128], BF16)

    psS = [kb.ps("psS0", [128, 384]), kb.ps("psS1", [128, 384])]
    psO2 = kb.ps("psO2", [128, 128])
    psD2 = kb.ps("psD2", [1, 128])
    psB1 = kb.ps("psB1", [128, 512])
    psB2 = kb.ps("psB2", [128, 512])

    s_gp = kb.sem("s_gp")
    s_cast = kb.sem("s_cast") if F is None else F["cast"]["wc"]
    s_mt = kb.sem("s_mt")
    s_qk = kb.sem("s_qk")
    s_st = [kb.sem("s_st0"), kb.sem("s_st1")]
    s_sq = kb.sem("s_sq")
    s_red = kb.sem("s_red")
    s_stv = kb.sem("s_stv")
    s_stm = kb.sem("s_stm")
    s_sts = kb.sem("s_sts")
    s_wc = kb.sem("s_wc")
    s_msk = kb.sem("s_mskd")
    s_sm = kb.sem("s_sm")
    s_ma = kb.sem("s_ma")
    s_exp = kb.sem("s_exp")
    s_pv = kb.sem("s_pv")
    s_dne = kb.sem("s_dne")
    s_row = kb.sem("s_row")
    s_bcq = kb.sem("s_bcq")
    s_rbe = kb.sem("s_rbe")
    s_sm2 = kb.sem("s_sm2")
    s_vt = [kb.sem("s_vt0"), kb.sem("s_vt1")]
    s_o1p = [kb.sem("s_o1p0"), kb.sem("s_o1p1")]
    s_vx = kb.sem("s_vx")

    blk = kb.block() if F is None else F["blk"]
    s_ccs = kb.sem("s_ccs")

    @blk.gpsimd
    def _(g):
        s_gp.bump(g.memset(ones_bf[:, :], 1.0))
        s_gp.bump(g.memset(onesrow_bf[:, :], 1.0))
        s_gp.bump(g.memset(mbias[:, :], MB))
        s_gp.bump(g.memset(m5b[:, :], -5.0))
        s_gp.bump(g.memset(zer[:, :], 0.0))
        s_gp.bump(g.iota(dprime[:, :], pattern=[[1, 128]], base=0, channel_multiplier=-1,
                         allow_small_or_imprecise_dtypes=True))
        emit_identity(g, ident, s_gp)
        if F is None:
            s_cast.bump(g.dma_start(out=wc1_bf, in_=w_c1), 16)
            s_cast.bump(g.dma_start(out=wc2_bf, in_=w_c2), 16)
        if F is not None:
            g.wait_ge(s_red.h, 48)
            s_ccs.bump(g.dma_start(out=cc_in, in_=fsum[:, 0:18]), 16)
            g.wait_ge(s_ccs.h, 16)
            s_ccs.bump(g.collective_compute(
                "AllReduce", ALU.add, replica_groups=[list(range(NC))],
                ins=[cc_in.opt()], outs=[cc_out.opt()]))
            g.wait_ge(s_ccs.h, 17)
            s_ccs.bump(g.dma_start(out=fsum[:, 0:18], in_=cc_out), 16)

    @blk.sync
    def _(sy):
        if F is not None:
            F["sems"]["l1pre_outp"][0].wait(sy)
            F["sems"]["l1pre_outp"][1].wait(sy)
            F["sems"]["l1pre_xn"].wait(sy)
        s_mt.bump(sy.dma_start(out=msb[:, :], in_=meta), 16)
        s_qk.bump(sy.dma_start(out=q1sb[:, :].rearrange("p (h t) -> p h t", t=T),
                               in_=q1T.rearrange("(h p) t -> p h t", p=128)), 16)
        s_qk.bump(sy.dma_start(out=k1sb[:, :].rearrange("p (h t) -> p h t", t=TH),
                               in_=k1T.rearrange("(h p) t -> p h t", p=128)), 16)
        for c in range(C):
            if c >= 2:
                sy.wait_ge(s_red.h, 3 * (c - 2) + 1)
                sy.wait_ge(s_sq.h, c - 1)
            s_st[c % 2].bump(sy.dma_start(
                out=stg[:, (c % 2) * WS:(c % 2) * WS + WS],
                in_=stats_src(c)), 16)

    @blk.scalar
    def _(sc):
        for c in range(C):
            sc.wait_ge(s_st[c % 2].h, (c // 2 + 1) * 16)
            if c >= 1:
                sc.wait_ge(s_red.h, 3 * (c - 1) + 2)
            s_sq.bump(sc.activation(out=sqscr[:, :], in_=stg[:, (c % 2) * WS:(c % 2) * WS + WS],
                                    func=AF.Square))

    @blk.vector
    def _(v):
        for c in range(C):
            v.wait_ge(s_st[c % 2].h, (c // 2 + 1) * 16)
            s_red.bump(v.reduce_sum(fsum[:, c:c + 1], stg[:, (c % 2) * WS:(c % 2) * WS + WS],
                                    axis=AX))
            v.wait_ge(s_sq.h, c + 1)
            s_red.bump(v.reduce_sum(fsum[:, 16:17], sqscr[:, :], axis=AX))
            v.wait_ge(s_red.h, s_red.n)
            if c == 0:
                s_red.bump(v.tensor_copy(out=fsum[:, 17:18], in_=fsum[:, 16:17]))
            else:
                s_red.bump(v.tensor_tensor(fsum[:, 17:18], fsum[:, 17:18],
                                           fsum[:, 16:17], ALU.add))
            v.wait_ge(s_red.h, s_red.n)
        if F is not None:
            v.wait_ge(s_ccs.h, 33)
        s_stv.bump(v.tensor_copy(out=fsb[:, 0:16], in_=fsum[:, 0:16]))
        v.wait_ge(s_stv.h, s_stv.n)
        s_stv.bump(v.tensor_copy(out=fsb[:, 16:17], in_=fsum[:, 17:18]))
        v.wait_ge(s_stv.h, s_stv.n)
        s_stv.bump(v.tensor_scalar(xmb[:, :], fsum[:, 0:16], 1.0 / S, None, ALU.mult))
        v.wait_ge(s_stv.h, s_stv.n)

    @blk.tensor
    def _(t):
        s_stv.wait(t)
        s_gp.wait(t)
        s_stm.bump(t.matmul(psB1[0:1, 0:17], lhsT=ones_bf[:, :], rhs=fsb[:, 0:17],
                            start=True, stop=True))

    @blk.vector
    def _(v):
        s_stm.wait(v)
        s_stv.bump(v.reduce_sum(rows1[0:1, 0:1], psB1[0:1, 0:16], axis=AX))
        v.wait_ge(s_stv.h, s_stv.n)
        s_stv.bump(v.tensor_copy(out=rows1[0:1, 1:2], in_=psB1[0:1, 16:17]))
        v.wait_ge(s_stv.h, s_stv.n)
        s_stv.bump(v.tensor_tensor(rows1[0:1, 2:3], rows1[0:1, 0:1], rows1[0:1, 0:1],
                                   ALU.mult))
        v.wait_ge(s_stv.h, s_stv.n)
        s_stv.bump(v.tensor_scalar(rows1[0:1, 2:3], rows1[0:1, 2:3], -1.0 / (S * D),
                                   None, ALU.mult))
        v.wait_ge(s_stv.h, s_stv.n)
        s_stv.bump(v.tensor_tensor(rows1[0:1, 2:3], rows1[0:1, 1:2], rows1[0:1, 2:3],
                                   ALU.add))
        v.wait_ge(s_stv.h, s_stv.n)
        s_stv.bump(v.tensor_scalar(rows1[0:1, 2:3], rows1[0:1, 2:3],
                                   1.0 / (S * D - 1), None, ALU.mult))
        v.wait_ge(s_stv.h, s_stv.n)

    @blk.sync
    def _(sy):
        s_cast.wait(sy)
        for dc in range(C):
            s_wc.bump(sy.dma_start(out=wcbuf[:, dc * 512:dc * 512 + 512],
                                   in_=wc1_bf[dc * 128:(dc + 1) * 128, 0:512]), 16)
        s_wc.bump(sy.dma_start(out=w2sb[:, :], in_=wc2_bf), 16)

    @blk.tensor
    def _(t):
        s_wc.wait(t)
        s_stv.wait(t)
        for dc in range(C):
            inst = t.matmul(psB2[0:1, 0:512], lhsT=xmb[:, dc:dc + 1],
                            rhs=wcbuf[:, dc * 512:dc * 512 + 512],
                            start=(dc == 0), stop=(dc == C - 1))
        s_stm.bump(inst)

    @blk.scalar
    def _(sc):
        sc.wait_ge(s_stm.h, 2)
        s_sts.bump(sc.activation(out=hid[:, :], in_=psB2[0:1, 0:512], func=AF.Silu))

    stv_base = [0]
    stm_base = [0]

    @blk.tensor
    def _(t):
        s_sts.wait(t)
        stv_base[0] = s_stv.n
        stm_base[0] = s_stm.n
        for j in range(4):
            if j >= 1:
                t.wait_ge(s_stv.h, stv_base[0] + 2 * j)
            dst = psB1 if j % 2 == 0 else psB2
            s_stm.bump(t.transpose(dst[:, 0:1].bitcast(BF16)[:, 0:1], hid[0:1, j * 128:(j + 1) * 128],
                                   ident[0:1, 0:1]))
            dst2 = psB2 if j % 2 == 0 else psB1
            s_stm.bump(t.transpose(dst2[:, 1:2].bitcast(BF16)[:, 0:1], w2sb[0:1, j * 128:(j + 1) * 128],
                                   ident[0:1, 0:1]))

    @blk.vector
    def _(v):
        for j in range(4):
            v.wait_ge(s_stm.h, stm_base[0] + 2 * (j + 1))
            dst = psB1 if j % 2 == 0 else psB2
            dst2 = psB2 if j % 2 == 0 else psB1
            s_stv.bump(v.tensor_copy(out=hT[:, j:j + 1], in_=dst[:, 0:1].bitcast(BF16)[:, 0:1]))
            v.wait_ge(s_stv.h, s_stv.n)
            s_stv.bump(v.tensor_copy(out=w2T[:, j:j + 1], in_=dst2[:, 1:2].bitcast(BF16)[:, 0:1]))
            v.wait_ge(s_stv.h, s_stv.n)

    @blk.tensor
    def _(t):
        s_stv.wait(t)
        for j in range(4):
            inst = t.matmul(psB1[0:1, 64:65], lhsT=hT[:, j:j + 1], rhs=w2T[:, j:j + 1],
                            start=(j == 0), stop=(j == 3))
        s_stm.bump(inst)

    @blk.scalar
    def _(sc):
        s_stm.wait(sc)
        s_gp.wait(sc)
        s_sts.bump(sc.activation(out=rows1[0:1, 4:5], in_=psB1[0:1, 64:65],
                                 func=AF.Sigmoid))
        s_sts.bump(sc.activation(out=rows1[0:1, 3:4], in_=rows1[0:1, 2:3],
                                 func=AF.Sigmoid, scale=10.0, bias=m5b[0:1, 0:1]))

    @blk.vector
    def _(v):
        s_sts.wait(v)
        s_mt.wait(v)
        s_vx.bump(v.tensor_tensor(rows1[0:1, 5:6], rows1[0:1, 3:4], rows1[0:1, 4:5],
                                  ALU.add))
        v.wait_ge(s_vx.h, s_vx.n)
        s_vx.bump(v.tensor_scalar(rows1[0:1, 5:6], rows1[0:1, 5:6], 96.0, 64.0,
                                  ALU.mult, ALU.add))
        v.wait_ge(s_vx.h, s_vx.n)
        s_vx.bump(v.tensor_copy(out=rows1b[0:1, 0:1], in_=rows1[0:1, 5:6]))
        v.wait_ge(s_vx.h, s_vx.n)
        s_vx.bump(v.tensor_tensor(rows1b[0:1, 1:2],
                                  rows1[0:1, 5:6], rows1b[0:1, 0:1], ALU.subtract))
        v.wait_ge(s_vx.h, s_vx.n)
        s_vx.bump(v.tensor_copy(out=mrow_bf[0:1, 0:1], in_=msb[0:1, 1:2]))
        v.wait_ge(s_vx.h, s_vx.n)

    @blk.tensor
    def _(t):
        s_vx.wait(t)
        t.matmul(psB2[:, 0:1], lhsT=onesrow_bf[0:1, :], rhs=rows1b[0:1, 0:1],
                 start=True, stop=False)
        s_stm.bump(t.matmul(psB2[:, 0:1], lhsT=onesrow_bf[0:1, :], rhs=rows1b[0:1, 1:2],
                            start=False, stop=True))
        s_stm.bump(t.matmul(psB2[:, 1:2], lhsT=onesrow_bf[0:1, :], rhs=mrow_bf[0:1, 0:1],
                            start=True, stop=True))

    @blk.gpsimd
    def _(g):
        s_gp.bump(g.memset(pieces[:, 0:128], 0.0))
        g.wait_ge(s_gp.h, s_gp.n)
        s_gp.bump(g.affine_select(out=pieces[:, 0:128], in_=pieces[:, 0:128],
                                  compare_op=ALU.is_ge, fill=-5e8, base=0,
                                  pattern=[[1, 128]], channel_multiplier=-1))

    @blk.vector
    def _(v):
        s_stm.wait(v)
        s_msk.bump(v.tensor_copy(out=fb[:, :], in_=psB2[:, 0:1]))
        s_msk.bump(v.tensor_copy(out=hvb[:, :], in_=psB2[:, 1:2]))
        v.wait_ge(s_msk.h, s_msk.n)
        for j, dd in enumerate([0.0, 128.0, 256.0]):
            s_msk.bump(v.tensor_scalar(grow[:, j:j + 1], fb[:, :], -(1.0 + dd), None,
                                       ALU.add))
        s_msk.bump(v.tensor_scalar(grow[:, 3:4], hvb[:, :], 1.0, 5e8,
                                   ALU.subtract, ALU.mult))
        v.wait_ge(s_msk.h, s_msk.n)
        for j in range(3):
            s_msk.bump(v.tensor_scalar(pieces[:, (1 + j) * 128:(2 + j) * 128],
                                       dprime[:, :], grow[:, j:j + 1], None, ALU.is_le))
        v.wait_ge(s_msk.h, s_msk.n)
        for j in range(3):
            s_msk.bump(v.tensor_scalar(pieces[:, (1 + j) * 128:(2 + j) * 128],
                                       pieces[:, (1 + j) * 128:(2 + j) * 128],
                                       1.0, 5e8, ALU.subtract, ALU.mult))
        v.wait_ge(s_msk.h, s_msk.n)
        for j in [1, 2]:
            s_msk.bump(v.tensor_scalar(pieces[:, (3 + j) * 128:(4 + j) * 128],
                                       pieces[:, (j + 1) * 128:(j + 2) * 128],
                                       grow[:, 3:4], None, ALU.add))
        v.wait_ge(s_msk.h, s_msk.n)
        s_gp.wait(v)
        s_msk.bump(v.tensor_tensor(pieces[:, 128:256], pieces[:, 128:256],
                                   pieces[:, 0:128], ALU.add))
        v.wait_ge(s_msk.h, s_msk.n)
        trip = [[5, 4, 1], [5, 2, 1], [3, 2, 1]]
        for a in range(3):
            for j in range(3):
                s_msk.bump(v.tensor_copy(out=amask[:, a * 384 + j * 128:a * 384 + (j + 1) * 128],
                                         in_=pieces[:, trip[a][j] * 128:(trip[a][j] + 1) * 128]))
        v.wait_ge(s_msk.h, s_msk.n)

    @blk.sync
    def _(sy):
        for h in range(H):
            if h >= 2:
                sy.wait_ge(s_pv.h, (h - 1) * 8)
            for kb in range(10):
                s_vt[h % 2].bump(sy.dma_start(
                    out=vbuf[:, (h % 2) * 1280 + kb * 128:(h % 2) * 1280 + kb * 128 + 128],
                    in_=v1T[h * 128:(h + 1) * 128, kb * 128:(kb + 1) * 128],
                    transpose=True), 16)

    @blk.tensor
    def _(t):
        s_qk.wait(t)
        s_msk.wait(t)
        for g2 in range(130):
            if g2 < 128:
                h, qt = g2 // 8, g2 % 8
                sl = g2 % 2
                if g2 >= 2:
                    t.wait_ge(s_ma.h, g2 - 1)
                for j in range(3):
                    kb = qt + j
                    inst = t.matmul(psS[sl][:, j * 128:(j + 1) * 128],
                                    lhsT=k1sb[:, h * TH + kb * 128:h * TH + kb * 128 + 128],
                                    rhs=q1sb[:, h * T + qt * 128:h * T + qt * 128 + 128],
                                    start=True, stop=True)
                s_sm.bump(inst)
            if g2 >= 2:
                gb = g2 - 2
                t.wait_ge(s_row.h, gb + 1)
                if gb >= 1:
                    t.wait_ge(s_rbe.h, gb)
                t.matmul(psB1[:, 0:128], lhsT=onesrow_bf[0:1, :],
                         rhs=dnb[0:1, 0:128], start=True, stop=False)
                s_bcq.bump(t.matmul(psB1[:, 0:128], lhsT=onesrow_bf[0:1, :],
                                    rhs=dnb[0:1, 128:256], start=False, stop=True))

            if 1 <= g2 <= 128:
                ga = g2 - 1
                h, qt = ga // 8, ga % 8
                t.wait_ge(s_exp.h, ga + 1)
                t.wait_ge(s_vt[h % 2].h, (h // 2 + 1) * 160)
                if ga >= 1:
                    t.wait_ge(s_sm2.h, ga)
                    t.wait_ge(s_dne.h, ga)
                psl = (ga % 2) * 384
                for j in range(3):
                    kb = qt + j
                    t.matmul(psO2[:, :],
                             lhsT=vbuf[:, (h % 2) * 1280 + kb * 128:(h % 2) * 1280 + kb * 128 + 128],
                             rhs=Pt[:, psl + j * 128:psl + (j + 1) * 128],
                             start=(j == 0), stop=(j == 2))
                for j in range(3):
                    inst = t.matmul(psD2[0:1, :], lhsT=ones_bf[:, :],
                                    rhs=Pt[:, psl + j * 128:psl + (j + 1) * 128],
                                    start=(j == 0), stop=(j == 2))
                s_pv.bump(inst)
    @blk.scalar
    def _(sc):
        for g2 in range(128):
            h, qt = g2 // 8, g2 % 8
            if qt == 0 and h >= 1:
                sc.wait_ge(s_sm2.h, 8 * h)
                po = h - 1
                s_o1p[po % 2].bump(sc.dma_start(
                    out=s1T[po * 128:(po + 1) * 128, :],
                    in_=stbuf[:, (po % 2) * T:(po % 2) * T + T]), 16)
            sc.wait_ge(s_ma.h, g2 + 1)
            if g2 >= 2:
                sc.wait_ge(s_pv.h, g2 - 1)
            s_exp.bump(sc.activation(out=Pt[:, (g2 % 2) * 384:(g2 % 2) * 384 + 384],
                                     in_=smask[:, (g2 % 2) * 384:(g2 % 2) * 384 + 384],
                                     func=AF.Exp, bias=mbias[:, :], scale=ATT_SCALE))
        sc.wait_ge(s_sm2.h, 128)
        po = 15
        s_o1p[po % 2].bump(sc.dma_start(
            out=s1T[po * 128:(po + 1) * 128, :],
            in_=stbuf[:, (po % 2) * T:(po % 2) * T + T]), 16)
        if F is not None:
            F["sems"]["attn1_outp"] = s_o1p
        sc.wait_ge(s_o1p[0].h, s_o1p[0].n)
        sc.wait_ge(s_o1p[1].h, s_o1p[1].n)

    @blk.vector
    def _(v):
        for g2 in range(130):
            if g2 < 128:
                h, qt = g2 // 8, g2 % 8
                sl = g2 % 2
                v.wait_ge(s_sm.h, g2 + 1)
                if g2 >= 2:
                    v.wait_ge(s_exp.h, g2 - 1)
                ac = 0 if qt == 0 else (1 if qt == 1 else 2)
                s_ma.bump(v.tensor_tensor(smask[:, sl * 384:sl * 384 + 384],
                                          psS[sl][:, :], amask[:, ac * 384:ac * 384 + 384],
                                          ALU.add))
            if g2 >= 2:
                gb = g2 - 2
                h, qt = gb // 8, gb % 8
                v.wait_ge(s_pv.h, gb + 1)
                v.wait_ge(s_bcq.h, gb + 1)
                s_rbe.bump(v.tensor_copy(out=rbq[:, :], in_=psB1[:, 0:128]))
                v.wait_ge(s_rbe.h, gb + 1)
                if gb >= 16:
                    v.wait_ge(s_o1p[(gb // 8) % 2].h, ((gb // 8) // 2) * 16)
                s_sm2.bump(v.tensor_tensor(
                    stbuf[:, (h % 2) * T + qt * 128:(h % 2) * T + qt * 128 + 128],
                    psO2[:, :], rbq[:, :], ALU.mult))

            if 1 <= g2 <= 128:
                ga = g2 - 1
                v.wait_ge(s_pv.h, ga + 1)
                s_dne.bump(v.tensor_copy(out=dnr[0:1, 0:128], in_=psD2[0:1, :]))
                v.wait_ge(s_dne.h, ga + 1)
                s_vx.bump(v.reciprocal(dnr[0:1, 128:256], dnr[0:1, 0:128]))
                v.wait_ge(s_vx.h, s_vx.n)
                s_vx.bump(v.tensor_copy(out=dnb[0:1, 0:128], in_=dnr[0:1, 128:256]))
                v.wait_ge(s_vx.h, s_vx.n)
                s_row.bump(v.tensor_tensor(dnb[0:1, 128:256], dnr[0:1, 128:256],
                                           dnb[0:1, 0:128], ALU.subtract))
    if F is not None:
        return None
    kb.ctx.close()
    return kb.nc





def build_l1post(F=None):
    if F is None:
        kb = KB("l1post")
        x1T = kb.dram("x1T", [D, T], F32, "ExternalInput")
        xnT = kb.dram("xnT", [D, T], BF16, "ExternalInput")
        s1T = kb.dram("s1T", [D, T], BF16, "ExternalInput")
        w_out = kb.dram("w_out", [D, D], F32, "ExternalInput")
        W_base = kb.dram("W_base", [D, D], F32, "ExternalInput")
        b_base = kb.dram("b_base", [D], F32, "ExternalInput")
        W_router = kb.dram("W_router", [D, 8], F32, "ExternalInput")
        A_w = kb.dram("A_w", [D, 64], F32, "ExternalInput")
        B_w = kb.dram("B_w", [8, D, 8], F32, "ExternalInput")
        x2T = kb.dram("x2T", [D, T], F32, "ExternalOutput")
        xn_src = None
    else:
        kb = F["kb"]
        ext = F["ext"]
        x1T = ext["x1T"]
        xnT = None
        xn_src = ext["xn1T"]
        s1T = ext["s1T"]
        w_out, W_base, b_base = ext["w_out"], ext["W_base"], ext["b_base"]
        W_router, A_w, B_w = ext["W_router"], ext["A_w"], ext["B_w"]
        x2T = ext["x2T"]
    nc = kb.nc
    if F is None:
        wout_bf = kb.dram("wout_bf", [D, D], BF16)
        wbase_bf = kb.dram("wbase_bf", [D, D], BF16)
        A_bf = kb.dram("A_bf", [D, 64], BF16)
        B_bf = kb.dram("B_bf", [8, D, 8], BF16)
    else:
        wout_bf, wbase_bf = ext["wout_bf"], ext["wbase_bf"]
        A_bf, B_bf = ext["A_bf"], ext["B_bf"]
    Bscr = kb.dram("Bscr", [64, D], BF16)

    xnbf = kb.sb("xnbf", [128, C * T], BF16)
    s1sb = kb.sb("s1sb", [128, C * T], BF16)
    wbuf = kb.sb("wbuf", [128, C * 512], BF16, is_transpose_dest=True)
    wobuf = kb.sb("wobuf", [128, C * 512], BF16, is_transpose_dest=True)
    atile = kb.sb("atile", [128, C * 64], BF16, is_transpose_dest=True)
    wrT = kb.sb("wrT", [128, C * 8], BF16)
    Bsb = kb.sb("Bsb", [128, C * 8], BF16)
    best = kb.sb("best", [8, D], BF16)
    bmat = kb.sb("bmat", [64, D], BF16)
    hT = kb.sb("hT", [64, 8 * 128], BF16)
    btmp = kb.sb("btmp", [16, 128], F32)
    bbT = kb.sb("bbT", [128, 16], F32)
    ident = kb.sb("ident", [128, 128], F32)
    ident_bf = kb.sb("ident_bf", [128, 128], BF16)
    ones8 = kb.sb("ones8", [128, 8], F32)
    lgs = kb.sb("lgs", [128, 8], F32)
    grow = kb.sb("grow", [128, 8], F32)
    eqs = kb.sb("eqs", [128, 16], F32)
    wgt = kb.sb("wgt", [128, 8], F32)
    wexp = kb.sb("wexp", [128, 64], F32)
    hscr = kb.sb("hscr", [128, 64], F32)
    hh = kb.sb("hh", [128, 64], BF16)
    gscr = kb.sb("gscr", [128, 512], F32)
    oscr = kb.sb("oscr", [128, 512], F32)
    tscr = kb.sb("tscr", [128, 512], F32)
    x2sb = kb.sb("x2sb", [128, 2 * 512], F32)
    xstage = kb.sb("xstage", [128, 8 * 512], F32)

    psF = [kb.ps("psF%d" % j, [128, 512]) for j in range(2)]
    psZ = kb.ps("psZ", [128, 512])
    psW = kb.ps("psW", [128, 512])
    psM = kb.ps("psM", [128, 128])

    s_ld = kb.sem("s_ld")
    s_gp = kb.sem("s_gp")
    if F is None:
        cast = {n: kb.sem("cast_" + n) for n in ["wout", "wbase", "A", "B"]}
    else:
        cast = {n: F["cast"][n] for n in ["wout", "wbase", "A", "B"]}
    s_wrl = kb.sem("s_wrl")
    s_al = kb.sem("s_al")
    s_bl = kb.sem("s_bl")
    s_tr = kb.sem("s_tr")
    s_te = kb.sem("s_te")
    s_bs = kb.sem("s_bs")
    s_bm = kb.sem("s_bm")
    s_rm = kb.sem("s_rm")
    s_re = kb.sem("s_re")
    s_gt = kb.sem("s_gt")
    s_gs = kb.sem("s_gs")
    s_hm = kb.sem("s_hm")
    s_he = kb.sem("s_he")
    s_hh = kb.sem("s_hh")
    s_ht = kb.sem("s_ht")
    s_hte = kb.sem("s_hte")
    s_wd = kb.sem("s_wd")
    s_zm = kb.sem("s_zm")
    s_om = kb.sem("s_om")
    s_ge = kb.sem("s_ge")
    s_oe = kb.sem("s_oe")
    s_t1 = kb.sem("s_t1")
    s_x2 = kb.sem("s_x2")
    s_xl = kb.sem("s_xl")
    s_outp = [kb.sem("s_out0"), kb.sem("s_out1")]

    blk = kb.block() if F is None else F["blk"]

    @blk.gpsimd
    def _(g):
        emit_identity(g, ident, s_gp)
        s_gp.bump(g.memset(ones8[:, :], 1.0))
        if F is None:
            cast["wout"].bump(g.dma_start(out=wout_bf, in_=w_out), 16)
            cast["wbase"].bump(g.dma_start(out=wbase_bf, in_=W_base), 16)
            cast["A"].bump(g.dma_start(out=A_bf, in_=A_w), 16)
            cast["B"].bump(g.dma_start(out=B_bf, in_=B_w), 16)
        s_wrl.bump(g.dma_start(out=wrT[:, :].rearrange("p (c r) -> p c r", r=8),
                               in_=W_router.rearrange("(c p) r -> p c r", p=128)), 16)
        for e in range(8):
            g.wait_ge(s_te.h, (e + 1) * 16)
            s_bs.bump(g.dma_start(out=Bscr[8 * e:8 * e + 8, :], in_=best[0:8, :]), 16)
            g.wait_ge(s_bs.h, s_bs.n)
        g.wait_ge(s_bs.h, 128)
        s_bm.bump(g.dma_start(out=bmat[0:64, :], in_=Bscr), 16)

    @blk.vector
    def _(v):
        s_gp.wait(v)
        s_gp.bump(v.tensor_copy(out=ident_bf[:, :], in_=ident[:, :]))

    s_be = emit_bias_load_transpose(blk, kb, [b_base], btmp, bbT, ident, psM,
                                    s_gp, tag="l1p")

    @blk.sync
    def _(sy):
        if F is not None:
            F["sems"]["attn1_outp"][0].wait(sy)
            F["sems"]["attn1_outp"][1].wait(sy)
            s_ld.bump(sy.dma_start(
                out=xnbf[:, :].rearrange("p (c t) -> p c t", t=T),
                in_=xn_src.rearrange("(c p) t -> p c t", p=128)[:, :, HALO:TH]), 16)
        else:
            s_ld.bump(sy.dma_start(out=xnbf[:, :].rearrange("p (c t) -> p c t", t=T),
                                   in_=xnT.rearrange("(c p) t -> p c t", p=128)), 16)
        s_ld.bump(sy.dma_start(out=s1sb[:, :].rearrange("p (c t) -> p c t", t=T),
                               in_=s1T.rearrange("(c p) t -> p c t", p=128)), 16)
        cast["A"].wait(sy)
        for c in range(C):
            s_al.bump(sy.dma_start(out=atile[:, c * 64:(c + 1) * 64],
                                   in_=A_bf[c * 128:(c + 1) * 128, 0:64]), 16)
        cast["B"].wait(sy)
        for e in range(8):
            if e >= 1:
                sy.wait_ge(s_tr.h, e * 16)
            s_bl.bump(sy.dma_start(
                out=Bsb[:, :].rearrange("p (c r) -> p c r", r=8),
                in_=B_bf[e].rearrange("(c p) r -> p c r", p=128)), 16)

    @blk.tensor
    def _(t):
        s_gp.wait(t)
        s_be.wait(t)
        for e in range(8):
            t.wait_ge(s_bl.h, (e + 1) * 16)
            for c in range(C):
                k = e * 16 + c
                if k >= 2:
                    t.wait_ge(s_te.h, k - 1)
                dst = psF[0] if k % 2 == 0 else psF[1]
                s_tr.bump(t.transpose(dst[:, 0:128].bitcast(BF16)[0:8, 0:128],
                                      Bsb[:, c * 8:(c + 1) * 8], ident_bf[:, :]))

    @blk.scalar
    def _(sc):
        for e in range(8):
            for c in range(C):
                k = e * 16 + c
                if c == 0 and e >= 1:
                    sc.wait_ge(s_bs.h, e * 16)
                sc.wait_ge(s_tr.h, k + 1)
                src = psF[0] if k % 2 == 0 else psF[1]
                s_te.bump(sc.activation(out=best[0:8, c * 128:(c + 1) * 128],
                                        in_=src[:, 0:128].bitcast(BF16)[0:8, 0:128],
                                        func=AF.Identity, scale=2.0))
                if c == C - 1:
                    sc.wait_ge(s_te.h, s_te.n)

    @blk.tensor
    def _(t):
        s_ld.wait(t)
        s_al.wait(t)
        s_wrl.wait(t)
        s_te.wait(t)
        for tt in range(8):
            if tt >= 1:
                t.wait_ge(s_re.h, tt)
            for c in range(C):
                inst = t.matmul(psF[0][:, 0:8],
                                lhsT=xnbf[:, c * T + tt * 128:c * T + tt * 128 + 128],
                                rhs=wrT[:, c * 8:(c + 1) * 8],
                                start=(c == 0), stop=(c == C - 1))
            s_rm.bump(inst)
            if tt >= 1:
                t.wait_ge(s_he.h, tt)
            for c in range(C):
                inst = t.matmul(psF[1][:, 0:64],
                                lhsT=xnbf[:, c * T + tt * 128:c * T + tt * 128 + 128],
                                rhs=atile[:, c * 64:(c + 1) * 64],
                                start=(c == 0), stop=(c == C - 1))
            s_hm.bump(inst)
            t.wait_ge(s_hh.h, tt + 1)
            if tt >= 1:
                t.wait_ge(s_hte.h, tt)
            s_ht.bump(t.transpose(psM[:, 0:128].bitcast(BF16)[0:64, 0:128],
                                  hh[:, :], ident_bf[:, :]))

    @blk.vector
    def _(v):
        for tt in range(8):
            v.wait_ge(s_rm.h, tt + 1)
            if tt >= 1:
                v.wait_ge(s_gs.h, tt)
            s_re.bump(v.tensor_copy(out=lgs[:, :], in_=psF[0][:, 0:8]))
            v.wait_ge(s_re.h, tt + 1)
            s_gt.bump(v.reduce_max(grow[:, 0:1], lgs[:, :], axis=AX))
            v.wait_ge(s_gt.h, s_gt.n)
            s_gt.bump(v.tensor_scalar(grow[:, 1:2], grow[:, 0:1], -1.0, None, ALU.mult))
            v.wait_ge(s_gt.h, s_gt.n)
            s_gt.bump(v.tensor_scalar(eqs[:, 0:8], lgs[:, :], grow[:, 0:1], None,
                                      ALU.is_equal))
            v.wait_ge(s_gt.h, s_gt.n)
            s_gt.bump(v.scalar_tensor_tensor(eqs[:, 8:16], eqs[:, 0:8], -1e9,
                                             lgs[:, :], ALU.mult, ALU.add))
            v.wait_ge(s_gt.h, s_gt.n)
            s_gt.bump(v.reduce_max(grow[:, 2:3], eqs[:, 8:16], axis=AX))
            v.wait_ge(s_gt.h, s_gt.n)
            s_gt.bump(v.tensor_scalar(eqs[:, 8:16], eqs[:, 8:16], grow[:, 2:3], None,
                                      ALU.is_equal))
            v.wait_ge(s_gt.h, s_gt.n)
            v.wait_ge(s_gs.h, tt + 1)
            s_gt.bump(v.tensor_scalar(grow[:, 4:5], grow[:, 3:4], 1.0, None, ALU.add))
            v.wait_ge(s_gt.h, s_gt.n)
            s_gt.bump(v.reciprocal(grow[:, 5:6], grow[:, 4:5]))
            v.wait_ge(s_gt.h, s_gt.n)
            s_gt.bump(v.tensor_tensor(grow[:, 6:7], grow[:, 3:4], grow[:, 5:6],
                                      ALU.mult))
            v.wait_ge(s_gt.h, s_gt.n)
            s_gt.bump(v.tensor_scalar(eqs[:, 0:8], eqs[:, 0:8], grow[:, 5:6], None,
                                      ALU.mult))
            v.wait_ge(s_gt.h, s_gt.n)
            s_gt.bump(v.scalar_tensor_tensor(wgt[:, :], eqs[:, 8:16], grow[:, 6:7],
                                             eqs[:, 0:8], ALU.mult, ALU.add))
            v.wait_ge(s_gt.h, s_gt.n)
            for e in range(8):
                s_gt.bump(v.tensor_scalar(wexp[:, 8 * e:8 * e + 8], ones8[:, :],
                                          wgt[:, e:e + 1], None, ALU.mult))
            v.wait_ge(s_gt.h, s_gt.n)
            v.wait_ge(s_hm.h, tt + 1)
            s_he.bump(v.tensor_copy(out=hscr[:, :], in_=psF[1][:, 0:64]))
            v.wait_ge(s_he.h, tt + 1)
            if tt >= 1:
                v.wait_ge(s_ht.h, tt)
            s_hh.bump(v.tensor_tensor(hh[:, :], hscr[:, :], wexp[:, :], ALU.mult))
            v.wait_ge(s_hh.h, s_hh.n)

    @blk.scalar
    def _(sc):
        for tt in range(8):
            sc.wait_ge(s_gt.h, tt * 19 + 6)
            s_gs.bump(sc.activation(out=grow[:, 3:4], in_=grow[:, 2:3], func=AF.Exp,
                                    bias=grow[:, 1:2]))
            sc.wait_ge(s_ht.h, tt + 1)
            s_hte.bump(sc.activation(out=hT[0:64, tt * 128:(tt + 1) * 128],
                                     in_=psM[:, 0:128].bitcast(BF16)[0:64, 0:128],
                                     func=AF.Identity))

    @blk.sync
    def _(sy):
        cast["wbase"].wait(sy)
        cast["wout"].wait(sy)
        for og in range(4):
            if og >= 1:
                sy.wait_ge(s_zm.h, og * 8)
                sy.wait_ge(s_om.h, og * 8)
            for dc in range(C):
                s_wd.bump(sy.dma_start(
                    out=wbuf[:, dc * 512:dc * 512 + 512],
                    in_=wbase_bf[dc * 128:(dc + 1) * 128, og * 512:og * 512 + 512]), 16)
            for dc in range(C):
                s_wd.bump(sy.dma_start(
                    out=wobuf[:, dc * 512:dc * 512 + 512],
                    in_=wout_bf[dc * 128:(dc + 1) * 128, og * 512:og * 512 + 512]), 16)
            if og >= 1:
                sy.wait_ge(s_x2.h, og * 8)
            for oc4 in range(4):
                oc = og * 4 + oc4
                for tp in range(2):
                    s_xl.bump(sy.dma_start(
                        out=xstage[:, (oc4 * 2 + tp) * 512:(oc4 * 2 + tp) * 512 + 512],
                        in_=x1T[oc * 128:oc * 128 + 128, tp * 512:tp * 512 + 512]), 16)

    @blk.tensor
    def _(t):
        s_bm.wait(t)
        s_hte.wait(t)
        for og in range(4):
            t.wait_ge(s_wd.h, (og + 1) * 2 * C * 16)
            for oc4 in range(4):
                oc = og * 4 + oc4
                for tp in range(2):
                    k = (og * 4 + oc4) * 2 + tp
                    if k >= 1:
                        t.wait_ge(s_ge.h, k)
                    for dc in range(C):
                        t.matmul(psZ[:, :],
                                 lhsT=wbuf[:, dc * 512 + oc4 * 128:dc * 512 + oc4 * 128 + 128],
                                 rhs=xnbf[:, dc * T + tp * 512:dc * T + tp * 512 + 512],
                                 start=(dc == 0), stop=False)
                    s_zm.bump(t.matmul(psZ[:, :], lhsT=bmat[0:64, oc * 128:oc * 128 + 128],
                                       rhs=hT[0:64, tp * 512:tp * 512 + 512],
                                       start=False, stop=True))
                    if k >= 1:
                        t.wait_ge(s_oe.h, k)
                    for dc in range(C):
                        inst = t.matmul(psW[:, :],
                                        lhsT=wobuf[:, dc * 512 + oc4 * 128:dc * 512 + oc4 * 128 + 128],
                                        rhs=s1sb[:, dc * T + tp * 512:dc * T + tp * 512 + 512],
                                        start=(dc == 0), stop=(dc == C - 1))
                    s_om.bump(inst)

    @blk.scalar
    def _(sc):
        for og in range(4):
            for oc4 in range(4):
                oc = og * 4 + oc4
                for tp in range(2):
                    k = (og * 4 + oc4) * 2 + tp
                    sc.wait_ge(s_zm.h, k + 1)
                    if k >= 1:
                        sc.wait_ge(s_t1.h, k)
                    s_ge.bump(sc.activation(out=gscr[:, :], in_=psZ[:, :],
                                            func=AF.Silu, bias=bbT[:, oc:oc + 1]))
                    sc.wait_ge(s_om.h, k + 1)
                    s_oe.bump(sc.activation(out=oscr[:, :], in_=psW[:, :],
                                            func=AF.Identity))
                    sc.wait_ge(s_x2.h, k)
                    if k >= 1:
                        po = k - 1
                        s_outp[po % 2].bump(sc.dma_start(
                            out=x2T[(po // 2) * 128:(po // 2) * 128 + 128,
                                    (po % 2) * 512:(po % 2) * 512 + 512],
                            in_=x2sb[:, (po % 2) * 512:(po % 2) * 512 + 512]), 16)

    @blk.vector
    def _(v):
        for og in range(4):
            v.wait_ge(s_xl.h, (og + 1) * 128)
            for oc4 in range(4):
                for tp in range(2):
                    k = (og * 4 + oc4) * 2 + tp
                    v.wait_ge(s_ge.h, k + 1)
                    v.wait_ge(s_oe.h, k + 1)
                    if k >= 1:
                        v.wait_ge(s_x2.h, k)
                    s_t1.bump(v.tensor_tensor(tscr[:, :], oscr[:, :], gscr[:, :],
                                              ALU.mult))
                    v.wait_ge(s_t1.h, s_t1.n)
                    if k >= 2:
                        v.wait_ge(s_outp[k % 2].h, (k // 2) * 16)
                    s_x2.bump(v.tensor_tensor(
                        x2sb[:, (k % 2) * 512:(k % 2) * 512 + 512],
                        xstage[:, (oc4 * 2 + tp) * 512:(oc4 * 2 + tp) * 512 + 512],
                        tscr[:, :], ALU.add))

    @blk.scalar
    def _(sc):
        sc.wait_ge(s_x2.h, 32)
        po = 31
        s_outp[po % 2].bump(sc.dma_start(
            out=x2T[(po // 2) * 128:(po // 2) * 128 + 128,
                    (po % 2) * 512:(po % 2) * 512 + 512],
            in_=x2sb[:, (po % 2) * 512:(po % 2) * 512 + 512]), 16)
        sc.wait_ge(s_outp[0].h, s_outp[0].n)
        sc.wait_ge(s_outp[1].h, s_outp[1].n)

    if F is not None:
        F["sems"]["l1post_outp"] = s_outp
        return None
    kb.ctx.close()
    return kb.nc





class NullCtr:

    def wait(self, eng, val=None):
        pass

    def __getitem__(self, i):
        return self


def emit_phase_barrier(blk, kb, tag, clear=()):
    ctr = kb.sem_outer("bar_" + tag)

    def one(eng):
        eng.drain()
        eng.sem_inc(ctr.h, 1)
        eng.wait_ge(ctr.h, 6)

    @blk.sync
    def _(sy):
        one(sy)

    @blk.tensor
    def _(t):
        one(t)

    @blk.scalar
    def _(sc):
        one(sc)

    @blk.vector
    def _(v):
        one(v)

    @blk.gpsimd
    def _(g):
        g.drain()
        g.sem_inc(ctr.h, 1)
        g.wait_ge(ctr.h, 5)
        for c in clear:
            g.sem_clear(c.h)
        g.sem_inc(ctr.h, 1)
    ctr.n = 6
    return ctr


def build_fused(upto=9):
    kb = KB("fused")
    nc = kb.nc
    ext = {}
    ext["xT"] = kb.dram("xT", [D, T], F32, "ExternalInput")
    for n in "qkv":
        ext["W" + n] = kb.dram("W" + n, [D, D], F32, "ExternalInput")
        ext["b" + n] = kb.dram("b" + n, [D], F32, "ExternalInput")
        ext["B" + n] = kb.dram("B" + n, [16, D], F32, "ExternalInput")
        ext["A" + n] = kb.dram("A" + n, [16, D], F32, "ExternalInput")
    ext["Wo"] = kb.dram("Wo", [D, D], F32, "ExternalInput")
    ext["bo"] = kb.dram("bo", [D], F32, "ExternalInput")
    ext["Bo"] = kb.dram("Bo", [16, D], F32, "ExternalInput")
    ext["Ao"] = kb.dram("Ao", [16, D], F32, "ExternalInput")
    ext["gate_w"] = kb.dram("gate_w", [D, INTER], F32, "ExternalInput")
    ext["up_w"] = kb.dram("up_w", [D, INTER], F32, "ExternalInput")
    ext["down_w"] = kb.dram("down_w", [INTER, D], F32, "ExternalInput")
    ext["gate_s"] = kb.dram("gate_s", [INTER], F32, "ExternalInput")
    ext["up_s"] = kb.dram("up_s", [INTER], F32, "ExternalInput")
    ext["down_s"] = kb.dram("down_s", [D], F32, "ExternalInput")
    ext["w_qkv"] = kb.dram("w_qkv", [D, 3 * D], F32, "ExternalInput")
    ext["w_c1"] = kb.dram("w_c1", [D, 512], F32, "ExternalInput")
    ext["w_c2"] = kb.dram("w_c2", [1, 512], F32, "ExternalInput")
    ext["w_out"] = kb.dram("w_out", [D, D], F32, "ExternalInput")
    ext["W_base"] = kb.dram("W_base", [D, D], F32, "ExternalInput")
    ext["b_base"] = kb.dram("b_base", [D], F32, "ExternalInput")
    ext["W_router"] = kb.dram("W_router", [D, 8], F32, "ExternalInput")
    ext["A_w"] = kb.dram("A_w", [D, 64], F32, "ExternalInput")
    ext["B_w"] = kb.dram("B_w", [8, D, 8], F32, "ExternalInput")
    ext["meta"] = kb.dram("meta", [1, 4], F32, "ExternalInput")
    ext["x2T"] = kb.dram("x2T", [D, T], F32, "ExternalOutput")
    for n in "qkv":
        ext[n + "T"] = kb.dram(n + "T", [D, T], BF16)
    ext["xnT"] = kb.dram("xnT", [D, T], BF16)
    ext["sT"] = kb.dram("sT", [D, T], BF16)
    ext["x1T"] = kb.dram("x1T", [D, T], F32)
    ext["q1T"] = kb.dram("q1T", [D, T], BF16)
    ext["k1T"] = kb.dram("k1T", [D, TH], BF16)
    ext["v1T"] = kb.dram("v1T", [D, TH], BF16)
    ext["xn1T"] = kb.dram("xn1T", [D, TH], BF16)
    ext["s1T"] = kb.dram("s1T", [D, T], BF16)
    ext["kTg"] = kb.dram("kTg", [NC * D, T], BF16)
    ext["vTg"] = kb.dram("vTg", [NC * D, T], BF16)
    ext["x1tail"] = kb.dram("x1tail", [D, HALO], F32)
    ext["x1tails_g"] = kb.dram("x1tails_g", [NC * D, HALO], F32)
    ext["x1halo"] = kb.dram("x1halo", [D, HALO], F32)
    ext["cc_stats_in"] = kb.dram("cc_stats_in", [128, 18], F32)
    ext["cc_stats_out"] = kb.dram("cc_stats_out", [128, 18], F32)
    ext["Wo_bf"] = kb.dram("Wo_bf", [D, D], BF16)
    ext["gate_bf"] = kb.dram("gate_bf", [D, INTER], BF16)
    ext["up_bf"] = kb.dram("up_bf", [D, INTER], BF16)
    ext["down_bf"] = kb.dram("down_bf", [INTER, D], BF16)
    ext["wq_bf"] = kb.dram("wq_bf", [D, 3 * D], BF16)
    ext["wc1_bf"] = kb.dram("wc1_bf", [D, 512], BF16)
    ext["wc2_bf"] = kb.dram("wc2_bf", [1, 512], BF16)
    ext["wout_bf"] = kb.dram("wout_bf", [D, D], BF16)
    ext["wbase_bf"] = kb.dram("wbase_bf", [D, D], BF16)
    ext["A_bf"] = kb.dram("A_bf", [D, 64], BF16)
    ext["B_bf"] = kb.dram("B_bf", [8, D, 8], BF16)

    blk = kb.block()
    F = {"kb": kb, "blk": blk, "ext": ext, "sems": {}}
    F["cast"] = {n: kb.sem_outer("cast_" + n)
                 for n in ["Wo", "Bo", "gate", "up", "down", "wq", "wc",
                           "wout", "wbase", "A", "B"]}

    with kb.phase("p0"):
        build_l0pre(F=F)

        @blk.gpsimd
        def _(g):
            FC = F["cast"]
            FC["gate"].bump(g.dma_start(out=ext["gate_bf"], in_=ext["gate_w"]), 16)
            FC["up"].bump(g.dma_start(out=ext["up_bf"], in_=ext["up_w"]), 16)
            FC["Wo"].bump(g.dma_start(out=ext["Wo_bf"], in_=ext["Wo"]), 16)
            FC["down"].bump(g.dma_start(out=ext["down_bf"], in_=ext["down_w"]), 16)

        cc_kv = kb.sem_outer("cc_kv")
        F["sems"]["cc_kv"] = cc_kv

        @blk.gpsimd
        def _(g):
            g.wait_ge(F["sems"]["l0pre_outp"][0].h, 512)
            g.wait_ge(F["sems"]["l0pre_outp"][1].h, 512)
            cc_kv.bump(g.collective_compute(
                "AllGather", ALU.bypass, replica_groups=[list(range(NC))],
                ins=[ext["kT"].opt()], outs=[ext["kTg"].opt()]))
            cc_kv.bump(g.collective_compute(
                "AllGather", ALU.bypass, replica_groups=[list(range(NC))],
                ins=[ext["vT"].opt()], outs=[ext["vTg"].opt()]))

        emit_phase_barrier(blk, kb, "b1", clear=kb.sem_track[-1])
    if upto <= 0:
        kb.ctx.close()
        return kb.nc
    F["sems"]["l0pre_outp"] = NullCtr()
    F["sems"]["l0pre_xn"] = NullCtr()
    with kb.phase("p1"):
        build_attn0(F=F)

        @blk.gpsimd
        def _(g):
            FC = F["cast"]
            FC["wq"].bump(g.dma_start(out=ext["wq_bf"], in_=ext["w_qkv"]), 16)
            FC["wc"].bump(g.dma_start(out=ext["wc1_bf"], in_=ext["w_c1"]), 16)
            FC["wc"].bump(g.dma_start(out=ext["wc2_bf"], in_=ext["w_c2"]), 16)
            FC["wout"].bump(g.dma_start(out=ext["wout_bf"], in_=ext["w_out"]), 16)
            FC["wbase"].bump(g.dma_start(out=ext["wbase_bf"], in_=ext["W_base"]), 16)
            FC["A"].bump(g.dma_start(out=ext["A_bf"], in_=ext["A_w"]), 16)
            FC["B"].bump(g.dma_start(out=ext["B_bf"], in_=ext["B_w"]), 16)
        emit_phase_barrier(blk, kb, "b2", clear=kb.sem_track[-1])
    if upto <= 1:
        kb.ctx.close()
        return kb.nc
    F["sems"]["attn0_outp"] = NullCtr()
    with kb.phase("p2"):
        build_l0post(F=F)
        emit_phase_barrier(blk, kb, "b2b", clear=kb.sem_track[-1])
    if upto <= 2:
        kb.ctx.close()
        return kb.nc
    F["sems"]["l0post_outp"] = NullCtr()

    cc_halo = kb.sem_outer("cc_halo")
    F["sems"]["cc_halo"] = cc_halo
    with kb.phase("halo"):
        hstg = kb.sb("hstg", [128, HALO], F32)
        hidxs = kb.sb("hidxs", [128, C], mybir.dt.int32)
        hidxf = kb.sb("hidxf", [128, C], F32)
        hbase = kb.sb("hbase", [128, 1], F32)
        hmsb = kb.sb("hmsb", [1, 4], F32)
        hmb = kb.sb("hmb", [1, 1], BF16)
        hones = kb.sb("hones", [1, 128], BF16)
        psH1 = kb.ps("psH1", [128, 1])
        s_hx = kb.sem("s_hx")
        s_hv2 = kb.sem("s_hv2")
        s_ht2 = kb.sem("s_ht2")

        @blk.gpsimd
        def _(g):
            s_hx.bump(g.memset(hones[:, :], 1.0))
            s_hx.bump(g.iota(hidxf[:, :], pattern=[[128, C]], base=0,
                             channel_multiplier=1,
                             allow_small_or_imprecise_dtypes=True))
            F["sems"]["l0post_outp"][0].wait(g)
            F["sems"]["l0post_outp"][1].wait(g)
            s_hx.bump(g.dma_start(out=ext["x1tail"], in_=ext["x1T"][:, T - HALO:T]), 16)
            g.wait_ge(s_hx.h, s_hx.n)
            cc_halo.bump(g.collective_compute(
                "AllGather", ALU.bypass, replica_groups=[list(range(NC))],
                ins=[ext["x1tail"].opt()], outs=[ext["x1tails_g"].opt()]))
            s_hx.bump(g.dma_start(out=hmsb[:, :], in_=ext["meta"]), 16)

        @blk.vector
        def _(v):
            v.wait_ge(s_hx.h, s_hx.n)
            s_hv2.bump(v.tensor_copy(out=hmb[:, :], in_=hmsb[0:1, 2:3]))

        @blk.tensor
        def _(t):
            s_hv2.wait(t)
            s_ht2.bump(t.matmul(psH1[:, 0:1], lhsT=hones[0:1, :], rhs=hmb[0:1, 0:1],
                                start=True, stop=True))

        @blk.vector
        def _(v):
            s_ht2.wait(v)
            s_hv2.bump(v.tensor_copy(out=hbase[:, :], in_=psH1[:, 0:1]))
            v.wait_ge(s_hv2.h, s_hv2.n)
            s_hv2.bump(v.tensor_scalar(hidxf[:, :], hidxf[:, :], hbase[:, :], None,
                                       ALU.add))
            v.wait_ge(s_hv2.h, s_hv2.n)
            s_hv2.bump(v.tensor_copy(out=hidxs[:, :], in_=hidxf[:, :]))
            v.wait_ge(s_hv2.h, s_hv2.n)

        @blk.gpsimd
        def _(g):
            g.wait_ge(s_hv2.h, s_hv2.n)
            g.wait_ge(cc_halo.h, 1)
            for c in range(C):
                s_hx.bump(g.memset(hstg[:, :], 0.0))
                g.wait_ge(s_hx.h, s_hx.n)
                s_hx.bump(g.indirect_dma_start(
                    out=hstg[:, :], out_offset=None,
                    in_=ext["x1tails_g"],
                    in_offset=bass.IndirectOffsetOnAxis(ap=hidxs[:, c:c + 1], axis=0)), 16)
                g.wait_ge(s_hx.h, s_hx.n)
                s_hx.bump(g.dma_start(out=ext["x1halo"][c * 128:(c + 1) * 128, :],
                                      in_=hstg[:, :]), 16)
                g.wait_ge(s_hx.h, s_hx.n)
            cc_halo.bump(g.nop())
            g.wait_ge(cc_halo.h, cc_halo.n)

        emit_phase_barrier(blk, kb, "b3", clear=kb.sem_track[-1])
    if upto <= 3:
        kb.ctx.close()
        return kb.nc

    F["sems"]["cc_halo"] = NullCtr()
    with kb.phase("p3"):
        build_l1pre(F=F)
        emit_phase_barrier(blk, kb, "b4", clear=kb.sem_track[-1])
    if upto <= 4:
        kb.ctx.close()
        return kb.nc
    F["sems"]["l1pre_outp"] = NullCtr()
    F["sems"]["l1pre_xn"] = NullCtr()
    with kb.phase("p4"):
        build_attn1(F=F)
        emit_phase_barrier(blk, kb, "b5", clear=kb.sem_track[-1])
    if upto <= 5:
        kb.ctx.close()
        return kb.nc
    F["sems"]["attn1_outp"] = NullCtr()
    with kb.phase("p5"):
        build_l1post(F=F)

        @blk.sync
        def _(sy):
            F["sems"]["l1post_outp"][0].wait(sy)
            F["sems"]["l1post_outp"][1].wait(sy)

    kb.ctx.close()
    return kb.nc


_builders = {}


def get_nc(name):
    if name not in _builders:
        _builders[name] = {"l0pre": build_l0pre, "attn0": build_attn0, "l0post": build_l0post, "l1pre": build_l1pre, "attn1": build_attn1, "l1post": build_l1post, "fused": build_fused}[name]()
    return _builders[name]


def kernel(**inputs):
    x = np.asarray(inputs["L0_attn_x"], np.float32)[0]
    nc = get_nc("fused")
    names_f32 = {
        "Wq": "L0_attn_Wq", "bq": "L0_attn_bq", "Bq": "L0_attn_Bq", "Aq": "L0_attn_Aq",
        "Wk": "L0_attn_Wk", "bk": "L0_attn_bk", "Bk": "L0_attn_Bk", "Ak": "L0_attn_Ak",
        "Wv": "L0_attn_Wv", "bv": "L0_attn_bv", "Bv": "L0_attn_Bv", "Av": "L0_attn_Av",
        "Wo": "L0_attn_Wo", "bo": "L0_attn_bo", "Bo": "L0_attn_Bo", "Ao": "L0_attn_Ao",
        "gate_w": "L0_ffn_gate_w", "up_w": "L0_ffn_up_w", "down_w": "L0_ffn_down_w",
        "gate_s": "L0_ffn_gate_s", "up_s": "L0_ffn_up_s", "down_s": "L0_ffn_down_s",
        "w_qkv": "L1_attn_w_qkv", "w_c1": "L1_attn_w_c1", "w_c2": "L1_attn_w_c2",
        "w_out": "L1_attn_w_out", "W_base": "L1_ffn_W_base", "b_base": "L1_ffn_b_base",
        "W_router": "L1_ffn_W_router", "A_w": "L1_ffn_A_w", "B_w": "L1_ffn_B_w",
    }
    transposed = {"Wq", "Wk", "Wv", "Wo", "W_base", "w_out", "w_qkv", "w_c1",
                  "gate_w", "up_w", "down_w",
                  "W_router", "Aq", "Ak", "Av", "Ao"}
    weights = {}
    for k, v in names_f32.items():
        a = np.asarray(inputs[v], np.float32)
        if k == "A_w":
            a = a.reshape(64, D).T
        elif k in transposed:
            a = a.T
        weights[k] = np.ascontiguousarray(a)
    maps = []
    for c in range(NC):
        m = dict(weights)
        m["xT"] = np.ascontiguousarray(x[c * T:(c + 1) * T].T)
        hoff = float((c - 1) * D) if c >= 1 else 0.0
        m["meta"] = np.array([[float(c * T - HALO), 0.0 if c == 0 else 1.0,
                               hoff, 0.0]], np.float32)
        maps.append(m)
    res = run_bass_kernel_spmd(nc, maps, core_ids=list(range(NC))).results
    x2 = np.concatenate([np.asarray(res[c]["x2T"], np.float32) for c in range(NC)],
                        axis=1)
    return np.ascontiguousarray(x2.T)[None, :, :]


def kernel_unfused(**inputs):

    x = np.asarray(inputs["L0_attn_x"], np.float32)[0]

    nc = get_nc("l0pre")
    maps = []
    for c in range(NC):
        m = {"xT": np.ascontiguousarray(x[c * T:(c + 1) * T].T)}
        for n in "qkv":
            m["W" + n] = np.asarray(inputs[f"L0_attn_W{n}"], np.float32)
            m["b" + n] = np.asarray(inputs[f"L0_attn_b{n}"], np.float32)
            m["B" + n] = np.asarray(inputs[f"L0_attn_B{n}"], np.float32)
            m["A" + n] = np.asarray(inputs[f"L0_attn_A{n}"], np.float32)
        maps.append(m)
    r1 = run_bass_kernel_spmd(nc, maps, core_ids=list(range(NC))).results
    qT = [np.asarray(r1[c]["qT"]) for c in range(NC)]
    kTf = np.ascontiguousarray(np.concatenate([np.asarray(r1[c]["kT"]) for c in range(NC)], axis=1))
    vTf = np.ascontiguousarray(np.concatenate([np.asarray(r1[c]["vT"]) for c in range(NC)], axis=1))
    xnT = [np.asarray(r1[c]["xnT"]) for c in range(NC)]

    nc = get_nc("attn0")
    maps = [{"qT": qT[c], "kTf": kTf, "vTf": vTf} for c in range(NC)]
    r2 = run_bass_kernel_spmd(nc, maps, core_ids=list(range(NC))).results
    sT = [np.asarray(r2[c]["sT"]) for c in range(NC)]

    nc = get_nc("l0post")
    maps = []
    for c in range(NC):
        maps.append({
            "xT": np.ascontiguousarray(x[c * T:(c + 1) * T].T),
            "xnT": xnT[c], "sT": sT[c],
            "Wo": np.asarray(inputs["L0_attn_Wo"], np.float32),
            "bo": np.asarray(inputs["L0_attn_bo"], np.float32),
            "Bo": np.asarray(inputs["L0_attn_Bo"], np.float32),
            "Ao": np.asarray(inputs["L0_attn_Ao"], np.float32),
            "gate_w": np.asarray(inputs["L0_ffn_gate_w"], np.float32),
            "up_w": np.asarray(inputs["L0_ffn_up_w"], np.float32),
            "down_w": np.asarray(inputs["L0_ffn_down_w"], np.float32),
            "gate_s": np.asarray(inputs["L0_ffn_gate_s"], np.float32),
            "up_s": np.asarray(inputs["L0_ffn_up_s"], np.float32),
            "down_s": np.asarray(inputs["L0_ffn_down_s"], np.float32),
        })
    r3 = run_bass_kernel_spmd(nc, maps, core_ids=list(range(NC))).results
    x1T = [np.ascontiguousarray(np.asarray(r3[c]["x1T"], np.float32)) for c in range(NC)]

    nc = get_nc("l1pre")
    maps = []
    metas = []
    for c in range(NC):
        xh = np.zeros((D, TH), np.float32)
        if c == 0:
            xh[:, HALO:] = x1T[0]
        else:
            xh[:, :HALO] = x1T[c - 1][:, T - HALO:]
            xh[:, HALO:] = x1T[c]
        meta = np.array([[float(c * T - HALO), 0.0 if c == 0 else 1.0, 0.0, 0.0]],
                        np.float32)
        metas.append(meta)
        maps.append({"x1Th": xh, "w_qkv": np.asarray(inputs["L1_attn_w_qkv"], np.float32),
                     "meta": meta})
    r4 = run_bass_kernel_spmd(nc, maps, core_ids=list(range(NC))).results
    q1T = [np.asarray(r4[c]["q1T"]) for c in range(NC)]
    k1T = [np.asarray(r4[c]["k1T"]) for c in range(NC)]
    v1T = [np.asarray(r4[c]["v1T"]) for c in range(NC)]
    xn1T = [np.asarray(r4[c]["xn1T"]) for c in range(NC)]
    xn1f = np.ascontiguousarray(np.concatenate([xn1T[c][:, HALO:] for c in range(NC)],
                                               axis=1))

    nc = get_nc("attn1")
    maps = [{"q1T": q1T[c], "k1T": k1T[c], "v1T": v1T[c], "xn1f": xn1f,
             "w_c1": np.asarray(inputs["L1_attn_w_c1"], np.float32),
             "w_c2": np.asarray(inputs["L1_attn_w_c2"], np.float32),
             "meta": metas[c]} for c in range(NC)]
    r5 = run_bass_kernel_spmd(nc, maps, core_ids=list(range(NC))).results
    s1T = [np.asarray(r5[c]["s1T"]) for c in range(NC)]

    nc = get_nc("l1post")
    maps = [{"x1T": x1T[c],
             "xnT": np.ascontiguousarray(xn1T[c][:, HALO:]),
             "s1T": s1T[c],
             "w_out": np.asarray(inputs["L1_attn_w_out"], np.float32),
             "W_base": np.asarray(inputs["L1_ffn_W_base"], np.float32),
             "b_base": np.asarray(inputs["L1_ffn_b_base"], np.float32),
             "W_router": np.asarray(inputs["L1_ffn_W_router"], np.float32),
             "A_w": np.asarray(inputs["L1_ffn_A_w"], np.float32),
             "B_w": np.asarray(inputs["L1_ffn_B_w"], np.float32)} for c in range(NC)]
    r6 = run_bass_kernel_spmd(nc, maps, core_ids=list(range(NC))).results
    x2 = np.concatenate([np.asarray(r6[c]["x2T"], np.float32) for c in range(NC)],
                        axis=1)
    return np.ascontiguousarray(x2.T)[None, :, :]


if __name__ == "__main__":
    print("kernel module ok")


# version 08
# speedup vs baseline: 1.0184x; 1.0184x over previous
import sys
sys.path.insert(0, "/opt/trn_rl_repo")
from contextlib import ExitStack

import numpy as np
import ml_dtypes

import concourse.bass as bass
import concourse.mybir as mybir
from concourse.bass_utils import run_bass_kernel_spmd

F32 = mybir.dt.float32
BF16 = mybir.dt.bfloat16
AF = mybir.ActivationFunctionType
ALU = mybir.AluOpType
AX = mybir.AxisListType.X

NC = 8
S = 8192
D = 2048
T = S // NC
C = D // 128
H = 16
HD = 128
HALO = 256
TH = T + HALO
INTER = 5632
IC = INTER // 128
ATT_SCALE = float(HD) ** -0.5
MB = -25.0
BIG = 1e9
LN_EPS = 1e-5


class Ctr:
    __slots__ = ("h", "n")

    def __init__(self, h):
        self.h = h
        self.n = 0

    def bump(self, inst, k=1):
        inst.then_inc(self.h, k)
        self.n += k
        return self.n

    def wait(self, eng, val=None):
        eng.wait_ge(self.h, self.n if val is None else val)


class KB:

    def __init__(self, name):
        self.nc = bass.Bass("TRN2", target_bir_lowering=False, name=name)
        self.ctx = ExitStack()
        self.stack = [self.ctx]
        self.sem_track = [[]]
        self.prefix = ""

    def sb(self, name, shape, dt, **kw):
        return self.stack[-1].enter_context(
            self.nc.sbuf_tensor(self.prefix + name, shape, dt, **kw))

    def ps(self, name, shape, dt=F32):
        return self.stack[-1].enter_context(
            self.nc.psum_tensor(self.prefix + name, shape, dt))

    def sem(self, name):
        h = self.stack[-1].enter_context(self.nc.semaphore(self.prefix + name))
        c = Ctr(h)
        self.sem_track[-1].append(c)
        return c

    def sem_outer(self, name):
        h = self.ctx.enter_context(self.nc.semaphore(self.prefix + name))
        c = Ctr(h)
        self.sem_track[0].append(c)
        return c

    def dram(self, name, shape, dt, kind="Internal"):
        return self.nc.dram_tensor(name, shape, dt, kind=kind).ap()

    def phase(self, tag):
        from contextlib import contextmanager

        @contextmanager
        def _p():
            es = ExitStack()
            self.stack.append(es)
            self.sem_track.append([])
            old = self.prefix
            self.prefix = tag + "_"
            try:
                yield
            finally:
                es.close()
                self.stack.pop()
                self.sem_track.pop()
                self.prefix = old
        return _p()

    def block(self):
        return self.ctx.enter_context(self.nc.Block())


def cbf(a):
    return np.ascontiguousarray(a).astype(ml_dtypes.bfloat16)


def shard_T(x2d):
    return [np.ascontiguousarray(x2d[i * T:(i + 1) * T]) for i in range(NC)]



def emit_identity(g, ident, s_gp):
    s_gp.bump(g.memset(ident[:, :], 0.0))
    g.wait_ge(s_gp.h, s_gp.n)
    s_gp.bump(g.affine_select(out=ident[:, :], in_=ident[:, :],
                              compare_op=ALU.not_equal, fill=1.0, base=0,
                              pattern=[[-1, ident.shape[1]]], channel_multiplier=1))


def emit_ln(blk, kb, xts, xnbf, Tn, s_x, s_xn, psA, psB, psC, psD,
            xsq, lnrows, negMb, Rb, lnscr, ones_bf, onesrow_bf, epsb,
            var_row_out=None, tag="", s_gp=None, pre_bcast_waits=()):
    nc = kb.nc
    s_sq = kb.sem("lnsq" + tag)
    s_t2 = kb.sem("lnt2" + tag)
    s_vx = kb.sem("lnvx" + tag)
    s_sc = kb.sem("lnsc" + tag)
    nhalf = (Tn + 511) // 512
    xbh = xsq

    @blk.scalar
    def _(sc):
        s_x.wait(sc)
        for c in range(C):
            if c >= 1:
                s_t2.wait(sc, c)
            s_sq.bump(sc.activation(out=xbh[:, 0:Tn],
                                    in_=xts[:, c * Tn:(c + 1) * Tn], func=AF.Identity))
            s_sq.bump(sc.activation(out=xbh[:, Tn:2 * Tn],
                                    in_=xts[:, c * Tn:(c + 1) * Tn], func=AF.Square))

    @blk.tensor
    def _(t):
        if s_gp is not None:
            s_gp.wait(t)
        for c in range(C):
            s_sq.wait(t, 2 * (c + 1))
            for hf in range(nhalf):
                w = min(512, Tn - hf * 512)
                t.matmul(psA[0:1, hf * 512:hf * 512 + w], lhsT=ones_bf[:, :],
                         rhs=xbh[:, hf * 512:hf * 512 + w],
                         start=(c == 0), stop=(c == C - 1))
            for hf in range(nhalf):
                w = min(512, Tn - hf * 512)
                inst = t.matmul(psB[0:1, hf * 512:hf * 512 + w], lhsT=ones_bf[:, :],
                                rhs=xbh[:, Tn + hf * 512:Tn + hf * 512 + w],
                                start=(c == 0), stop=(c == C - 1))
                if hf == nhalf - 1:
                    s_t2.bump(inst)

    NM, MS, VR, SD = 0, Tn, 2 * Tn, 3 * Tn
    HB = 4 * Tn
    lnrows_bf = lnrows[0:1, HB:HB + 2 * Tn].bitcast(BF16)

    @blk.vector
    def _(v):
        s_t2.wait(v, C)
        s_vx.bump(v.tensor_scalar(lnrows[0:1, NM:NM + Tn], psA[0:1, 0:Tn],
                                  -1.0 / D, None, ALU.mult))
        s_vx.bump(v.tensor_scalar(lnrows[0:1, MS:MS + Tn], psB[0:1, 0:Tn],
                                  1.0 / D, None, ALU.mult))
        v.wait_ge(s_vx.h, s_vx.n)
        s_vx.bump(v.tensor_tensor(lnrows[0:1, VR:VR + Tn], lnrows[0:1, NM:NM + Tn],
                                  lnrows[0:1, NM:NM + Tn], ALU.mult))
        v.wait_ge(s_vx.h, s_vx.n)
        s_vx.bump(v.tensor_tensor(lnrows[0:1, VR:VR + Tn], lnrows[0:1, MS:MS + Tn],
                                  lnrows[0:1, VR:VR + Tn], ALU.subtract))
        if var_row_out is not None:
            v.wait_ge(s_vx.h, s_vx.n)
            s_vx.bump(v.tensor_copy(out=var_row_out[0:1, 0:Tn],
                                    in_=lnrows[0:1, VR:VR + Tn]))

    @blk.scalar
    def _(sc):
        s_vx.wait(sc)
        s_sc.bump(sc.activation(out=lnrows[0:1, SD:SD + Tn], in_=lnrows[0:1, VR:VR + Tn],
                                func=AF.Sqrt, bias=epsb[0:1, 0:1]))

    @blk.vector
    def _(v):
        s_sc.wait(v)
        s_vx.bump(v.reciprocal(lnrows[0:1, VR:VR + Tn], lnrows[0:1, SD:SD + Tn]))
        for i, src in enumerate([NM, VR]):
            v.wait_ge(s_vx.h, s_vx.n)
            s_vx.bump(v.tensor_copy(out=lnrows_bf[0:1, (2 * i) * Tn:(2 * i) * Tn + Tn],
                                    in_=lnrows[0:1, src:src + Tn]))
            v.wait_ge(s_vx.h, s_vx.n)
            s_vx.bump(v.tensor_tensor(lnrows_bf[0:1, (2 * i + 1) * Tn:(2 * i + 2) * Tn],
                                      lnrows[0:1, src:src + Tn],
                                      lnrows_bf[0:1, (2 * i) * Tn:(2 * i) * Tn + Tn],
                                      ALU.subtract))

    s_bc = kb.sem("lnbc" + tag)
    s_bv = kb.sem("lnbv" + tag)

    @blk.tensor
    def _(t):
        s_vx.wait(t)
        for w_ in pre_bcast_waits:
            w_.wait(t)
        for i, dstT in enumerate([negMb, Rb]):
            if i == 1:
                s_bv.wait(t, nhalf)
            for hf in range(nhalf):
                w = min(512, Tn - hf * 512)
                dst = psC if hf % 2 == 0 else psD
                t.matmul(dst[:, 0:w], lhsT=onesrow_bf[0:1, :],
                         rhs=lnrows_bf[0:1, (2 * i) * Tn + hf * 512:(2 * i) * Tn + hf * 512 + w],
                         start=True, stop=False)
                s_bc.bump(t.matmul(dst[:, 0:w], lhsT=onesrow_bf[0:1, :],
                                   rhs=lnrows_bf[0:1, (2 * i + 1) * Tn + hf * 512:(2 * i + 1) * Tn + hf * 512 + w],
                                   start=False, stop=True))

    @blk.vector
    def _(v):
        for i, dstT in enumerate([negMb, Rb]):
            for hf in range(nhalf):
                w = min(512, Tn - hf * 512)
                src = psC if hf % 2 == 0 else psD
                s_bc.wait(v, i * nhalf + hf + 1)
                s_bv.bump(v.tensor_copy(out=dstT[:, hf * 512:hf * 512 + w], in_=src[:, 0:w]))
        v.wait_ge(s_bv.h, s_bv.n)
        s_x.wait(v)
        for c in range(C):
            sl = (c % 2) * Tn
            s_vx.bump(v.tensor_tensor(lnscr[:, sl:sl + Tn], xts[:, c * Tn:(c + 1) * Tn],
                            negMb[:, 0:Tn], ALU.add))
            v.wait_ge(s_vx.h, s_vx.n)
            s_xn.bump(v.tensor_tensor(xnbf[:, c * Tn:(c + 1) * Tn],
                                      lnscr[:, sl:sl + Tn], Rb[:, 0:Tn], ALU.mult))
            v.wait_ge(s_xn.h, s_xn.n)
    return {"s_bv": s_bv}


def emit_bias_load_transpose(blk, kb, bias_aps, btmp, bT, ident, psM, s_gp, tag=""):
    s_bl = kb.sem("bl" + tag)
    s_bt = kb.sem("bt" + tag)
    s_be = kb.sem("be" + tag)
    nch = [ap.shape[0] // 128 for ap in bias_aps]

    @blk.sync
    def _(sy):
        for i, ap in enumerate(bias_aps):
            n = nch[i]
            s_bl.bump(sy.dma_start(out=btmp[0:n, i * 128:i * 128 + 128],
                                   in_=ap.rearrange("(c f) -> c f", f=128)), 16)

    @blk.tensor
    def _(t):
        s_bl.wait(t)
        s_gp.wait(t)
        for i, ap in enumerate(bias_aps):
            n = nch[i]
            if i > 0:
                s_be.wait(t, i)
            s_bt.bump(t.transpose(psM[:, 0:n], btmp[0:n, i * 128:i * 128 + 128],
                                  ident[0:n, 0:n]))

    @blk.vector
    def _(v):
        col = 0
        for i, ap in enumerate(bias_aps):
            n = nch[i]
            s_bt.wait(v, i + 1)
            s_be.bump(v.tensor_copy(out=bT[:, col:col + n], in_=psM[:, 0:n]))
            col += n

    return s_be



def build_l0pre(stage=100, F=None):
    if F is None:
        kb = KB("l0pre")
        xT = kb.dram("xT", [D, T], F32, "ExternalInput")
        Ws = {n: kb.dram("W" + n, [D, D], F32, "ExternalInput") for n in "qkv"}
        bs = {n: kb.dram("b" + n, [D], F32, "ExternalInput") for n in "qkv"}
        Bs = {n: kb.dram("B" + n, [16, D], F32, "ExternalInput") for n in "qkv"}
        As = {n: kb.dram("A" + n, [16, D], F32, "ExternalInput") for n in "qkv"}
        outs = {n: kb.dram(n + "T", [D, T], BF16, "ExternalOutput") for n in "qkv"}
        xnT = kb.dram("xnT", [D, T], BF16, "ExternalOutput")
    else:
        kb = F["kb"]
        ext = F["ext"]
        xT = ext["xT"]
        Ws = {n: ext["W" + n] for n in "qkv"}
        bs = {n: ext["b" + n] for n in "qkv"}
        Bs = {n: ext["B" + n] for n in "qkv"}
        As = {n: ext["A" + n] for n in "qkv"}
        outs = {n: ext[n + "T"] for n in "qkv"}
        xnT = ext["xnT"]
    nc = kb.nc
    dbgkind = "ExternalOutput" if F is None else "Internal"
    dbg = kb.dram("dbg", [1, 4 * T], F32, dbgkind)
    dbg2 = kb.dram("dbg2", [128, 2 * T], F32, dbgkind)
    dbg3 = kb.dram("dbg3", [16, 3 * T], BF16, dbgkind)
    dbg4 = kb.dram("dbg4", [128, 48], F32, dbgkind)
    dbg5 = kb.dram("dbg5", [16, 3 * D], BF16, dbgkind)
    Wbf = {n: kb.dram("W" + n + "_bf", [D, D], BF16) for n in "qkv"}

    xts = kb.sb("xts", [128, C * T], F32)
    xnbf = kb.sb("xnbf", [128, C * T], BF16)
    xsq = kb.sb("xsq", [128, 2 * T], BF16)
    lnrows = kb.sb("lnrows", [1, 6 * T], F32)
    negMb = kb.sb("negMb", [128, T], F32)
    Rb = kb.sb("Rb", [128, T], F32)
    lnscr = kb.sb("lnscr", [128, 2 * T], F32)
    ones = kb.sb("ones", [128, 1], BF16)
    onesrow = kb.sb("onesrow", [1, 128], BF16)
    epsb = kb.sb("epsb", [1, 1], F32)
    ident = kb.sb("ident", [128, 128], F32)
    btmp = kb.sb("btmp", [16, 3 * 128], F32)
    bT = kb.sb("bT", [128, 3 * 16], F32)
    Ahat = kb.sb("Ahat", [16, 3 * D], BF16)
    btile = kb.sb("btile", [128, 3 * C * 16], BF16, is_transpose_dest=True)
    Bsb16 = kb.sb("Bsb16", [16, 3 * D], BF16)
    ident_bf = kb.sb("ident_bf", [128, 128], BF16)
    ubf = kb.sb("ubf", [16, 3 * T], BF16)
    wbuf = kb.sb("wbuf", [128, C * 512], BF16, is_transpose_dest=True)
    ost = kb.sb("ost", [128, 2 * 512], BF16)

    psA = kb.ps("psA", [128, 1024])
    psB = kb.ps("psB", [128, 1024])
    psC = kb.ps("psC", [128, 512])
    psD = kb.ps("psD", [128, 512])
    psU = kb.ps("psU", [16, 512])
    psM = kb.ps("psM", [128, 128])

    s_x = kb.sem("s_x")
    s_gp = kb.sem("s_gp")
    s_castW = {n: kb.sem("s_castW" + n) for n in "qkv"}
    s_castB = {n: kb.sem("s_castB" + n) for n in "qkv"}
    s_xn = kb.sem("s_xn")
    s_ae = kb.sem("s_ae")
    s_bt = kb.sem("s_btl")
    s_btp = kb.sem("s_btp")
    s_ul = kb.sem("s_ul")
    s_ue = kb.sem("s_ue")
    s_wl = kb.sem("s_wl")
    s_wl2 = kb.sem("s_wl2")
    s_mm = kb.sem("s_mm")
    s_ev = kb.sem("s_ev")
    s_outp = [kb.sem("s_out0"), kb.sem("s_out1")]
    s_out = kb.sem("s_out")
    if F is not None:
        F["sems"]["l0pre_outp"] = s_outp
        F["sems"]["l0pre_xn"] = s_out

    blk = kb.block() if F is None else F["blk"]

    @blk.gpsimd
    def _(g):
        s_gp.bump(g.memset(ones[:, :], 1.0))
        s_gp.bump(g.memset(onesrow[:, :], 1.0))
        s_gp.bump(g.memset(epsb[:, :], LN_EPS))
        emit_identity(g, ident, s_gp)
        for n in "qkv":
            s_castW[n].bump(g.dma_start(out=Wbf[n], in_=Ws[n]), 16)
        for i, n in enumerate("qkv"):
            s_castB[n].bump(g.dma_start(out=Bsb16[0:16, i * D:(i + 1) * D],
                                        in_=Bs[n]), 16)
        for i, n in enumerate("qkv"):
            s_ae.bump(g.dma_start(out=Ahat[0:16, i * D:(i + 1) * D], in_=As[n]), 16)

    @blk.sync
    def _(sy):
        s_x.bump(sy.dma_start(out=xts[:, :].rearrange("p (c t) -> p c t", t=T),
                              in_=xT.rearrange("(c p) t -> p c t", p=128)), 16)

    if stage >= 2:
        emit_ln(blk, kb, xts, xnbf, T, s_x, s_xn, psA, psB, psC, psD,
                xsq, lnrows, negMb, Rb, lnscr, ones, onesrow, epsb, tag="l0",
                s_gp=s_gp)


    if stage < 25:
        @blk.sync
        def _(sy):
            if stage >= 2:
                s_xn.wait(sy)
                s_out.bump(sy.dma_start(out=xnT.rearrange("(c p) t -> p c t", p=128),
                                        in_=xnbf[:, :].rearrange("p (c t) -> p c t", t=T)), 16)
                s_out.bump(sy.dma_start(out=dbg, in_=lnrows[0:1, 0:4*T]), 16)
                s_out.bump(sy.dma_start(out=dbg2[:, 0:T], in_=negMb[:, :]), 16)
                s_out.bump(sy.dma_start(out=dbg2[:, T:2*T], in_=Rb[:, :]), 16)
            s_castW["q"].wait(sy)
            s_castW["k"].wait(sy)
            s_castW["v"].wait(sy)
            s_castB["q"].wait(sy)
            s_castB["k"].wait(sy)
            s_castB["v"].wait(sy)
            sy.wait_ge(s_out.h, s_out.n)
        kb.ctx.close()
        return kb.nc

    if stage >= 25:
        s_be = emit_bias_load_transpose(blk, kb, [bs[n] for n in "qkv"],
                                        btmp, bT, ident, psM, s_gp, tag="l0")
    if stage == 25:
        @blk.sync
        def _(sy):
            s_be.wait(sy)
            s_out.bump(sy.dma_start(out=dbg4, in_=bT[:, :]), 16)
            sy.wait_ge(s_out.h, s_out.n)
        kb.ctx.close()
        return kb.nc

    if stage == 26:
        @blk.sync
        def _(sy):
            s_ae.wait(sy)
            s_out.bump(sy.dma_start(out=dbg5, in_=Ahat[:, :]), 16)
            sy.wait_ge(s_out.h, s_out.n)
        kb.ctx.close()
        return kb.nc

    @blk.vector
    def _(v):
        s_gp.wait(v)
        s_gp.bump(v.tensor_copy(out=ident_bf[:, :], in_=ident[:, :]))

    @blk.tensor
    def _(t):
        s_gp.wait(t)
        s_be.wait(t)
        for i, n in enumerate("qkv"):
            s_castB[n].wait(t)
            for c in range(C):
                k = i * C + c
                if k >= 1:
                    s_bt.wait(t, k)
                s_btp.bump(t.transpose(
                    psM[:, 0:128].bitcast(BF16)[0:128, 0:16],
                    Bsb16[0:16, i * D + c * 128:i * D + c * 128 + 128],
                    ident_bf[0:16, 0:16]))

    @blk.scalar
    def _(sc):
        for k in range(3 * C):
            s_btp.wait(sc, k + 1)
            s_bt.bump(sc.activation(
                out=btile[:, k * 16:k * 16 + 16],
                in_=psM[:, 0:128].bitcast(BF16)[0:128, 0:16],
                func=AF.Identity))

    if stage == 27:
        @blk.sync
        def _(sy):
            s_bt.wait(sy)
            s_out.bump(sy.dma_start(out=dbg3[0:16, 0:768].bitcast(BF16) if False else dbg3[0:16, 0:768], in_=btile[:, :][0:16, 0:768]), 16)
            sy.wait_ge(s_out.h, s_out.n)
        kb.ctx.close()
        return kb.nc

    @blk.tensor
    def _(t):
        s_xn.wait(t)
        s_bt.wait(t)
        s_be.wait(t)
        k = 0
        for i in range(3):
            for tp in range(2):
                if k >= 1:
                    s_ue.wait(t, k)
                for c in range(C):
                    inst = t.matmul(psU[:, 0:512],
                             lhsT=btile[:, (i * C + c) * 16:(i * C + c) * 16 + 16],
                             rhs=xnbf[:, c * T + tp * 512:c * T + tp * 512 + 512],
                             start=(c == 0), stop=(c == C - 1))
                s_ul.bump(inst)
                k += 1

    @blk.vector
    def _(v):
        k = 0
        for i in range(3):
            for tp in range(2):
                s_ul.wait(v, k + 1)
                s_ue.bump(v.tensor_scalar(ubf[0:16, i * T + tp * 512:i * T + tp * 512 + 512],
                                          psU[:, 0:512], 0.5, None, ALU.mult))
                k += 1

    if stage == 3:
        @blk.sync
        def _(sy):
            s_ue.wait(sy)
            s_be.wait(sy)
            s_ae.wait(sy)
            s_out.bump(sy.dma_start(out=dbg3, in_=ubf[:, :]), 16)
            s_out.bump(sy.dma_start(out=dbg4, in_=bT[:, :]), 16)
            s_out.bump(sy.dma_start(out=dbg5, in_=Ahat[:, :]), 16)
            sy.wait_ge(s_out.h, s_out.n)
        kb.ctx.close()
        return kb.nc

    PORD = "kvq" if F is not None else "qkv"

    @blk.sync
    def _(sy):
        for ip, n in enumerate(PORD):
            s_castW[n].wait(sy)
            for og in range(4):
                gidx = ip * 4 + og
                if gidx >= 1:
                    sy.wait_ge(s_mm.h, gidx * 8)
                for c in range(0, C, 2):
                    s_wl.bump(sy.dma_start(
                        out=wbuf[:, c * 512:c * 512 + 512],
                        in_=Wbf[n][c * 128:(c + 1) * 128, og * 512:og * 512 + 512]), 16)

    def _wload_odd(g, projs, base_ip):
        for ip0, n in enumerate(projs):
            ip = base_ip + ip0
            s_castW[n].wait(g)
            for og in range(4):
                gidx = ip * 4 + og
                if gidx >= 1:
                    g.wait_ge(s_mm.h, gidx * 8)
                for c in range(1, C, 2):
                    s_wl2.bump(g.dma_start(
                        out=wbuf[:, c * 512:c * 512 + 512],
                        in_=Wbf[n][c * 128:(c + 1) * 128, og * 512:og * 512 + 512]), 16)

    @blk.gpsimd
    def _(g):
        _wload_odd(g, PORD[:2], 0)

    if F is not None and "cc_hook" in F:
        F["cc_hook"]()

    @blk.gpsimd
    def _(g):
        _wload_odd(g, PORD[2:], 2)

    @blk.tensor
    def _(t):
        s_ue.wait(t)
        s_ae.wait(t)
        s_be.wait(t)
        k = 0
        for ip, n in enumerate(PORD):
            io = "qkv".index(n)
            for og in range(4):
                gidx = ip * 4 + og
                t.wait_ge(s_wl.h, (gidx + 1) * 8 * 16)
                t.wait_ge(s_wl2.h, (gidx + 1) * 8 * 16)
                wb = wbuf
                for oc4 in range(4):
                    oc = og * 4 + oc4
                    for tp in range(2):
                        ps = psC if k % 2 == 0 else psD
                        if k >= 2:
                            s_ev.wait(t, k - 1)
                        for c in range(C):
                            t.matmul(ps[:, 0:512],
                                     lhsT=wb[:, c * 512 + oc4 * 128:c * 512 + oc4 * 128 + 128],
                                     rhs=xnbf[:, c * T + tp * 512:c * T + tp * 512 + 512],
                                     start=(c == 0), stop=False)
                        s_mm.bump(t.matmul(ps[:, 0:512],
                                 lhsT=Ahat[:, io * D + oc * 128:io * D + oc * 128 + 128],
                                 rhs=ubf[0:16, io * T + tp * 512:io * T + tp * 512 + 512],
                                 start=False, stop=True))
                        k += 1

    @blk.scalar
    def _(sc):
        k = 0
        for ip, n in enumerate(PORD):
            io = "qkv".index(n)
            for og in range(4):
                for oc4 in range(4):
                    oc = og * 4 + oc4
                    for tp in range(2):
                        ps = psC if k % 2 == 0 else psD
                        s_mm.wait(sc, k + 1)
                        if k >= 2:
                            sc.wait_ge(s_outp[k % 2].h, (k // 2) * 16)
                        s_ev.bump(sc.activation(
                            out=ost[:, (k % 2) * 512:(k % 2) * 512 + 512],
                            in_=ps[:, 0:512], func=AF.Identity,
                            bias=bT[:, io * 16 + oc:io * 16 + oc + 1]))
                        sc.wait_ge(s_ev.h, k + 1)
                        s_outp[k % 2].bump(sc.dma_start(
                            out=outs[n][oc * 128:(oc + 1) * 128, tp * 512:tp * 512 + 512],
                            in_=ost[:, (k % 2) * 512:(k % 2) * 512 + 512]), 16)
                        k += 1

    @blk.sync
    def _(sy):
        s_xn.wait(sy)
        s_out.bump(sy.dma_start(out=xnT.rearrange("(c p) t -> p c t", p=128),
                                in_=xnbf[:, :].rearrange("p (c t) -> p c t", t=T)), 16)
        sy.wait_ge(s_outp[0].h, s_outp[0].n)
        sy.wait_ge(s_outp[1].h, s_outp[1].n)
        sy.wait_ge(s_out.h, s_out.n)

    if F is not None:
        F["sems"]["l0pre_outp"] = s_outp
        F["sems"]["l0pre_xn"] = s_out
        return None
    kb.ctx.close()
    return kb.nc



def build_attn0(F=None):
    if F is None:
        kb = KB("attn0")
        qT = kb.dram("qT", [D, T], BF16, "ExternalInput")
        kTf = kb.dram("kTf", [D, S], BF16, "ExternalInput")
        vTf = kb.dram("vTf", [D, S], BF16, "ExternalInput")
        sT = kb.dram("sT", [D, T], BF16, "ExternalOutput")

        def kb_ap(h, r):
            return kTf[h * 128:(h + 1) * 128, r * 1024:(r + 1) * 1024]

        def vt_ap(h, kt):
            return vTf[h * 128:(h + 1) * 128, kt * 128:(kt + 1) * 128]
    else:
        kb = F["kb"]
        ext = F["ext"]
        qT = ext["qT"]
        kTg = ext["kTg"]
        vTg = ext["vTg"]
        sT = ext["sT"]

        def kb_ap(h, r):
            return kTg[r * D + h * 128:r * D + h * 128 + 128, 0:1024]

        def vt_ap(h, kt):
            r, j = kt // 8, kt % 8
            return vTg[r * D + h * 128:r * D + h * 128 + 128, j * 128:(j + 1) * 128]
    nc = kb.nc

    qsb = kb.sb("qsb", [128, H * T], BF16)
    kbuf = kb.sb("kbuf", [128, 3 * 1024], BF16)
    vbuf = kb.sb("vbuf", [128, 4 * 128], BF16, is_transpose_dest=True)
    Pt = kb.sb("Pt", [128, 2 * 1024], BF16)
    rows = kb.sb("rows", [1, 2 * 1024], F32)
    rbrows = kb.sb("rbrows", [1, 2 * 1024], BF16)
    rb = kb.sb("rb", [128, 1024], F32)
    stbuf = kb.sb("stbuf", [128, 2 * 1024], BF16)
    ones_bf = kb.sb("ones_bf", [128, 1], BF16)
    onesrow_bf = kb.sb("onesrow_bf", [1, 128], BF16)
    mbias = kb.sb("mbias", [128, 1], F32)

    psS = [kb.ps("psS0", [128, 1024]), kb.ps("psS1", [128, 1024])]
    psO = kb.ps("psO", [128, 1024])
    psD = kb.ps("psD", [128, 1024])

    s_q = kb.sem("s_q")
    s_gp = kb.sem("s_gp")
    s_kb = [kb.sem(f"s_kb{i}") for i in range(3)]
    s_vt = [kb.sem(f"s_vt{i}") for i in range(4)]
    s_sT = kb.sem("s_sT")
    s_exp = kb.sem("s_exp")
    s_pav = kb.sem("s_pav")
    s_dnr = kb.sem("s_dnr")
    s_bc = kb.sem("s_bc")
    s_rbe = kb.sem("s_rbe")
    s_dne = kb.sem("s_dne")
    s_se = kb.sem("s_se")
    s_vx = kb.sem("s_vx")
    s_outp = [kb.sem("s_out0"), kb.sem("s_out1")]

    blk = kb.block() if F is None else F["blk"]

    @blk.gpsimd
    def _(g):
        s_gp.bump(g.memset(ones_bf[:, :], 1.0))
        s_gp.bump(g.memset(onesrow_bf[:, :], 1.0))
        s_gp.bump(g.memset(mbias[:, :], MB))

    @blk.sync
    def _(sy):
        if F is not None:
            F["sems"]["l0pre_outp"][0].wait(sy)
            F["sems"]["l0pre_outp"][1].wait(sy)
        s_q.bump(sy.dma_start(out=qsb[:, :].rearrange("p (h t) -> p h t", t=T),
                              in_=qT.rearrange("(h p) t -> p h t", p=128)), 16)
        if F is not None:
            F["sems"]["cc_kv"].wait(sy)
        for h in range(H):
            for kt in range(64):
                if kt % 8 == 0:
                    r = kt // 8
                    i = h * 8 + r
                    if i >= 3:
                        rel = i - 3
                        sy.wait_ge(s_exp.h, (rel // 8) * 64 + ((rel % 8) + 1) * 8)
                    s_kb[i % 3].bump(sy.dma_start(
                        out=kbuf[:, (i % 3) * 1024:(i % 3) * 1024 + 1024],
                        in_=kb_ap(h, r)), 16)
                i = h * 64 + kt
                if i >= 4:
                    sy.wait_ge(s_pav.h, i - 3)
                s_vt[i % 4].bump(sy.dma_start(
                    out=vbuf[:, (i % 4) * 128:(i % 4) * 128 + 128],
                    in_=vt_ap(h, kt),
                    transpose=True), 16)

    @blk.tensor
    def _(t):
        s_q.wait(t)
        s_gp.wait(t)
        for h in range(H):
            for kt in range(65):
                g = h * 64 + kt
                if kt < 64:
                    r = kt // 8
                    if g >= 2:
                        t.wait_ge(s_exp.h, g - 1)
                    i_blk = h * 8 + r
                    t.wait_ge(s_kb[i_blk % 3].h, (i_blk // 3 + 1) * 16)
                    ko = ((h * 8 + r) % 3) * 1024 + (kt % 8) * 128
                    t.matmul(psS[g % 2][:, 0:512], lhsT=kbuf[:, ko:ko + 128],
                             rhs=qsb[:, h * T:h * T + 512], start=True, stop=True)
                    s_sT.bump(t.matmul(psS[g % 2][:, 512:1024], lhsT=kbuf[:, ko:ko + 128],
                             rhs=qsb[:, h * T + 512:h * T + 1024], start=True, stop=True))
                if kt >= 1:
                    ga = g - 1
                    kta = kt - 1
                    t.wait_ge(s_exp.h, ga + 1)
                    t.wait_ge(s_vt[ga % 4].h, (ga // 4 + 1) * 16)
                    if kta == 0 and h > 0:
                        t.wait_ge(s_se.h, 2 * h)
                        t.wait_ge(s_rbe.h, h)
                    st = (kta == 0)
                    sp = (kta == 63)
                    vsl = (ga % 4) * 128
                    psl = (ga % 2) * 1024
                    t.matmul(psO[:, 0:512], lhsT=vbuf[:, vsl:vsl + 128],
                             rhs=Pt[:, psl:psl + 512], start=st, stop=sp)
                    t.matmul(psO[:, 512:1024], lhsT=vbuf[:, vsl:vsl + 128],
                             rhs=Pt[:, psl + 512:psl + 1024], start=st, stop=sp)
                    t.matmul(psD[0:1, 0:512], lhsT=ones_bf[:, :],
                             rhs=Pt[:, psl:psl + 512], start=st, stop=sp)
                    s_pav.bump(t.matmul(psD[0:1, 512:1024], lhsT=ones_bf[:, :],
                             rhs=Pt[:, psl + 512:psl + 1024], start=st, stop=sp))
            t.wait_ge(s_dnr.h, h + 1)
            for hf in range(2):
                t.matmul(psD[:, hf * 512:hf * 512 + 512], lhsT=onesrow_bf[0:1, :],
                         rhs=rbrows[0:1, hf * 512:hf * 512 + 512], start=True, stop=False)
                s_bc.bump(t.matmul(psD[:, hf * 512:hf * 512 + 512], lhsT=onesrow_bf[0:1, :],
                         rhs=rbrows[0:1, 1024 + hf * 512:1024 + hf * 512 + 512],
                         start=False, stop=True))

    @blk.scalar
    def _(sc):
        for h in range(H):
            for kt in range(64):
                g = h * 64 + kt
                sc.wait_ge(s_sT.h, g + 1)
                if g >= 2:
                    sc.wait_ge(s_pav.h, g - 1)
                psl = (g % 2) * 1024
                s_exp.bump(sc.activation(out=Pt[:, psl:psl + 1024],
                                         in_=psS[g % 2][:, :], func=AF.Exp,
                                         bias=mbias[:, :], scale=ATT_SCALE))
            if h >= 1:
                sc.wait_ge(s_se.h, 2 * h)
                s_outp[(h - 1) % 2].bump(sc.dma_start(
                    out=sT[(h - 1) * 128:h * 128, :],
                    in_=stbuf[:, ((h - 1) % 2) * 1024:((h - 1) % 2) * 1024 + 1024]), 16)

    @blk.vector
    def _(v):
        for h in range(H):
            v.wait_ge(s_pav.h, (h + 1) * 64)
            s_dne.bump(v.tensor_copy(out=rows[0:1, 0:1024], in_=psD[0:1, :]))
            v.wait_ge(s_dne.h, h + 1)
            s_vx.bump(v.reciprocal(rows[0:1, 1024:2048], rows[0:1, 0:1024]))
            v.wait_ge(s_vx.h, s_vx.n)
            s_vx.bump(v.tensor_copy(out=rbrows[0:1, 0:1024], in_=rows[0:1, 1024:2048]))
            v.wait_ge(s_vx.h, s_vx.n)
            s_dnr.bump(v.tensor_tensor(rbrows[0:1, 1024:2048], rows[0:1, 1024:2048],
                                       rbrows[0:1, 0:1024], ALU.subtract))
            v.wait_ge(s_bc.h, (h + 1) * 2)
            s_vx.bump(v.tensor_copy(out=rb[:, 0:512], in_=psD[:, 0:512]))
            s_rbe.bump(v.tensor_copy(out=rb[:, 512:1024], in_=psD[:, 512:1024]))
            v.wait_ge(s_rbe.h, h + 1)
            if h >= 2:
                v.wait_ge(s_outp[h % 2].h, (h // 2) * 16)
            ssl = (h % 2) * 1024
            s_se.bump(v.tensor_tensor(stbuf[:, ssl:ssl + 512], psO[:, 0:512],
                                      rb[:, 0:512], ALU.mult))
            s_se.bump(v.tensor_tensor(stbuf[:, ssl + 512:ssl + 1024], psO[:, 512:1024],
                                      rb[:, 512:1024], ALU.mult))

    @blk.scalar
    def _(sc):
        sc.wait_ge(s_se.h, 2 * H)
        s_outp[(H - 1) % 2].bump(sc.dma_start(
            out=sT[(H - 1) * 128:H * 128, :],
            in_=stbuf[:, ((H - 1) % 2) * 1024:((H - 1) % 2) * 1024 + 1024]), 16)
        sc.wait_ge(s_outp[0].h, s_outp[0].n)
        sc.wait_ge(s_outp[1].h, s_outp[1].n)

    if F is not None:
        F["sems"]["attn0_outp"] = s_outp
        return None
    kb.ctx.close()
    return kb.nc





def build_l0post(F=None):
    if F is None:
        kb = KB("l0post")
        xT = kb.dram("xT", [D, T], F32, "ExternalInput")
        xnT = kb.dram("xnT", [D, T], BF16, "ExternalInput")
        sT = kb.dram("sT", [D, T], BF16, "ExternalInput")
        Wo = kb.dram("Wo", [D, D], F32, "ExternalInput")
        bo = kb.dram("bo", [D], F32, "ExternalInput")
        Bo = kb.dram("Bo", [16, D], F32, "ExternalInput")
        Ao = kb.dram("Ao", [16, D], F32, "ExternalInput")
        gate_w = kb.dram("gate_w", [D, INTER], F32, "ExternalInput")
        up_w = kb.dram("up_w", [D, INTER], F32, "ExternalInput")
        down_w = kb.dram("down_w", [INTER, D], F32, "ExternalInput")
        gate_s = kb.dram("gate_s", [INTER], F32, "ExternalInput")
        up_s = kb.dram("up_s", [INTER], F32, "ExternalInput")
        down_s = kb.dram("down_s", [D], F32, "ExternalInput")
        x1T = kb.dram("x1T", [D, T], F32, "ExternalOutput")
    else:
        kb = F["kb"]
        ext = F["ext"]
        xT = ext["xT"]
        xnT = ext["xnT"]
        sT = ext["sT"]
        Wo, bo, Bo, Ao = ext["Wo"], ext["bo"], ext["Bo"], ext["Ao"]
        gate_w, up_w, down_w = ext["gate_w"], ext["up_w"], ext["down_w"]
        gate_s, up_s, down_s = ext["gate_s"], ext["up_s"], ext["down_s"]
        x1T = ext["x1T"]
    nc = kb.nc
    if F is None:
        Wo_bf = kb.dram("Wo_bf", [D, D], BF16)
        Bo_bf = None
        gate_bf = kb.dram("gate_bf", [D, INTER], BF16)
        up_bf = kb.dram("up_bf", [D, INTER], BF16)
        down_bf = kb.dram("down_bf", [INTER, D], BF16)
    else:
        Wo_bf, Bo_bf = ext["Wo_bf"], None
        gate_bf, up_bf, down_bf = ext["gate_bf"], ext["up_bf"], ext["down_bf"]

    xnbf = kb.sb("xnbf", [128, C * T], BF16)
    ssb = kb.sb("ssb", [128, C * T], BF16)
    htb = kb.sb("htb", [128, IC * 512], BF16)
    wdbuf = kb.sb("wdbuf", [128, IC * 512], BF16, is_transpose_dest=True)
    wobuf = kb.sb("wobuf", [128, C * 512], BF16, is_transpose_dest=True)
    gs_scr = kb.sb("gs_scr", [128, 4 * 512], F32)
    gscr = kb.sb("gscr", [128, 512], F32)
    oscr = kb.sb("oscr", [128, 512], F32)
    x1sb = kb.sb("x1sb", [128, 2 * 512], F32)
    xstage = kb.sb("xstage", [128, 4 * 512], F32)
    ident = kb.sb("ident", [128, 128], F32)
    btmp = kb.sb("btmp", [44, 4 * 128], F32)
    bT = kb.sb("bT", [128, 120], F32)
    Aohat = kb.sb("Aohat", [16, D], BF16)
    botile = kb.sb("botile", [128, C * 16], BF16, is_transpose_dest=True)
    Bosb16 = kb.sb("Bosb16", [16, D], BF16)
    ident_bf = kb.sb("ident_bf", [128, 128], BF16)
    uo_bf = kb.sb("uo_bf", [16, T], BF16)

    psF = [kb.ps("psF%d" % j, [128, 512]) for j in range(4)]
    psX = kb.ps("psX", [128, 512])
    psY = kb.ps("psY", [128, 512])
    psU = kb.ps("psU", [16, 512])
    psM = kb.ps("psM", [128, 128])

    s_ld = kb.sem("s_ld")
    s_gp = kb.sem("s_gp")
    s_ae = kb.sem("s_ae")
    s_bt = kb.sem("s_btl")
    s_btp = kb.sem("s_btp")
    s_ul = kb.sem("s_ul")
    s_ue = kb.sem("s_ue")
    if F is None:
        cast = {n: kb.sem("cast_" + n) for n in ["Wo", "Bo", "gate", "up", "down"]}
    else:
        cast = {n: F["cast"][n] for n in ["Wo", "Bo", "gate", "up", "down"]}
    s_wf = [kb.sem("s_wf%d" % j) for j in range(4)]
    s_wu = [kb.sem("s_wu%d" % j) for j in range(4)]
    s_wd = [kb.sem("s_wd%d" % j) for j in range(4)]
    s_wdn = [kb.sem("s_wdn%d" % j) for j in range(4)]
    s_gm = kb.sem("s_gm")
    s_um = kb.sem("s_um")
    s_gv = kb.sem("s_gv")
    s_hv = kb.sem("s_hv")
    s_dm = kb.sem("s_dm")
    s_om = kb.sem("s_om")
    s_ge = kb.sem("s_ge")
    s_oe = kb.sem("s_oe")
    s_t1 = kb.sem("s_t1")
    s_x1 = kb.sem("s_x1")
    s_xl = kb.sem("s_xl")
    s_outp = [kb.sem("s_out0"), kb.sem("s_out1")]

    blk = kb.block() if F is None else F["blk"]
    NG = IC // 4

    @blk.gpsimd
    def _(g):
        emit_identity(g, ident, s_gp)
        if F is None:
            cast["Wo"].bump(g.dma_start(out=Wo_bf, in_=Wo), 16)
            cast["gate"].bump(g.dma_start(out=gate_bf, in_=gate_w), 16)
            cast["up"].bump(g.dma_start(out=up_bf, in_=up_w), 16)
            cast["down"].bump(g.dma_start(out=down_bf, in_=down_w), 16)
        cast["Bo"].bump(g.dma_start(out=Bosb16[0:16, :], in_=Bo), 16)
        s_ae.bump(g.dma_start(out=Aohat[0:16, :], in_=Ao), 16)

    @blk.sync
    def _(sy):
        if F is not None:
            F["sems"]["l0pre_xn"].wait(sy)
            F["sems"]["attn0_outp"][0].wait(sy)
            F["sems"]["attn0_outp"][1].wait(sy)
        s_ld.bump(sy.dma_start(out=xnbf[:, :].rearrange("p (c t) -> p c t", t=T),
                               in_=xnT.rearrange("(c p) t -> p c t", p=128)), 16)
        s_ld.bump(sy.dma_start(out=ssb[:, :].rearrange("p (c t) -> p c t", t=T),
                               in_=sT.rearrange("(c p) t -> p c t", p=128)), 16)

    s_be = emit_bias_load_transpose(blk, kb, [bo, down_s, gate_s, up_s],
                                    btmp, bT, ident, psM, s_gp, tag="l0p")

    @blk.vector
    def _(v):
        s_gp.wait(v)
        s_gp.bump(v.tensor_copy(out=ident_bf[:, :], in_=ident[:, :]))

    @blk.tensor
    def _(t):
        s_gp.wait(t)
        s_be.wait(t)
        cast["Bo"].wait(t)
        for c in range(C):
            if c >= 1:
                s_bt.wait(t, c)
            s_btp.bump(t.transpose(
                psM[:, 0:128].bitcast(BF16)[0:128, 0:16],
                Bosb16[0:16, c * 128:c * 128 + 128],
                ident_bf[0:16, 0:16]))

    @blk.scalar
    def _(sc):
        for c in range(C):
            s_btp.wait(sc, c + 1)
            s_bt.bump(sc.activation(
                out=botile[:, c * 16:c * 16 + 16],
                in_=psM[:, 0:128].bitcast(BF16)[0:128, 0:16],
                func=AF.Identity))

    @blk.tensor
    def _(t):
        s_ld.wait(t)
        s_bt.wait(t)
        s_ae.wait(t)
        s_be.wait(t)
        for tp in range(2):
            if tp >= 1:
                s_ue.wait(t, tp)
            for c in range(C):
                inst = t.matmul(psU[:, 0:512], lhsT=botile[:, c * 16:c * 16 + 16],
                                rhs=ssb[:, c * T + tp * 512:c * T + tp * 512 + 512],
                                start=(c == 0), stop=(c == C - 1))
            s_ul.bump(inst)

    @blk.vector
    def _(v):
        for tp in range(2):
            s_ul.wait(v, tp + 1)
            s_ue.bump(v.tensor_scalar(uo_bf[0:16, tp * 512:tp * 512 + 512],
                                      psU[:, 0:512], 0.5, None, ALU.mult))

    @blk.sync
    def _(sy):
        cast["gate"].wait(sy)
        cast["up"].wait(sy)
        cast["down"].wait(sy)
        cast["Wo"].wait(sy)
        for tb in range(2):
            for ig2 in range(2 * NG):
                gi2 = tb * 2 * NG + ig2
                if gi2 >= 2:
                    sy.wait_ge(s_gm.h, 2 * gi2 - 2)
                if tb == 1 and ig2 <= 1:
                    sy.wait_ge(s_dm.h, 16)
                for dc in range(C):
                    s_wf[dc // 4].bump(sy.dma_start(
                        out=wdbuf[:, (gi2 % 2) * 4096 + dc * 256:(gi2 % 2) * 4096 + dc * 256 + 256],
                        in_=gate_bf[dc * 128:(dc + 1) * 128, ig2 * 256:ig2 * 256 + 256]), 16)
            for og in range(4):
                ogi = tb * 4 + og
                if ogi == 0:
                    sy.wait_ge(s_hv.h, IC)
                elif og == 0:
                    sy.wait_ge(s_hv.h, 2 * IC)
                if ogi >= 1:
                    sy.wait_ge(s_dm.h, ogi * 4)
                    sy.wait_ge(s_om.h, ogi * 4)
                for ic in range(1, IC, 2):
                    s_wdn[1].bump(sy.dma_start(
                        out=wdbuf[:, ic * 512:ic * 512 + 512],
                        in_=down_bf[ic * 128:(ic + 1) * 128, og * 512:og * 512 + 512]), 16)
                for dc in range(C):
                    s_wd[dc // 4].bump(sy.dma_start(
                        out=wobuf[:, dc * 512:dc * 512 + 512],
                        in_=Wo_bf[dc * 128:(dc + 1) * 128, og * 512:og * 512 + 512]), 16)
                if ogi >= 1:
                    sy.wait_ge(s_x1.h, ogi * 4)
                for oc4 in range(4):
                    oc = og * 4 + oc4
                    s_xl.bump(sy.dma_start(
                        out=xstage[:, oc4 * 512:oc4 * 512 + 512],
                        in_=xT[oc * 128:oc * 128 + 128, tb * 512:tb * 512 + 512]), 16)

    @blk.gpsimd
    def _(g):
        for tb in range(2):
            for ig2 in range(2 * NG):
                gi2 = tb * 2 * NG + ig2
                if gi2 >= 2:
                    g.wait_ge(s_um.h, 2 * gi2 - 2)
                if tb == 1 and ig2 <= 1:
                    g.wait_ge(s_dm.h, 16)
                for dc in range(C):
                    s_wu[dc // 4].bump(g.dma_start(
                        out=wdbuf[:, 8192 + (gi2 % 2) * 4096 + dc * 256:8192 + (gi2 % 2) * 4096 + dc * 256 + 256],
                        in_=up_bf[dc * 128:(dc + 1) * 128, ig2 * 256:ig2 * 256 + 256]), 16)
            for og in range(4):
                ogi = tb * 4 + og
                if ogi == 0:
                    g.wait_ge(s_hv.h, IC)
                elif og == 0:
                    g.wait_ge(s_hv.h, 2 * IC)
                if ogi >= 1:
                    g.wait_ge(s_dm.h, ogi * 4)
                    g.wait_ge(s_om.h, ogi * 4)
                for ic in range(0, IC, 2):
                    s_wdn[0].bump(g.dma_start(
                        out=wdbuf[:, ic * 512:ic * 512 + 512],
                        in_=down_bf[ic * 128:(ic + 1) * 128, og * 512:og * 512 + 512]), 16)

    @blk.tensor
    def _(t):
        s_ue.wait(t)
        for tb in range(2):
            t0 = tb * 512
            for ig2 in range(2 * NG):
                gi2 = tb * 2 * NG + ig2
                for ic2 in range(2):
                    icg = tb * 44 + ig2 * 2 + ic2
                    bg = (2 * icg) % 4
                    bu = (2 * icg + 1) % 4
                    if icg >= 2:
                        t.wait_ge(s_gv.h, icg - 1)
                    if ic2 == 0:
                        for j in range(4):
                            t.wait_ge(s_wf[j].h, (gi2 + 1) * 64)
                    for dc in range(C):
                        inst = t.matmul(psF[bg][:, :],
                                        lhsT=wdbuf[:, (gi2 % 2) * 4096 + dc * 256 + ic2 * 128:(gi2 % 2) * 4096 + dc * 256 + ic2 * 128 + 128],
                                        rhs=xnbf[:, dc * T + t0:dc * T + t0 + 512],
                                        start=(dc == 0), stop=(dc == C - 1))
                    s_gm.bump(inst)
                    if icg >= 2:
                        t.wait_ge(s_hv.h, icg - 1)
                    if ic2 == 0:
                        for j in range(4):
                            t.wait_ge(s_wu[j].h, (gi2 + 1) * 64)
                    for dc in range(C):
                        inst = t.matmul(psF[bu][:, :],
                                        lhsT=wdbuf[:, 8192 + (gi2 % 2) * 4096 + dc * 256 + ic2 * 128:8192 + (gi2 % 2) * 4096 + dc * 256 + ic2 * 128 + 128],
                                        rhs=xnbf[:, dc * T + t0:dc * T + t0 + 512],
                                        start=(dc == 0), stop=(dc == C - 1))
                    s_um.bump(inst)
            for og in range(4):
                ogi = tb * 4 + og
                for oc4 in range(4):
                    ocg = ogi * 4 + oc4
                    oc = og * 4 + oc4
                    if ocg >= 1:
                        t.wait_ge(s_oe.h, ocg)
                    if oc4 == 0:
                        for j in range(4):
                            t.wait_ge(s_wd[j].h, (ogi + 1) * 64)
                    for dc in range(C):
                        t.matmul(psY[:, :],
                                 lhsT=wobuf[:, dc * 512 + oc4 * 128:dc * 512 + oc4 * 128 + 128],
                                 rhs=ssb[:, dc * T + t0:dc * T + t0 + 512],
                                 start=(dc == 0), stop=False)
                    s_om.bump(t.matmul(psY[:, :],
                             lhsT=Aohat[:, oc * 128:oc * 128 + 128],
                             rhs=uo_bf[0:16, t0:t0 + 512],
                             start=False, stop=True))
                    if ocg >= 1:
                        t.wait_ge(s_ge.h, ocg)
                    if oc4 == 0:
                        t.wait_ge(s_wdn[0].h, (ogi + 1) * 22 * 16)
                        t.wait_ge(s_wdn[1].h, (ogi + 1) * 22 * 16)
                    for ic in range(IC):
                        inst = t.matmul(psX[:, :],
                                        lhsT=wdbuf[:, ic * 512 + oc4 * 128:ic * 512 + oc4 * 128 + 128],
                                        rhs=htb[:, ic * 512:(ic + 1) * 512],
                                        start=(ic == 0), stop=(ic == IC - 1))
                    s_dm.bump(inst)

    @blk.scalar
    def _(sc):
        for tb in range(2):
            for ig in range(NG):
                for ic4 in range(4):
                    icg = (tb * NG + ig) * 4 + ic4
                    ic = ig * 4 + ic4
                    sc.wait_ge(s_gm.h, icg + 1)
                    if icg >= 4:
                        sc.wait_ge(s_hv.h, icg - 3)
                    s_gv.bump(sc.activation(
                        out=gs_scr[:, (icg % 4) * 512:(icg % 4) * 512 + 512],
                        in_=psF[(2 * icg) % 4][:, :], func=AF.Silu,
                        scale=bT[:, 32 + ic:32 + ic + 1]))
            for og in range(4):
                for oc4 in range(4):
                    ocg = (tb * 4 + og) * 4 + oc4
                    oc = og * 4 + oc4
                    sc.wait_ge(s_dm.h, ocg + 1)
                    if ocg >= 1:
                        sc.wait_ge(s_t1.h, ocg)
                    s_ge.bump(sc.activation(out=gscr[:, :], in_=psX[:, :],
                                            func=AF.Silu,
                                            scale=bT[:, 16 + oc:16 + oc + 1]))
                    sc.wait_ge(s_om.h, ocg + 1)
                    s_oe.bump(sc.activation(out=oscr[:, :], in_=psY[:, :],
                                            func=AF.Identity,
                                            bias=bT[:, oc:oc + 1]))
                    sc.wait_ge(s_x1.h, ocg)
                    if ocg >= 1:
                        po = ocg - 1
                        ptb, poc = (po // 16), (po % 16)
                        s_outp[po % 2].bump(sc.dma_start(
                            out=x1T[poc * 128:poc * 128 + 128, ptb * 512:ptb * 512 + 512],
                            in_=x1sb[:, (po % 2) * 512:(po % 2) * 512 + 512]), 16)

    @blk.vector
    def _(v):
        for tb in range(2):
            for ig in range(NG):
                for ic4 in range(4):
                    icg = (tb * NG + ig) * 4 + ic4
                    ic = ig * 4 + ic4
                    v.wait_ge(s_um.h, icg + 1)
                    v.wait_ge(s_gv.h, icg + 1)
                    if tb == 1 and ig == 0 and ic4 == 0:
                        v.wait_ge(s_dm.h, 16)
                    s_hv.bump(v.scalar_tensor_tensor(
                        htb[:, ic * 512:(ic + 1) * 512],
                        psF[(2 * icg + 1) % 4][:, :],
                        bT[:, 76 + ic:76 + ic + 1],
                        gs_scr[:, (icg % 4) * 512:(icg % 4) * 512 + 512],
                        ALU.mult, ALU.mult))
            for og in range(4):
                ogi = tb * 4 + og
                v.wait_ge(s_xl.h, (ogi + 1) * 64)
                for oc4 in range(4):
                    ocg = ogi * 4 + oc4
                    v.wait_ge(s_ge.h, ocg + 1)
                    v.wait_ge(s_oe.h, ocg + 1)
                    if ocg >= 1:
                        v.wait_ge(s_x1.h, ocg)
                    s_t1.bump(v.tensor_tensor(oscr[:, :], oscr[:, :], gscr[:, :],
                                              ALU.mult))
                    v.wait_ge(s_t1.h, s_t1.n)
                    if ocg >= 2:
                        v.wait_ge(s_outp[ocg % 2].h, (ocg // 2) * 16)
                    s_x1.bump(v.tensor_tensor(
                        x1sb[:, (ocg % 2) * 512:(ocg % 2) * 512 + 512],
                        xstage[:, oc4 * 512:oc4 * 512 + 512], oscr[:, :], ALU.add))

    @blk.scalar
    def _(sc):
        sc.wait_ge(s_x1.h, 32)
        po = 31
        s_outp[po % 2].bump(sc.dma_start(
            out=x1T[15 * 128:16 * 128, 512:1024],
            in_=x1sb[:, (po % 2) * 512:(po % 2) * 512 + 512]), 16)
        sc.wait_ge(s_outp[0].h, s_outp[0].n)
        sc.wait_ge(s_outp[1].h, s_outp[1].n)

    if F is not None:
        F["sems"]["l0post_outp"] = s_outp
        return None
    kb.ctx.close()
    return kb.nc





PI2 = 6.283185307179586
PI2_HI = 6.28125
PI2_LO = PI2 - PI2_HI
MAGIC = 12582912.0


def emit_ln_stream(blk, kb, xbf, x_dram, xnbf, Tn, s_xb, s_xn, psA, psB, psC, psD,
                   xsq, lnrows, negMb, Rb, lnscr, ones_bf, onesrow_bf, epsb,
                   tag="", s_gp=None, pre_bcast_waits=(), x_srcs=None, pre_x_waits=()):
    s_sq = kb.sem("lnsq" + tag)
    s_t2 = kb.sem("lnt2" + tag)
    s_vx = kb.sem("lnvx" + tag)
    s_sc = kb.sem("lnsc" + tag)
    s_tt = kb.sem("lntt" + tag)
    s_xg = [kb.sem("lnxg%d" % i + tag) for i in range(2)]
    nhalf = (Tn + 511) // 512

    @blk.scalar
    def _(sc):
        s_xb.wait(sc)
        for c in range(C):
            if c >= 2:
                s_t2.wait(sc, c - 1)
            s_sq.bump(sc.activation(out=xsq[:, (c % 2) * Tn:(c % 2) * Tn + Tn],
                                    in_=xbf[:, c * Tn:(c + 1) * Tn], func=AF.Square))

    @blk.tensor
    def _(t):
        if s_gp is not None:
            s_gp.wait(t)
        s_xb.wait(t)
        for c in range(C):
            for hf in range(nhalf):
                w = min(512, Tn - hf * 512)
                t.matmul(psA[0:1, hf * 512:hf * 512 + w], lhsT=ones_bf[:, :],
                         rhs=xbf[:, c * Tn + hf * 512:c * Tn + hf * 512 + w],
                         start=(c == 0), stop=(c == C - 1))
            s_sq.wait(t, c + 1)
            for hf in range(nhalf):
                w = min(512, Tn - hf * 512)
                inst = t.matmul(psB[0:1, hf * 512:hf * 512 + w], lhsT=ones_bf[:, :],
                                rhs=xsq[:, (c % 2) * Tn + hf * 512:(c % 2) * Tn + hf * 512 + w],
                                start=(c == 0), stop=(c == C - 1))
                if hf == nhalf - 1:
                    s_t2.bump(inst)

    NM, MS, VR, SD = 0, Tn, 2 * Tn, 3 * Tn
    HB = 4 * Tn
    lnrows_bf = lnrows[0:1, HB:HB + 2 * Tn].bitcast(BF16)

    @blk.vector
    def _(v):
        s_t2.wait(v, C)
        s_vx.bump(v.tensor_scalar(lnrows[0:1, NM:NM + Tn], psA[0:1, 0:Tn],
                                  -1.0 / D, None, ALU.mult))
        s_vx.bump(v.tensor_scalar(lnrows[0:1, MS:MS + Tn], psB[0:1, 0:Tn],
                                  1.0 / D, None, ALU.mult))
        v.wait_ge(s_vx.h, s_vx.n)
        s_vx.bump(v.tensor_tensor(lnrows[0:1, VR:VR + Tn], lnrows[0:1, NM:NM + Tn],
                                  lnrows[0:1, NM:NM + Tn], ALU.mult))
        v.wait_ge(s_vx.h, s_vx.n)
        s_vx.bump(v.tensor_tensor(lnrows[0:1, VR:VR + Tn], lnrows[0:1, MS:MS + Tn],
                                  lnrows[0:1, VR:VR + Tn], ALU.subtract))

    @blk.scalar
    def _(sc):
        s_vx.wait(sc)
        s_sc.bump(sc.activation(out=lnrows[0:1, SD:SD + Tn], in_=lnrows[0:1, VR:VR + Tn],
                                func=AF.Sqrt, bias=epsb[0:1, 0:1]))

    @blk.vector
    def _(v):
        s_sc.wait(v)
        s_vx.bump(v.reciprocal(lnrows[0:1, VR:VR + Tn], lnrows[0:1, SD:SD + Tn]))
        for i, src in enumerate([NM, VR]):
            v.wait_ge(s_vx.h, s_vx.n)
            s_vx.bump(v.tensor_copy(out=lnrows_bf[0:1, (2 * i) * Tn:(2 * i) * Tn + Tn],
                                    in_=lnrows[0:1, src:src + Tn]))
            v.wait_ge(s_vx.h, s_vx.n)
            s_vx.bump(v.tensor_tensor(lnrows_bf[0:1, (2 * i + 1) * Tn:(2 * i + 2) * Tn],
                                      lnrows[0:1, src:src + Tn],
                                      lnrows_bf[0:1, (2 * i) * Tn:(2 * i) * Tn + Tn],
                                      ALU.subtract))

    s_bc = kb.sem("lnbc" + tag)
    s_bv = kb.sem("lnbv" + tag)

    @blk.tensor
    def _(t):
        s_vx.wait(t)
        for w_ in pre_bcast_waits:
            w_.wait(t)
        b = 0
        for i in range(2):
            for hf in range(nhalf):
                w = min(512, Tn - hf * 512)
                dst = psC if hf % 2 == 0 else psD
                if b >= 1:
                    s_bv.wait(t, b)
                t.matmul(dst[:, 0:w], lhsT=onesrow_bf[0:1, :],
                         rhs=lnrows_bf[0:1, (2 * i) * Tn + hf * 512:(2 * i) * Tn + hf * 512 + w],
                         start=True, stop=False)
                s_bc.bump(t.matmul(dst[:, 0:w], lhsT=onesrow_bf[0:1, :],
                                   rhs=lnrows_bf[0:1, (2 * i + 1) * Tn + hf * 512:(2 * i + 1) * Tn + hf * 512 + w],
                                   start=False, stop=True))
                b += 1

    nxd = 1 if x_srcs is None else len(x_srcs)

    @blk.sync
    def _(sy):
        for w_ in pre_x_waits:
            w_.wait(sy)
        for c in range(C):
            if c >= 2:
                sy.wait_ge(s_xn.h, c - 1)
            if x_srcs is None:
                s_xg[c % 2].bump(sy.dma_start(
                    out=lnscr[:, (c % 2) * Tn:(c % 2) * Tn + Tn],
                    in_=x_dram[c * 128:(c + 1) * 128, :]), 16)
            else:
                for (col0, wdt, apf) in x_srcs:
                    s_xg[c % 2].bump(sy.dma_start(
                        out=lnscr[:, (c % 2) * Tn + col0:(c % 2) * Tn + col0 + wdt],
                        in_=apf(c)), 16)

    @blk.vector
    def _(v):
        for i, dstT in enumerate([negMb, Rb]):
            for hf in range(nhalf):
                w = min(512, Tn - hf * 512)
                src = psC if hf % 2 == 0 else psD
                s_bc.wait(v, i * nhalf + hf + 1)
                s_bv.bump(v.tensor_copy(out=dstT[:, hf * 512:hf * 512 + w], in_=src[:, 0:w]))
        v.wait_ge(s_bv.h, s_bv.n)
        for c in range(C):
            sl = (c % 2) * Tn
            v.wait_ge(s_xg[c % 2].h, (c // 2 + 1) * 16 * nxd)
            s_tt.bump(v.tensor_tensor(lnscr[:, sl:sl + Tn], lnscr[:, sl:sl + Tn],
                                      negMb[:, 0:Tn], ALU.add))
            v.wait_ge(s_tt.h, s_tt.n)
            s_xn.bump(v.tensor_tensor(xnbf[:, c * Tn:(c + 1) * Tn],
                                      lnscr[:, sl:sl + Tn], Rb[:, 0:Tn], ALU.mult))
            v.wait_ge(s_xn.h, s_xn.n)
    return {"s_bv": s_bv, "s_tt": s_tt}



def build_l1pre(F=None):
    if F is None:
        kb = KB("l1pre")
        x1Th = kb.dram("x1Th", [D, TH], F32, "ExternalInput")
        w_qkv = kb.dram("w_qkv", [D, 3 * D], F32, "ExternalInput")
        meta = kb.dram("meta", [1, 4], F32, "ExternalInput")
        q1T = kb.dram("q1T", [D, T], BF16, "ExternalOutput")
        k1T = kb.dram("k1T", [D, TH], BF16, "ExternalOutput")
        v1T = kb.dram("v1T", [D, TH], BF16, "ExternalOutput")
        xn1T = kb.dram("xn1T", [D, TH], BF16, "ExternalOutput")
        x1own = None
        x1halo = None
    else:
        kb = F["kb"]
        ext = F["ext"]
        x1Th = None
        x1own = ext["x1T"]
        x1halo = ext["x1halo"]
        w_qkv = ext["w_qkv"]
        meta = ext["meta"]
        q1T, k1T, v1T, xn1T = ext["q1T"], ext["k1T"], ext["v1T"], ext["xn1T"]
    nc = kb.nc
    wq_bf = kb.dram("wq_bf", [D, 3 * D], BF16) if F is None else ext["wq_bf"]

    xbf = kb.sb("xbf", [128, C * TH], BF16)
    xnbf = kb.sb("xnbf", [128, C * TH], BF16)
    xsq = kb.sb("xsq", [128, 2 * TH], BF16)
    lnrows = kb.sb("lnrows", [1, 6 * TH], F32)
    negMb = kb.sb("negMb", [128, TH], F32)
    Rb = kb.sb("Rb", [128, TH], F32)
    lnscr = kb.sb("lnscr", [128, 2 * TH], F32)
    ones_bf = kb.sb("ones_bf", [128, 1], BF16)
    onesrow_bf = kb.sb("onesrow_bf", [1, 128], BF16)
    epsb = kb.sb("epsb", [1, 1], F32)
    msb = kb.sb("msb", [1, 4], F32)
    mrow_bf = kb.sb("mrow_bf", [1, 1], BF16)
    t0b = kb.sb("t0b", [128, 1], F32)
    iota_f = kb.sb("iota_f", [128, TH], F32)
    iota_p = kb.sb("iota_p", [128, 1], F32)
    srow = kb.sb("srow", [128, 8], F32)
    scrA = kb.sb("scrA", [128, 4 * TH], F32)
    sraw = kb.sb("sraw", [128, TH], F32)
    craw = kb.sb("craw", [128, TH], F32)
    wbuf = kb.sb("wbuf", [128, C * 512], BF16, is_transpose_dest=True)
    rotst = kb.sb("rotst", [128, TH], BF16)
    vstage = kb.sb("vstage", [128, TH], BF16)

    tg = scrA[:, 0:TH]
    uu = scrA[:, TH:2 * TH]
    rr = scrA[:, 2 * TH:3 * TH]
    qs = scrA[:, 0:TH]
    qsw = scrA[:, TH:2 * TH]
    rs1 = scrA[:, 2 * TH:3 * TH]
    rs2 = scrA[:, 3 * TH:4 * TH]
    Cf = craw
    Sf = sraw

    psA = kb.ps("psA", [128, 1536])
    psB = kb.ps("psB", [128, 1536])
    psC = kb.ps("psC", [128, 512])
    psD = kb.ps("psD", [128, 512])

    s_xb = kb.sem("s_xb")
    s_gp = kb.sem("s_gp")
    s_xn = kb.sem("s_xn")
    s_mt = kb.sem("s_mt")
    s_tb = kb.sem("s_tb")
    s_tab = kb.sem("s_tab")
    s_sin = kb.sem("s_sin")
    s_cast = kb.sem("s_cast") if F is None else F["cast"]["wq"]
    s_wl = kb.sem("s_wl")
    s_mm = kb.sem("s_mm")
    s_ev = kb.sem("s_ev")
    s_rp = kb.sem("s_rp")
    s_outp = [kb.sem("s_outq0"), kb.sem("s_outq1")]
    s_oxn = kb.sem("s_oxn")

    blk = kb.block() if F is None else F["blk"]

    @blk.gpsimd
    def _(g):
        s_gp.bump(g.memset(ones_bf[:, :], 1.0))
        s_gp.bump(g.memset(onesrow_bf[:, :], 1.0))
        s_gp.bump(g.memset(epsb[:, :], LN_EPS))
        s_gp.bump(g.iota(iota_f[:, :], pattern=[[1, TH]], base=0, channel_multiplier=0,
                         allow_small_or_imprecise_dtypes=True))
        s_gp.bump(g.iota(iota_p[:, :], pattern=[[0, 1]], base=0, channel_multiplier=1,
                         allow_small_or_imprecise_dtypes=True))
        if F is None:
            s_cast.bump(g.dma_start(out=wq_bf, in_=w_qkv), 16)
        if F is None:
            s_xb.bump(g.dma_start(out=xbf[:, :].rearrange("p (c t) -> p c t", t=TH),
                                  in_=x1Th.rearrange("(c p) t -> p c t", p=128)), 16)
        else:
            F["sems"]["cc_halo"].wait(g)
            F["sems"]["l0post_outp"][0].wait(g)
            F["sems"]["l0post_outp"][1].wait(g)
            s_xb.bump(g.dma_start(
                out=xbf[:, :].rearrange("p (c t) -> p c t", t=TH)[:, :, 0:HALO],
                in_=x1halo.rearrange("(c p) t -> p c t", p=128)), 16)
            s_xb.bump(g.dma_start(
                out=xbf[:, :].rearrange("p (c t) -> p c t", t=TH)[:, :, HALO:TH],
                in_=x1own.rearrange("(c p) t -> p c t", p=128)), 16)

    @blk.sync
    def _(sy):
        s_mt.bump(sy.dma_start(out=msb[:, :], in_=meta), 16)

    @blk.vector
    def _(v):
        s_mt.wait(v)
        s_tb.bump(v.tensor_copy(out=mrow_bf[:, :], in_=msb[0:1, 0:1]))

    @blk.tensor
    def _(t):
        s_tb.wait(t)
        s_gp.wait(t)
        s_tb.bump(t.matmul(psC[:, 0:1], lhsT=onesrow_bf[0:1, :], rhs=mrow_bf[0:1, 0:1],
                           start=True, stop=True))

    @blk.vector
    def _(v):
        s_tb.wait(v)
        s_tb.bump(v.tensor_copy(out=t0b[:, :], in_=psC[:, 0:1]))
        v.wait_ge(s_tb.h, s_tb.n)
        s_tab.bump(v.tensor_scalar(srow[:, 0:1], iota_p[:, :], 1.0 / 64, -0.4921875,
                                   ALU.mult, ALU.add))
        v.wait_ge(s_tab.h, s_tab.n)
        s_tab.bump(v.tensor_scalar(srow[:, 1:2], srow[:, 0:1], MAGIC, None, ALU.add))
        v.wait_ge(s_tab.h, s_tab.n)
        s_tab.bump(v.tensor_scalar(srow[:, 2:3], srow[:, 1:2], -MAGIC, None, ALU.add))
        v.wait_ge(s_tab.h, s_tab.n)
        s_tab.bump(v.scalar_tensor_tensor(srow[:, 3:4], srow[:, 2:3], -64.0,
                                          iota_p[:, :], ALU.mult, ALU.add))
        v.wait_ge(s_tab.h, s_tab.n)
        s_tab.bump(v.tensor_scalar(srow[:, 4:5], srow[:, 3:4], -0.14391156831212787,
                                   None, ALU.mult))
        v.wait_ge(s_tab.h, s_tab.n)
        s_tab.bump(v.tensor_scalar(srow[:, 5:6], iota_p[:, :], 0.5, -0.25,
                                   ALU.mult, ALU.add))
        v.wait_ge(s_tab.h, s_tab.n)
        s_tab.bump(v.tensor_scalar(srow[:, 5:6], srow[:, 5:6], MAGIC, None, ALU.add))
        v.wait_ge(s_tab.h, s_tab.n)
        s_tab.bump(v.tensor_scalar(srow[:, 5:6], srow[:, 5:6], -MAGIC, None, ALU.add))
        v.wait_ge(s_tab.h, s_tab.n)
        s_tab.bump(v.scalar_tensor_tensor(srow[:, 6:7], srow[:, 5:6], -2.0,
                                          iota_p[:, :], ALU.mult, ALU.add))
        v.wait_ge(s_tab.h, s_tab.n)
        s_tab.bump(v.tensor_scalar(srow[:, 7:8], srow[:, 6:7], -2.0, 1.0,
                                   ALU.mult, ALU.add))

    @blk.scalar
    def _(sc):
        s_tab.wait(sc)
        s_sin.bump(sc.activation(out=srow[:, 4:5], in_=srow[:, 4:5], func=AF.Exp))

    def rope_reduce(v, src, dst, cnt):
        cnt.bump(v.tensor_scalar(rr[:, :], src[:, :], 1.0 / PI2, -0.5,
                                 ALU.mult, ALU.add))
        v.wait_ge(cnt.h, cnt.n)
        cnt.bump(v.tensor_scalar(rr[:, :], rr[:, :], MAGIC, None, ALU.add))
        v.wait_ge(cnt.h, cnt.n)
        cnt.bump(v.tensor_scalar(rr[:, :], rr[:, :], -MAGIC, None, ALU.add))
        v.wait_ge(cnt.h, cnt.n)
        cnt.bump(v.scalar_tensor_tensor(dst[:, :], rr[:, :], -PI2_HI, src[:, :],
                                        ALU.mult, ALU.add))
        v.wait_ge(cnt.h, cnt.n)
        cnt.bump(v.scalar_tensor_tensor(dst[:, :], rr[:, :], -PI2_LO, dst[:, :],
                                        ALU.mult, ALU.add))
        v.wait_ge(cnt.h, cnt.n)
        cnt.bump(v.tensor_scalar(dst[:, :], dst[:, :], -3.141592653589793, None,
                                 ALU.add))
        v.wait_ge(cnt.h, cnt.n)

    @blk.vector
    def _(v):
        s_sin.wait(v)
        s_gp.wait(v)
        s_tab.bump(v.tensor_scalar(tg[:, :], iota_f[:, :], t0b[:, :], None, ALU.add))
        v.wait_ge(s_tab.h, s_tab.n)
        s_tab.bump(v.tensor_scalar(uu[:, :], tg[:, :], srow[:, 4:5], None, ALU.mult))
        v.wait_ge(s_tab.h, s_tab.n)
        rope_reduce(v, uu, sraw, s_tab)
        s_tab.bump(v.tensor_scalar(tg[:, :], uu[:, :], 1.5707963267948966, None,
                                   ALU.add))
        v.wait_ge(s_tab.h, s_tab.n)
        rope_reduce(v, tg, craw, s_tab)

    @blk.scalar
    def _(sc):
        s_tab.wait(sc)
        s_sin.bump(sc.activation(out=sraw[:, :], in_=sraw[:, :], func=AF.Sin))
        sc.wait_ge(s_sin.h, s_sin.n)
        s_sin.bump(sc.activation(out=craw[:, :], in_=craw[:, :], func=AF.Sin))

    @blk.vector
    def _(v):
        s_sin.wait(v)
        s_tab.bump(v.tensor_scalar(Cf[:, :], craw[:, :], -1.0, None, ALU.mult))
        s_tab.bump(v.tensor_scalar(Sf[:, :], sraw[:, :], srow[:, 7:8], None, ALU.mult))
        v.wait_ge(s_tab.h, s_tab.n)
    s_tabdone = s_tab.n
    s_sindone = s_sin.n

    if F is None:
        lns = emit_ln_stream(blk, kb, xbf, x1Th, xnbf, TH, s_xb, s_xn,
                             psA, psB, psC, psD,
                             xsq, lnrows, negMb, Rb, lnscr, ones_bf, onesrow_bf, epsb,
                             tag="l1", s_gp=s_gp, pre_bcast_waits=(s_tb,))
    else:
        x_srcs = [(0, HALO, lambda c: x1halo[c * 128:(c + 1) * 128, :]),
                  (HALO, T, lambda c: x1own[c * 128:(c + 1) * 128, :])]
        lns = emit_ln_stream(blk, kb, xbf, None, xnbf, TH, s_xb, s_xn,
                             psA, psB, psC, psD,
                             xsq, lnrows, negMb, Rb, lnscr, ones_bf, onesrow_bf, epsb,
                             tag="l1", s_gp=s_gp, pre_bcast_waits=(s_tb,),
                             x_srcs=x_srcs,
                             pre_x_waits=(F["sems"]["cc_halo"],
                                          F["sems"]["l0post_outp"][0],
                                          F["sems"]["l0post_outp"][1]))
    s_lnbv = lns["s_bv"]

    passes_kv = [(0, 512), (512, 512), (1024, 256)]
    passes_q = [(256, 512), (768, 512)]

    def oc_passes(oc):
        return passes_q if oc < 16 else passes_kv

    mm_before = [0]
    for oc in range(48):
        mm_before.append(mm_before[-1] + len(oc_passes(oc)))

    @blk.sync
    def _(sy):
        s_cast.wait(sy)
        for wg in range(12):
            if wg >= 1:
                sy.wait_ge(s_mm.h, mm_before[4 * wg])
            for dc in range(C):
                s_wl.bump(sy.dma_start(
                    out=wbuf[:, dc * 512:dc * 512 + 512],
                    in_=wq_bf[dc * 128:(dc + 1) * 128, wg * 512:wg * 512 + 512]), 16)

    @blk.tensor
    def _(t):
        s_xn.wait(t)
        s_lnbv.wait(t)
        kq = 0
        for wg in range(12):
            t.wait_ge(s_wl.h, (wg + 1) * C * 16)
            for oc4 in range(4):
                oc = wg * 4 + oc4
                for (p0, pw) in oc_passes(oc):
                    ps = psC if kq % 2 == 0 else psD
                    if kq >= 2:
                        t.wait_ge(s_ev.h, kq - 1)
                    for dc in range(C):
                        inst = t.matmul(ps[:, 0:pw],
                                        lhsT=wbuf[:, dc * 512 + oc4 * 128:dc * 512 + oc4 * 128 + 128],
                                        rhs=xnbf[:, dc * TH + p0:dc * TH + p0 + pw],
                                        start=(dc == 0), stop=(dc == C - 1))
                    s_mm.bump(inst)
                    kq += 1

    @blk.scalar
    def _(sc):
        sc.wait_ge(s_tab.h, s_tabdone)
        kq = 0
        for wg in range(12):
            for oc4 in range(4):
                oc = wg * 4 + oc4
                sec = oc // 16
                h = oc % 16
                for ip, (p0, pw) in enumerate(oc_passes(oc)):
                    ps = psC if kq % 2 == 0 else psD
                    sc.wait_ge(s_mm.h, kq + 1)
                    if sec < 2:
                        if ip == 0 and oc >= 1:
                            sc.wait_ge(s_rp.h, 4 * oc)
                        s_ev.bump(sc.activation(out=qs[:, p0:p0 + pw], in_=ps[:, 0:pw],
                                                func=AF.Identity))
                    else:
                        if ip == 0 and oc >= 33:
                            po = oc - 1
                            sc.wait_ge(s_outp[po % 2].h, (po // 2 + 1) * 16)
                        s_ev.bump(sc.activation(out=vstage[:, p0:p0 + pw],
                                                in_=ps[:, 0:pw], func=AF.Identity))
                    kq += 1
                if sec < 2:
                    sc.wait_ge(s_rp.h, 4 * (oc + 1))
                    if sec == 0:
                        s_outp[oc % 2].bump(sc.dma_start(
                            out=q1T[h * 128:(h + 1) * 128, :],
                            in_=rotst[:, 256:TH]), 16)
                    else:
                        s_outp[oc % 2].bump(sc.dma_start(
                            out=k1T[h * 128:(h + 1) * 128, :],
                            in_=rotst[:, :]), 16)
                else:
                    sc.wait_ge(s_ev.h, kq)
                    s_outp[oc % 2].bump(sc.dma_start(
                        out=v1T[h * 128:(h + 1) * 128, :],
                        in_=vstage[:, :]), 16)

    @blk.vector
    def _(v):
        mask = []
        for i in range(0, 32, 2):
            mask += [i + 1, i]
        for oc in range(32):
            sec = oc // 16
            lo = 256 if sec == 0 else 0
            v.wait_ge(s_ev.h, mm_before[oc + 1])
            if oc >= 1:
                po = oc - 1
                v.wait_ge(s_outp[po % 2].h, (po // 2 + 1) * 16)
            s_rp.bump(v.stream_shuffle(qsw[:, lo:TH], qs[:, lo:TH], mask))
            v.wait_ge(s_rp.h, s_rp.n)
            s_rp.bump(v.tensor_tensor(rs1[:, lo:TH], qs[:, lo:TH], Cf[:, lo:TH],
                                      ALU.mult))
            v.wait_ge(s_rp.h, s_rp.n)
            s_rp.bump(v.tensor_tensor(rs2[:, lo:TH], qsw[:, lo:TH], Sf[:, lo:TH],
                                      ALU.mult))
            v.wait_ge(s_rp.h, s_rp.n)
            s_rp.bump(v.tensor_tensor(rotst[:, lo:TH], rs1[:, lo:TH], rs2[:, lo:TH],
                                      ALU.add))
            v.wait_ge(s_rp.h, s_rp.n)

    @blk.sync
    def _(sy):
        s_xn.wait(sy)
        s_oxn.bump(sy.dma_start(out=xn1T.rearrange("(c p) t -> p c t", p=128),
                                in_=xnbf[:, :].rearrange("p (c t) -> p c t", t=TH)), 16)
        sy.wait_ge(s_outp[0].h, s_outp[0].n)
        sy.wait_ge(s_outp[1].h, s_outp[1].n)
        sy.wait_ge(s_oxn.h, s_oxn.n)

    if F is not None:
        F["sems"]["l1pre_outp"] = s_outp
        F["sems"]["l1pre_xn"] = s_oxn
        return None
    kb.ctx.close()
    return kb.nc





def build_attn1(F=None):
    if F is None:
        kb = KB("attn1")
        q1T = kb.dram("q1T", [D, T], BF16, "ExternalInput")
        k1T = kb.dram("k1T", [D, TH], BF16, "ExternalInput")
        v1T = kb.dram("v1T", [D, TH], BF16, "ExternalInput")
        xn1f = kb.dram("xn1f", [D, S], BF16, "ExternalInput")
        w_c1 = kb.dram("w_c1", [D, 512], F32, "ExternalInput")
        w_c2 = kb.dram("w_c2", [1, 512], F32, "ExternalInput")
        meta = kb.dram("meta", [1, 4], F32, "ExternalInput")
        s1T = kb.dram("s1T", [D, T], BF16, "ExternalOutput")
        WS = S

        def stats_src(c):
            return xn1f[c * 128:(c + 1) * 128, :]
    else:
        kb = F["kb"]
        ext = F["ext"]
        q1T, k1T, v1T = ext["q1T"], ext["k1T"], ext["v1T"]
        xn1T_st = ext["xn1T"]
        w_c1, w_c2, meta = ext["w_c1"], ext["w_c2"], ext["meta"]
        s1T = ext["s1T"]
        cc_in = ext["cc_stats_in"]
        cc_out = ext["cc_stats_out"]
        WS = T

        def stats_src(c):
            return xn1T_st[c * 128:(c + 1) * 128, HALO:TH]
    nc = kb.nc
    if F is None:
        wc1_bf = kb.dram("wc1_bf", [D, 512], BF16)
        wc2_bf = kb.dram("wc2_bf", [1, 512], BF16)
    else:
        wc1_bf, wc2_bf = ext["wc1_bf"], ext["wc2_bf"]

    q1sb = kb.sb("q1sb", [128, H * T], BF16)
    k1sb = kb.sb("k1sb", [128, H * TH], BF16)
    vbuf = kb.sb("vbuf", [128, 2 * 10 * 128], BF16, is_transpose_dest=True)
    stg = kb.sb("stg", [128, 2 * WS], BF16)
    sqscr = kb.sb("sqscr", [128, WS], BF16)
    fsum = kb.sb("fsum", [128, 20], F32)
    fsb = kb.sb("fsb", [128, 32], BF16)
    xmb = kb.sb("xmb", [128, 16], BF16)
    wcbuf = kb.sb("wcbuf", [128, C * 512], BF16, is_transpose_dest=True)
    hid = kb.sb("hid", [1, 512], BF16)
    hT = kb.sb("hT", [128, 4], BF16)
    w2sb = kb.sb("w2sb", [1, 512], BF16)
    w2T = kb.sb("w2T", [128, 4], BF16)
    rows1 = kb.sb("rows1", [1, 16], F32)
    rows1b = kb.sb("rows1b", [1, 8], BF16)
    fb = kb.sb("fb", [128, 1], F32)
    hvb = kb.sb("hvb", [128, 1], F32)
    grow = kb.sb("grow", [128, 4], F32)
    dprime = kb.sb("dprime", [128, 128], F32)
    zer = kb.sb("zer", [128, 128], F32)
    pieces = kb.sb("pieces", [128, 6 * 128], F32)
    amask = kb.sb("amask", [128, 3 * 384], F32)
    smask = kb.sb("smask", [128, 2 * 384], F32)
    Pt = kb.sb("Pt", [128, 2 * 384], BF16)
    dnr = kb.sb("dnr", [1, 2 * 128], F32)
    dnb = kb.sb("dnb", [1, 2 * 128], BF16)
    rbq = kb.sb("rbq", [128, 128], F32)
    stbuf = kb.sb("stbuf", [128, 2 * T], BF16)
    ones_bf = kb.sb("ones_bf", [128, 1], BF16)
    onesrow_bf = kb.sb("onesrow_bf", [1, 128], BF16)
    mbias = kb.sb("mbias", [128, 1], F32)
    m5b = kb.sb("m5b", [1, 1], F32)
    msb = kb.sb("msb", [1, 4], F32)
    mrow_bf = kb.sb("mrow_bf", [1, 2], BF16)
    ident = kb.sb("ident", [128, 128], BF16)

    psS = [kb.ps("psS0", [128, 384]), kb.ps("psS1", [128, 384])]
    psO2 = kb.ps("psO2", [128, 128])
    psD2 = kb.ps("psD2", [1, 128])
    psB1 = kb.ps("psB1", [128, 512])
    psB2 = kb.ps("psB2", [128, 512])

    s_gp = kb.sem("s_gp")
    s_cast = kb.sem("s_cast") if F is None else F["cast"]["wc"]
    s_mt = kb.sem("s_mt")
    s_qk = kb.sem("s_qk")
    s_st = [kb.sem("s_st0"), kb.sem("s_st1")]
    s_sq = kb.sem("s_sq")
    s_red = kb.sem("s_red")
    s_stv = kb.sem("s_stv")
    s_stm = kb.sem("s_stm")
    s_sts = kb.sem("s_sts")
    s_wc = kb.sem("s_wc")
    s_msk = kb.sem("s_mskd")
    s_sm = kb.sem("s_sm")
    s_ma = kb.sem("s_ma")
    s_exp = kb.sem("s_exp")
    s_pv = kb.sem("s_pv")
    s_dne = kb.sem("s_dne")
    s_row = kb.sem("s_row")
    s_bcq = kb.sem("s_bcq")
    s_rbe = kb.sem("s_rbe")
    s_sm2 = kb.sem("s_sm2")
    s_vt = [kb.sem("s_vt0"), kb.sem("s_vt1")]
    s_o1p = [kb.sem("s_o1p0"), kb.sem("s_o1p1")]
    s_vx = kb.sem("s_vx")

    blk = kb.block() if F is None else F["blk"]
    s_ccs = kb.sem("s_ccs")

    @blk.gpsimd
    def _(g):
        s_gp.bump(g.memset(ones_bf[:, :], 1.0))
        s_gp.bump(g.memset(onesrow_bf[:, :], 1.0))
        s_gp.bump(g.memset(mbias[:, :], MB))
        s_gp.bump(g.memset(m5b[:, :], -5.0))
        s_gp.bump(g.memset(zer[:, :], 0.0))
        s_gp.bump(g.iota(dprime[:, :], pattern=[[1, 128]], base=0, channel_multiplier=-1,
                         allow_small_or_imprecise_dtypes=True))
        emit_identity(g, ident, s_gp)
        if F is None:
            s_cast.bump(g.dma_start(out=wc1_bf, in_=w_c1), 16)
            s_cast.bump(g.dma_start(out=wc2_bf, in_=w_c2), 16)
        if F is not None:
            g.wait_ge(s_red.h, 48)
            s_ccs.bump(g.dma_start(out=cc_in, in_=fsum[:, 0:18]), 16)
            g.wait_ge(s_ccs.h, 16)
            s_ccs.bump(g.collective_compute(
                "AllReduce", ALU.add, replica_groups=[list(range(NC))],
                ins=[cc_in.opt()], outs=[cc_out.opt()]))
            g.wait_ge(s_ccs.h, 17)
            s_ccs.bump(g.dma_start(out=fsum[:, 0:18], in_=cc_out), 16)

    @blk.sync
    def _(sy):
        if F is not None:
            F["sems"]["l1pre_outp"][0].wait(sy)
            F["sems"]["l1pre_outp"][1].wait(sy)
            F["sems"]["l1pre_xn"].wait(sy)
        s_mt.bump(sy.dma_start(out=msb[:, :], in_=meta), 16)
        s_qk.bump(sy.dma_start(out=q1sb[:, :].rearrange("p (h t) -> p h t", t=T),
                               in_=q1T.rearrange("(h p) t -> p h t", p=128)), 16)
        s_qk.bump(sy.dma_start(out=k1sb[:, :].rearrange("p (h t) -> p h t", t=TH),
                               in_=k1T.rearrange("(h p) t -> p h t", p=128)), 16)
        for c in range(C):
            if c >= 2:
                sy.wait_ge(s_red.h, 3 * (c - 2) + 1)
                sy.wait_ge(s_sq.h, c - 1)
            s_st[c % 2].bump(sy.dma_start(
                out=stg[:, (c % 2) * WS:(c % 2) * WS + WS],
                in_=stats_src(c)), 16)

    @blk.scalar
    def _(sc):
        for c in range(C):
            sc.wait_ge(s_st[c % 2].h, (c // 2 + 1) * 16)
            if c >= 1:
                sc.wait_ge(s_red.h, 3 * (c - 1) + 2)
            s_sq.bump(sc.activation(out=sqscr[:, :], in_=stg[:, (c % 2) * WS:(c % 2) * WS + WS],
                                    func=AF.Square))

    @blk.vector
    def _(v):
        for c in range(C):
            v.wait_ge(s_st[c % 2].h, (c // 2 + 1) * 16)
            s_red.bump(v.reduce_sum(fsum[:, c:c + 1], stg[:, (c % 2) * WS:(c % 2) * WS + WS],
                                    axis=AX))
            v.wait_ge(s_sq.h, c + 1)
            s_red.bump(v.reduce_sum(fsum[:, 16:17], sqscr[:, :], axis=AX))
            v.wait_ge(s_red.h, s_red.n)
            if c == 0:
                s_red.bump(v.tensor_copy(out=fsum[:, 17:18], in_=fsum[:, 16:17]))
            else:
                s_red.bump(v.tensor_tensor(fsum[:, 17:18], fsum[:, 17:18],
                                           fsum[:, 16:17], ALU.add))
            v.wait_ge(s_red.h, s_red.n)
        if F is not None:
            v.wait_ge(s_ccs.h, 33)
        s_stv.bump(v.tensor_copy(out=fsb[:, 0:16], in_=fsum[:, 0:16]))
        v.wait_ge(s_stv.h, s_stv.n)
        s_stv.bump(v.tensor_copy(out=fsb[:, 16:17], in_=fsum[:, 17:18]))
        v.wait_ge(s_stv.h, s_stv.n)
        s_stv.bump(v.tensor_scalar(xmb[:, :], fsum[:, 0:16], 1.0 / S, None, ALU.mult))
        v.wait_ge(s_stv.h, s_stv.n)

    @blk.tensor
    def _(t):
        s_stv.wait(t)
        s_gp.wait(t)
        s_stm.bump(t.matmul(psB1[0:1, 0:17], lhsT=ones_bf[:, :], rhs=fsb[:, 0:17],
                            start=True, stop=True))

    @blk.vector
    def _(v):
        s_stm.wait(v)
        s_stv.bump(v.reduce_sum(rows1[0:1, 0:1], psB1[0:1, 0:16], axis=AX))
        v.wait_ge(s_stv.h, s_stv.n)
        s_stv.bump(v.tensor_copy(out=rows1[0:1, 1:2], in_=psB1[0:1, 16:17]))
        v.wait_ge(s_stv.h, s_stv.n)
        s_stv.bump(v.tensor_tensor(rows1[0:1, 2:3], rows1[0:1, 0:1], rows1[0:1, 0:1],
                                   ALU.mult))
        v.wait_ge(s_stv.h, s_stv.n)
        s_stv.bump(v.tensor_scalar(rows1[0:1, 2:3], rows1[0:1, 2:3], -1.0 / (S * D),
                                   None, ALU.mult))
        v.wait_ge(s_stv.h, s_stv.n)
        s_stv.bump(v.tensor_tensor(rows1[0:1, 2:3], rows1[0:1, 1:2], rows1[0:1, 2:3],
                                   ALU.add))
        v.wait_ge(s_stv.h, s_stv.n)
        s_stv.bump(v.tensor_scalar(rows1[0:1, 2:3], rows1[0:1, 2:3],
                                   1.0 / (S * D - 1), None, ALU.mult))
        v.wait_ge(s_stv.h, s_stv.n)

    @blk.sync
    def _(sy):
        s_cast.wait(sy)
        for dc in range(C):
            s_wc.bump(sy.dma_start(out=wcbuf[:, dc * 512:dc * 512 + 512],
                                   in_=wc1_bf[dc * 128:(dc + 1) * 128, 0:512]), 16)
        s_wc.bump(sy.dma_start(out=w2sb[:, :], in_=wc2_bf), 16)

    @blk.tensor
    def _(t):
        s_wc.wait(t)
        s_stv.wait(t)
        for dc in range(C):
            inst = t.matmul(psB2[0:1, 0:512], lhsT=xmb[:, dc:dc + 1],
                            rhs=wcbuf[:, dc * 512:dc * 512 + 512],
                            start=(dc == 0), stop=(dc == C - 1))
        s_stm.bump(inst)

    @blk.scalar
    def _(sc):
        sc.wait_ge(s_stm.h, 2)
        s_sts.bump(sc.activation(out=hid[:, :], in_=psB2[0:1, 0:512], func=AF.Silu))

    stv_base = [0]
    stm_base = [0]

    @blk.tensor
    def _(t):
        s_sts.wait(t)
        stv_base[0] = s_stv.n
        stm_base[0] = s_stm.n
        for j in range(4):
            if j >= 1:
                t.wait_ge(s_stv.h, stv_base[0] + 2 * j)
            dst = psB1 if j % 2 == 0 else psB2
            s_stm.bump(t.transpose(dst[:, 0:1].bitcast(BF16)[:, 0:1], hid[0:1, j * 128:(j + 1) * 128],
                                   ident[0:1, 0:1]))
            dst2 = psB2 if j % 2 == 0 else psB1
            s_stm.bump(t.transpose(dst2[:, 1:2].bitcast(BF16)[:, 0:1], w2sb[0:1, j * 128:(j + 1) * 128],
                                   ident[0:1, 0:1]))

    @blk.vector
    def _(v):
        for j in range(4):
            v.wait_ge(s_stm.h, stm_base[0] + 2 * (j + 1))
            dst = psB1 if j % 2 == 0 else psB2
            dst2 = psB2 if j % 2 == 0 else psB1
            s_stv.bump(v.tensor_copy(out=hT[:, j:j + 1], in_=dst[:, 0:1].bitcast(BF16)[:, 0:1]))
            v.wait_ge(s_stv.h, s_stv.n)
            s_stv.bump(v.tensor_copy(out=w2T[:, j:j + 1], in_=dst2[:, 1:2].bitcast(BF16)[:, 0:1]))
            v.wait_ge(s_stv.h, s_stv.n)

    @blk.tensor
    def _(t):
        s_stv.wait(t)
        for j in range(4):
            inst = t.matmul(psB1[0:1, 64:65], lhsT=hT[:, j:j + 1], rhs=w2T[:, j:j + 1],
                            start=(j == 0), stop=(j == 3))
        s_stm.bump(inst)

    @blk.scalar
    def _(sc):
        s_stm.wait(sc)
        s_gp.wait(sc)
        s_sts.bump(sc.activation(out=rows1[0:1, 4:5], in_=psB1[0:1, 64:65],
                                 func=AF.Sigmoid))
        s_sts.bump(sc.activation(out=rows1[0:1, 3:4], in_=rows1[0:1, 2:3],
                                 func=AF.Sigmoid, scale=10.0, bias=m5b[0:1, 0:1]))

    @blk.vector
    def _(v):
        s_sts.wait(v)
        s_mt.wait(v)
        s_vx.bump(v.tensor_tensor(rows1[0:1, 5:6], rows1[0:1, 3:4], rows1[0:1, 4:5],
                                  ALU.add))
        v.wait_ge(s_vx.h, s_vx.n)
        s_vx.bump(v.tensor_scalar(rows1[0:1, 5:6], rows1[0:1, 5:6], 96.0, 64.0,
                                  ALU.mult, ALU.add))
        v.wait_ge(s_vx.h, s_vx.n)
        s_vx.bump(v.tensor_copy(out=rows1b[0:1, 0:1], in_=rows1[0:1, 5:6]))
        v.wait_ge(s_vx.h, s_vx.n)
        s_vx.bump(v.tensor_tensor(rows1b[0:1, 1:2],
                                  rows1[0:1, 5:6], rows1b[0:1, 0:1], ALU.subtract))
        v.wait_ge(s_vx.h, s_vx.n)
        s_vx.bump(v.tensor_copy(out=mrow_bf[0:1, 0:1], in_=msb[0:1, 1:2]))
        v.wait_ge(s_vx.h, s_vx.n)

    @blk.tensor
    def _(t):
        s_vx.wait(t)
        t.matmul(psB2[:, 0:1], lhsT=onesrow_bf[0:1, :], rhs=rows1b[0:1, 0:1],
                 start=True, stop=False)
        s_stm.bump(t.matmul(psB2[:, 0:1], lhsT=onesrow_bf[0:1, :], rhs=rows1b[0:1, 1:2],
                            start=False, stop=True))
        s_stm.bump(t.matmul(psB2[:, 1:2], lhsT=onesrow_bf[0:1, :], rhs=mrow_bf[0:1, 0:1],
                            start=True, stop=True))

    @blk.gpsimd
    def _(g):
        s_gp.bump(g.memset(pieces[:, 0:128], 0.0))
        g.wait_ge(s_gp.h, s_gp.n)
        s_gp.bump(g.affine_select(out=pieces[:, 0:128], in_=pieces[:, 0:128],
                                  compare_op=ALU.is_ge, fill=-5e8, base=0,
                                  pattern=[[1, 128]], channel_multiplier=-1))

    @blk.vector
    def _(v):
        s_stm.wait(v)
        s_msk.bump(v.tensor_copy(out=fb[:, :], in_=psB2[:, 0:1]))
        s_msk.bump(v.tensor_copy(out=hvb[:, :], in_=psB2[:, 1:2]))
        v.wait_ge(s_msk.h, s_msk.n)
        for j, dd in enumerate([0.0, 128.0, 256.0]):
            s_msk.bump(v.tensor_scalar(grow[:, j:j + 1], fb[:, :], -(1.0 + dd), None,
                                       ALU.add))
        s_msk.bump(v.tensor_scalar(grow[:, 3:4], hvb[:, :], 1.0, 5e8,
                                   ALU.subtract, ALU.mult))
        v.wait_ge(s_msk.h, s_msk.n)
        for j in range(3):
            s_msk.bump(v.tensor_scalar(pieces[:, (1 + j) * 128:(2 + j) * 128],
                                       dprime[:, :], grow[:, j:j + 1], None, ALU.is_le))
        v.wait_ge(s_msk.h, s_msk.n)
        for j in range(3):
            s_msk.bump(v.tensor_scalar(pieces[:, (1 + j) * 128:(2 + j) * 128],
                                       pieces[:, (1 + j) * 128:(2 + j) * 128],
                                       1.0, 5e8, ALU.subtract, ALU.mult))
        v.wait_ge(s_msk.h, s_msk.n)
        for j in [1, 2]:
            s_msk.bump(v.tensor_scalar(pieces[:, (3 + j) * 128:(4 + j) * 128],
                                       pieces[:, (j + 1) * 128:(j + 2) * 128],
                                       grow[:, 3:4], None, ALU.add))
        v.wait_ge(s_msk.h, s_msk.n)
        s_gp.wait(v)
        s_msk.bump(v.tensor_tensor(pieces[:, 128:256], pieces[:, 128:256],
                                   pieces[:, 0:128], ALU.add))
        v.wait_ge(s_msk.h, s_msk.n)
        trip = [[5, 4, 1], [5, 2, 1], [3, 2, 1]]
        for a in range(3):
            for j in range(3):
                s_msk.bump(v.tensor_copy(out=amask[:, a * 384 + j * 128:a * 384 + (j + 1) * 128],
                                         in_=pieces[:, trip[a][j] * 128:(trip[a][j] + 1) * 128]))
        v.wait_ge(s_msk.h, s_msk.n)

    @blk.sync
    def _(sy):
        for h in range(H):
            if h >= 2:
                sy.wait_ge(s_pv.h, (h - 1) * 8)
            for kb in range(10):
                s_vt[h % 2].bump(sy.dma_start(
                    out=vbuf[:, (h % 2) * 1280 + kb * 128:(h % 2) * 1280 + kb * 128 + 128],
                    in_=v1T[h * 128:(h + 1) * 128, kb * 128:(kb + 1) * 128],
                    transpose=True), 16)

    @blk.tensor
    def _(t):
        s_qk.wait(t)
        s_msk.wait(t)
        for g2 in range(130):
            if g2 < 128:
                h, qt = g2 // 8, g2 % 8
                sl = g2 % 2
                if g2 >= 2:
                    t.wait_ge(s_ma.h, g2 - 1)
                for j in range(3):
                    kb = qt + j
                    inst = t.matmul(psS[sl][:, j * 128:(j + 1) * 128],
                                    lhsT=k1sb[:, h * TH + kb * 128:h * TH + kb * 128 + 128],
                                    rhs=q1sb[:, h * T + qt * 128:h * T + qt * 128 + 128],
                                    start=True, stop=True)
                s_sm.bump(inst)
            if g2 >= 2:
                gb = g2 - 2
                t.wait_ge(s_row.h, gb + 1)
                if gb >= 1:
                    t.wait_ge(s_rbe.h, gb)
                t.matmul(psB1[:, 0:128], lhsT=onesrow_bf[0:1, :],
                         rhs=dnb[0:1, 0:128], start=True, stop=False)
                s_bcq.bump(t.matmul(psB1[:, 0:128], lhsT=onesrow_bf[0:1, :],
                                    rhs=dnb[0:1, 128:256], start=False, stop=True))

            if 1 <= g2 <= 128:
                ga = g2 - 1
                h, qt = ga // 8, ga % 8
                t.wait_ge(s_exp.h, ga + 1)
                t.wait_ge(s_vt[h % 2].h, (h // 2 + 1) * 160)
                if ga >= 1:
                    t.wait_ge(s_sm2.h, ga)
                    t.wait_ge(s_dne.h, ga)
                psl = (ga % 2) * 384
                for j in range(3):
                    kb = qt + j
                    t.matmul(psO2[:, :],
                             lhsT=vbuf[:, (h % 2) * 1280 + kb * 128:(h % 2) * 1280 + kb * 128 + 128],
                             rhs=Pt[:, psl + j * 128:psl + (j + 1) * 128],
                             start=(j == 0), stop=(j == 2))
                for j in range(3):
                    inst = t.matmul(psD2[0:1, :], lhsT=ones_bf[:, :],
                                    rhs=Pt[:, psl + j * 128:psl + (j + 1) * 128],
                                    start=(j == 0), stop=(j == 2))
                s_pv.bump(inst)
    @blk.scalar
    def _(sc):
        for g2 in range(128):
            h, qt = g2 // 8, g2 % 8
            if qt == 0 and h >= 1:
                sc.wait_ge(s_sm2.h, 8 * h)
                po = h - 1
                s_o1p[po % 2].bump(sc.dma_start(
                    out=s1T[po * 128:(po + 1) * 128, :],
                    in_=stbuf[:, (po % 2) * T:(po % 2) * T + T]), 16)
            sc.wait_ge(s_ma.h, g2 + 1)
            if g2 >= 2:
                sc.wait_ge(s_pv.h, g2 - 1)
            s_exp.bump(sc.activation(out=Pt[:, (g2 % 2) * 384:(g2 % 2) * 384 + 384],
                                     in_=smask[:, (g2 % 2) * 384:(g2 % 2) * 384 + 384],
                                     func=AF.Exp, bias=mbias[:, :], scale=ATT_SCALE))
        sc.wait_ge(s_sm2.h, 128)
        po = 15
        s_o1p[po % 2].bump(sc.dma_start(
            out=s1T[po * 128:(po + 1) * 128, :],
            in_=stbuf[:, (po % 2) * T:(po % 2) * T + T]), 16)
        if F is not None:
            F["sems"]["attn1_outp"] = s_o1p
        sc.wait_ge(s_o1p[0].h, s_o1p[0].n)
        sc.wait_ge(s_o1p[1].h, s_o1p[1].n)

    @blk.vector
    def _(v):
        for g2 in range(130):
            if g2 < 128:
                h, qt = g2 // 8, g2 % 8
                sl = g2 % 2
                v.wait_ge(s_sm.h, g2 + 1)
                if g2 >= 2:
                    v.wait_ge(s_exp.h, g2 - 1)
                ac = 0 if qt == 0 else (1 if qt == 1 else 2)
                s_ma.bump(v.tensor_tensor(smask[:, sl * 384:sl * 384 + 384],
                                          psS[sl][:, :], amask[:, ac * 384:ac * 384 + 384],
                                          ALU.add))
            if g2 >= 2:
                gb = g2 - 2
                h, qt = gb // 8, gb % 8
                v.wait_ge(s_pv.h, gb + 1)
                v.wait_ge(s_bcq.h, gb + 1)
                s_rbe.bump(v.tensor_copy(out=rbq[:, :], in_=psB1[:, 0:128]))
                v.wait_ge(s_rbe.h, gb + 1)
                if gb >= 16:
                    v.wait_ge(s_o1p[(gb // 8) % 2].h, ((gb // 8) // 2) * 16)
                s_sm2.bump(v.tensor_tensor(
                    stbuf[:, (h % 2) * T + qt * 128:(h % 2) * T + qt * 128 + 128],
                    psO2[:, :], rbq[:, :], ALU.mult))

            if 1 <= g2 <= 128:
                ga = g2 - 1
                v.wait_ge(s_pv.h, ga + 1)
                s_dne.bump(v.tensor_copy(out=dnr[0:1, 0:128], in_=psD2[0:1, :]))
                v.wait_ge(s_dne.h, ga + 1)
                s_vx.bump(v.reciprocal(dnr[0:1, 128:256], dnr[0:1, 0:128]))
                v.wait_ge(s_vx.h, s_vx.n)
                s_vx.bump(v.tensor_copy(out=dnb[0:1, 0:128], in_=dnr[0:1, 128:256]))
                v.wait_ge(s_vx.h, s_vx.n)
                s_row.bump(v.tensor_tensor(dnb[0:1, 128:256], dnr[0:1, 128:256],
                                           dnb[0:1, 0:128], ALU.subtract))
    if F is not None:
        return None
    kb.ctx.close()
    return kb.nc





def build_l1post(F=None):
    if F is None:
        kb = KB("l1post")
        x1T = kb.dram("x1T", [D, T], F32, "ExternalInput")
        xnT = kb.dram("xnT", [D, T], BF16, "ExternalInput")
        s1T = kb.dram("s1T", [D, T], BF16, "ExternalInput")
        w_out = kb.dram("w_out", [D, D], F32, "ExternalInput")
        W_base = kb.dram("W_base", [D, D], F32, "ExternalInput")
        b_base = kb.dram("b_base", [D], F32, "ExternalInput")
        W_router = kb.dram("W_router", [D, 8], F32, "ExternalInput")
        A_w = kb.dram("A_w", [D, 64], F32, "ExternalInput")
        B_w = kb.dram("B_w", [8, D, 8], F32, "ExternalInput")
        x2T = kb.dram("x2T", [D, T], F32, "ExternalOutput")
        xn_src = None
    else:
        kb = F["kb"]
        ext = F["ext"]
        x1T = ext["x1T"]
        xnT = None
        xn_src = ext["xn1T"]
        s1T = ext["s1T"]
        w_out, W_base, b_base = ext["w_out"], ext["W_base"], ext["b_base"]
        W_router, A_w, B_w = ext["W_router"], ext["A_w"], ext["B_w"]
        x2T = ext["x2T"]
    nc = kb.nc
    if F is None:
        wout_bf = kb.dram("wout_bf", [D, D], BF16)
        wbase_bf = kb.dram("wbase_bf", [D, D], BF16)
        A_bf = kb.dram("A_bf", [D, 64], BF16)
        B_bf = kb.dram("B_bf", [8, D, 8], BF16)
    else:
        wout_bf, wbase_bf = ext["wout_bf"], ext["wbase_bf"]
        A_bf, B_bf = ext["A_bf"], ext["B_bf"]
    Bscr = kb.dram("Bscr", [64, D], BF16)

    xnbf = kb.sb("xnbf", [128, C * T], BF16)
    s1sb = kb.sb("s1sb", [128, C * T], BF16)
    wbuf = kb.sb("wbuf", [128, C * 512], BF16, is_transpose_dest=True)
    wobuf = kb.sb("wobuf", [128, C * 512], BF16, is_transpose_dest=True)
    atile = kb.sb("atile", [128, C * 64], BF16, is_transpose_dest=True)
    wrT = kb.sb("wrT", [128, C * 8], BF16)
    Bsb = kb.sb("Bsb", [128, C * 8], BF16)
    best = kb.sb("best", [8, D], BF16)
    bmat = kb.sb("bmat", [64, D], BF16)
    hT = kb.sb("hT", [64, 8 * 128], BF16)
    btmp = kb.sb("btmp", [16, 128], F32)
    bbT = kb.sb("bbT", [128, 16], F32)
    ident = kb.sb("ident", [128, 128], F32)
    ident_bf = kb.sb("ident_bf", [128, 128], BF16)
    ones8 = kb.sb("ones8", [128, 8], F32)
    lgs = kb.sb("lgs", [128, 8], F32)
    grow = kb.sb("grow", [128, 8], F32)
    eqs = kb.sb("eqs", [128, 16], F32)
    wgt = kb.sb("wgt", [128, 8], F32)
    wexp = kb.sb("wexp", [128, 64], F32)
    hscr = kb.sb("hscr", [128, 64], F32)
    hh = kb.sb("hh", [128, 64], BF16)
    gscr = kb.sb("gscr", [128, 512], F32)
    oscr = kb.sb("oscr", [128, 512], F32)
    tscr = kb.sb("tscr", [128, 512], F32)
    x2sb = kb.sb("x2sb", [128, 2 * 512], F32)
    xstage = kb.sb("xstage", [128, 8 * 512], F32)

    psF = [kb.ps("psF%d" % j, [128, 512]) for j in range(2)]
    psZ = kb.ps("psZ", [128, 512])
    psW = kb.ps("psW", [128, 512])
    psM = kb.ps("psM", [128, 128])

    s_ld = kb.sem("s_ld")
    s_gp = kb.sem("s_gp")
    if F is None:
        cast = {n: kb.sem("cast_" + n) for n in ["wout", "wbase", "A", "B"]}
    else:
        cast = {n: F["cast"][n] for n in ["wout", "wbase", "A", "B"]}
    s_wrl = kb.sem("s_wrl")
    s_al = kb.sem("s_al")
    s_bl = kb.sem("s_bl")
    s_tr = kb.sem("s_tr")
    s_te = kb.sem("s_te")
    s_bs = kb.sem("s_bs")
    s_bm = kb.sem("s_bm")
    s_rm = kb.sem("s_rm")
    s_re = kb.sem("s_re")
    s_gt = kb.sem("s_gt")
    s_gs = kb.sem("s_gs")
    s_hm = kb.sem("s_hm")
    s_he = kb.sem("s_he")
    s_hh = kb.sem("s_hh")
    s_ht = kb.sem("s_ht")
    s_hte = kb.sem("s_hte")
    s_wd = kb.sem("s_wd")
    s_zm = kb.sem("s_zm")
    s_om = kb.sem("s_om")
    s_ge = kb.sem("s_ge")
    s_oe = kb.sem("s_oe")
    s_t1 = kb.sem("s_t1")
    s_x2 = kb.sem("s_x2")
    s_xl = kb.sem("s_xl")
    s_outp = [kb.sem("s_out0"), kb.sem("s_out1")]

    blk = kb.block() if F is None else F["blk"]

    @blk.gpsimd
    def _(g):
        emit_identity(g, ident, s_gp)
        s_gp.bump(g.memset(ones8[:, :], 1.0))
        if F is None:
            cast["wout"].bump(g.dma_start(out=wout_bf, in_=w_out), 16)
            cast["wbase"].bump(g.dma_start(out=wbase_bf, in_=W_base), 16)
            cast["A"].bump(g.dma_start(out=A_bf, in_=A_w), 16)
            cast["B"].bump(g.dma_start(out=B_bf, in_=B_w), 16)
        s_wrl.bump(g.dma_start(out=wrT[:, :].rearrange("p (c r) -> p c r", r=8),
                               in_=W_router.rearrange("(c p) r -> p c r", p=128)), 16)
        for e in range(8):
            g.wait_ge(s_te.h, (e + 1) * 16)
            s_bs.bump(g.dma_start(out=Bscr[8 * e:8 * e + 8, :], in_=best[0:8, :]), 16)
            g.wait_ge(s_bs.h, s_bs.n)
        g.wait_ge(s_bs.h, 128)
        s_bm.bump(g.dma_start(out=bmat[0:64, :], in_=Bscr), 16)

    @blk.vector
    def _(v):
        s_gp.wait(v)
        s_gp.bump(v.tensor_copy(out=ident_bf[:, :], in_=ident[:, :]))

    s_be = emit_bias_load_transpose(blk, kb, [b_base], btmp, bbT, ident, psM,
                                    s_gp, tag="l1p")

    @blk.sync
    def _(sy):
        if F is not None:
            F["sems"]["attn1_outp"][0].wait(sy)
            F["sems"]["attn1_outp"][1].wait(sy)
            s_ld.bump(sy.dma_start(
                out=xnbf[:, :].rearrange("p (c t) -> p c t", t=T),
                in_=xn_src.rearrange("(c p) t -> p c t", p=128)[:, :, HALO:TH]), 16)
        else:
            s_ld.bump(sy.dma_start(out=xnbf[:, :].rearrange("p (c t) -> p c t", t=T),
                                   in_=xnT.rearrange("(c p) t -> p c t", p=128)), 16)
        s_ld.bump(sy.dma_start(out=s1sb[:, :].rearrange("p (c t) -> p c t", t=T),
                               in_=s1T.rearrange("(c p) t -> p c t", p=128)), 16)
        cast["A"].wait(sy)
        for c in range(C):
            s_al.bump(sy.dma_start(out=atile[:, c * 64:(c + 1) * 64],
                                   in_=A_bf[c * 128:(c + 1) * 128, 0:64]), 16)
        cast["B"].wait(sy)
        for e in range(8):
            if e >= 1:
                sy.wait_ge(s_tr.h, e * 16)
            s_bl.bump(sy.dma_start(
                out=Bsb[:, :].rearrange("p (c r) -> p c r", r=8),
                in_=B_bf[e].rearrange("(c p) r -> p c r", p=128)), 16)

    @blk.tensor
    def _(t):
        s_gp.wait(t)
        s_be.wait(t)
        for e in range(8):
            t.wait_ge(s_bl.h, (e + 1) * 16)
            for c in range(C):
                k = e * 16 + c
                if k >= 2:
                    t.wait_ge(s_te.h, k - 1)
                dst = psF[0] if k % 2 == 0 else psF[1]
                s_tr.bump(t.transpose(dst[:, 0:128].bitcast(BF16)[0:8, 0:128],
                                      Bsb[:, c * 8:(c + 1) * 8], ident_bf[:, :]))

    @blk.scalar
    def _(sc):
        for e in range(8):
            for c in range(C):
                k = e * 16 + c
                if c == 0 and e >= 1:
                    sc.wait_ge(s_bs.h, e * 16)
                sc.wait_ge(s_tr.h, k + 1)
                src = psF[0] if k % 2 == 0 else psF[1]
                s_te.bump(sc.activation(out=best[0:8, c * 128:(c + 1) * 128],
                                        in_=src[:, 0:128].bitcast(BF16)[0:8, 0:128],
                                        func=AF.Identity, scale=2.0))
                if c == C - 1:
                    sc.wait_ge(s_te.h, s_te.n)

    @blk.tensor
    def _(t):
        s_ld.wait(t)
        s_al.wait(t)
        s_wrl.wait(t)
        s_te.wait(t)
        for tt in range(8):
            if tt >= 1:
                t.wait_ge(s_re.h, tt)
            for c in range(C):
                inst = t.matmul(psF[0][:, 0:8],
                                lhsT=xnbf[:, c * T + tt * 128:c * T + tt * 128 + 128],
                                rhs=wrT[:, c * 8:(c + 1) * 8],
                                start=(c == 0), stop=(c == C - 1))
            s_rm.bump(inst)
            if tt >= 1:
                t.wait_ge(s_he.h, tt)
            for c in range(C):
                inst = t.matmul(psF[1][:, 0:64],
                                lhsT=xnbf[:, c * T + tt * 128:c * T + tt * 128 + 128],
                                rhs=atile[:, c * 64:(c + 1) * 64],
                                start=(c == 0), stop=(c == C - 1))
            s_hm.bump(inst)
            t.wait_ge(s_hh.h, tt + 1)
            if tt >= 1:
                t.wait_ge(s_hte.h, tt)
            s_ht.bump(t.transpose(psM[:, 0:128].bitcast(BF16)[0:64, 0:128],
                                  hh[:, :], ident_bf[:, :]))

    @blk.vector
    def _(v):
        for tt in range(8):
            v.wait_ge(s_rm.h, tt + 1)
            if tt >= 1:
                v.wait_ge(s_gs.h, tt)
            s_re.bump(v.tensor_copy(out=lgs[:, :], in_=psF[0][:, 0:8]))
            v.wait_ge(s_re.h, tt + 1)
            s_gt.bump(v.reduce_max(grow[:, 0:1], lgs[:, :], axis=AX))
            v.wait_ge(s_gt.h, s_gt.n)
            s_gt.bump(v.tensor_scalar(grow[:, 1:2], grow[:, 0:1], -1.0, None, ALU.mult))
            v.wait_ge(s_gt.h, s_gt.n)
            s_gt.bump(v.tensor_scalar(eqs[:, 0:8], lgs[:, :], grow[:, 0:1], None,
                                      ALU.is_equal))
            v.wait_ge(s_gt.h, s_gt.n)
            s_gt.bump(v.scalar_tensor_tensor(eqs[:, 8:16], eqs[:, 0:8], -1e9,
                                             lgs[:, :], ALU.mult, ALU.add))
            v.wait_ge(s_gt.h, s_gt.n)
            s_gt.bump(v.reduce_max(grow[:, 2:3], eqs[:, 8:16], axis=AX))
            v.wait_ge(s_gt.h, s_gt.n)
            s_gt.bump(v.tensor_scalar(eqs[:, 8:16], eqs[:, 8:16], grow[:, 2:3], None,
                                      ALU.is_equal))
            v.wait_ge(s_gt.h, s_gt.n)
            v.wait_ge(s_gs.h, tt + 1)
            s_gt.bump(v.tensor_scalar(grow[:, 4:5], grow[:, 3:4], 1.0, None, ALU.add))
            v.wait_ge(s_gt.h, s_gt.n)
            s_gt.bump(v.reciprocal(grow[:, 5:6], grow[:, 4:5]))
            v.wait_ge(s_gt.h, s_gt.n)
            s_gt.bump(v.tensor_tensor(grow[:, 6:7], grow[:, 3:4], grow[:, 5:6],
                                      ALU.mult))
            v.wait_ge(s_gt.h, s_gt.n)
            s_gt.bump(v.tensor_scalar(eqs[:, 0:8], eqs[:, 0:8], grow[:, 5:6], None,
                                      ALU.mult))
            v.wait_ge(s_gt.h, s_gt.n)
            s_gt.bump(v.scalar_tensor_tensor(wgt[:, :], eqs[:, 8:16], grow[:, 6:7],
                                             eqs[:, 0:8], ALU.mult, ALU.add))
            v.wait_ge(s_gt.h, s_gt.n)
            for e in range(8):
                s_gt.bump(v.tensor_scalar(wexp[:, 8 * e:8 * e + 8], ones8[:, :],
                                          wgt[:, e:e + 1], None, ALU.mult))
            v.wait_ge(s_gt.h, s_gt.n)
            v.wait_ge(s_hm.h, tt + 1)
            s_he.bump(v.tensor_copy(out=hscr[:, :], in_=psF[1][:, 0:64]))
            v.wait_ge(s_he.h, tt + 1)
            if tt >= 1:
                v.wait_ge(s_ht.h, tt)
            s_hh.bump(v.tensor_tensor(hh[:, :], hscr[:, :], wexp[:, :], ALU.mult))
            v.wait_ge(s_hh.h, s_hh.n)

    @blk.scalar
    def _(sc):
        for tt in range(8):
            sc.wait_ge(s_gt.h, tt * 19 + 6)
            s_gs.bump(sc.activation(out=grow[:, 3:4], in_=grow[:, 2:3], func=AF.Exp,
                                    bias=grow[:, 1:2]))
            sc.wait_ge(s_ht.h, tt + 1)
            s_hte.bump(sc.activation(out=hT[0:64, tt * 128:(tt + 1) * 128],
                                     in_=psM[:, 0:128].bitcast(BF16)[0:64, 0:128],
                                     func=AF.Identity))

    @blk.sync
    def _(sy):
        cast["wbase"].wait(sy)
        cast["wout"].wait(sy)
        for og in range(4):
            if og >= 1:
                sy.wait_ge(s_zm.h, og * 8)
                sy.wait_ge(s_om.h, og * 8)
            for dc in range(C):
                s_wd.bump(sy.dma_start(
                    out=wbuf[:, dc * 512:dc * 512 + 512],
                    in_=wbase_bf[dc * 128:(dc + 1) * 128, og * 512:og * 512 + 512]), 16)
            for dc in range(C):
                s_wd.bump(sy.dma_start(
                    out=wobuf[:, dc * 512:dc * 512 + 512],
                    in_=wout_bf[dc * 128:(dc + 1) * 128, og * 512:og * 512 + 512]), 16)
            if og >= 1:
                sy.wait_ge(s_x2.h, og * 8)
            for oc4 in range(4):
                oc = og * 4 + oc4
                for tp in range(2):
                    s_xl.bump(sy.dma_start(
                        out=xstage[:, (oc4 * 2 + tp) * 512:(oc4 * 2 + tp) * 512 + 512],
                        in_=x1T[oc * 128:oc * 128 + 128, tp * 512:tp * 512 + 512]), 16)

    @blk.tensor
    def _(t):
        s_bm.wait(t)
        s_hte.wait(t)
        for og in range(4):
            t.wait_ge(s_wd.h, (og + 1) * 2 * C * 16)
            for oc4 in range(4):
                oc = og * 4 + oc4
                for tp in range(2):
                    k = (og * 4 + oc4) * 2 + tp
                    if k >= 1:
                        t.wait_ge(s_ge.h, k)
                    for dc in range(C):
                        t.matmul(psZ[:, :],
                                 lhsT=wbuf[:, dc * 512 + oc4 * 128:dc * 512 + oc4 * 128 + 128],
                                 rhs=xnbf[:, dc * T + tp * 512:dc * T + tp * 512 + 512],
                                 start=(dc == 0), stop=False)
                    s_zm.bump(t.matmul(psZ[:, :], lhsT=bmat[0:64, oc * 128:oc * 128 + 128],
                                       rhs=hT[0:64, tp * 512:tp * 512 + 512],
                                       start=False, stop=True))
                    if k >= 1:
                        t.wait_ge(s_oe.h, k)
                    for dc in range(C):
                        inst = t.matmul(psW[:, :],
                                        lhsT=wobuf[:, dc * 512 + oc4 * 128:dc * 512 + oc4 * 128 + 128],
                                        rhs=s1sb[:, dc * T + tp * 512:dc * T + tp * 512 + 512],
                                        start=(dc == 0), stop=(dc == C - 1))
                    s_om.bump(inst)

    @blk.scalar
    def _(sc):
        for og in range(4):
            for oc4 in range(4):
                oc = og * 4 + oc4
                for tp in range(2):
                    k = (og * 4 + oc4) * 2 + tp
                    sc.wait_ge(s_zm.h, k + 1)
                    if k >= 1:
                        sc.wait_ge(s_t1.h, k)
                    s_ge.bump(sc.activation(out=gscr[:, :], in_=psZ[:, :],
                                            func=AF.Silu, bias=bbT[:, oc:oc + 1]))
                    sc.wait_ge(s_om.h, k + 1)
                    s_oe.bump(sc.activation(out=oscr[:, :], in_=psW[:, :],
                                            func=AF.Identity))
                    sc.wait_ge(s_x2.h, k)
                    if k >= 1:
                        po = k - 1
                        s_outp[po % 2].bump(sc.dma_start(
                            out=x2T[(po // 2) * 128:(po // 2) * 128 + 128,
                                    (po % 2) * 512:(po % 2) * 512 + 512],
                            in_=x2sb[:, (po % 2) * 512:(po % 2) * 512 + 512]), 16)

    @blk.vector
    def _(v):
        for og in range(4):
            v.wait_ge(s_xl.h, (og + 1) * 128)
            for oc4 in range(4):
                for tp in range(2):
                    k = (og * 4 + oc4) * 2 + tp
                    v.wait_ge(s_ge.h, k + 1)
                    v.wait_ge(s_oe.h, k + 1)
                    if k >= 1:
                        v.wait_ge(s_x2.h, k)
                    s_t1.bump(v.tensor_tensor(tscr[:, :], oscr[:, :], gscr[:, :],
                                              ALU.mult))
                    v.wait_ge(s_t1.h, s_t1.n)
                    if k >= 2:
                        v.wait_ge(s_outp[k % 2].h, (k // 2) * 16)
                    s_x2.bump(v.tensor_tensor(
                        x2sb[:, (k % 2) * 512:(k % 2) * 512 + 512],
                        xstage[:, (oc4 * 2 + tp) * 512:(oc4 * 2 + tp) * 512 + 512],
                        tscr[:, :], ALU.add))

    @blk.scalar
    def _(sc):
        sc.wait_ge(s_x2.h, 32)
        po = 31
        s_outp[po % 2].bump(sc.dma_start(
            out=x2T[(po // 2) * 128:(po // 2) * 128 + 128,
                    (po % 2) * 512:(po % 2) * 512 + 512],
            in_=x2sb[:, (po % 2) * 512:(po % 2) * 512 + 512]), 16)
        sc.wait_ge(s_outp[0].h, s_outp[0].n)
        sc.wait_ge(s_outp[1].h, s_outp[1].n)

    if F is not None:
        F["sems"]["l1post_outp"] = s_outp
        return None
    kb.ctx.close()
    return kb.nc





class NullCtr:

    def wait(self, eng, val=None):
        pass

    def __getitem__(self, i):
        return self


def emit_phase_barrier(blk, kb, tag, clear=()):
    ctr = kb.sem_outer("bar_" + tag)

    def one(eng):
        eng.drain()
        eng.sem_inc(ctr.h, 1)
        eng.wait_ge(ctr.h, 6)

    @blk.sync
    def _(sy):
        one(sy)

    @blk.tensor
    def _(t):
        one(t)

    @blk.scalar
    def _(sc):
        one(sc)

    @blk.vector
    def _(v):
        one(v)

    @blk.gpsimd
    def _(g):
        g.drain()
        g.sem_inc(ctr.h, 1)
        g.wait_ge(ctr.h, 5)
        for c in clear:
            g.sem_clear(c.h)
        g.sem_inc(ctr.h, 1)
    ctr.n = 6
    return ctr


def build_fused(upto=9):
    kb = KB("fused")
    nc = kb.nc
    ext = {}
    ext["xT"] = kb.dram("xT", [D, T], F32, "ExternalInput")
    for n in "qkv":
        ext["W" + n] = kb.dram("W" + n, [D, D], F32, "ExternalInput")
        ext["b" + n] = kb.dram("b" + n, [D], F32, "ExternalInput")
        ext["B" + n] = kb.dram("B" + n, [16, D], F32, "ExternalInput")
        ext["A" + n] = kb.dram("A" + n, [16, D], F32, "ExternalInput")
    ext["Wo"] = kb.dram("Wo", [D, D], F32, "ExternalInput")
    ext["bo"] = kb.dram("bo", [D], F32, "ExternalInput")
    ext["Bo"] = kb.dram("Bo", [16, D], F32, "ExternalInput")
    ext["Ao"] = kb.dram("Ao", [16, D], F32, "ExternalInput")
    ext["gate_w"] = kb.dram("gate_w", [D, INTER], F32, "ExternalInput")
    ext["up_w"] = kb.dram("up_w", [D, INTER], F32, "ExternalInput")
    ext["down_w"] = kb.dram("down_w", [INTER, D], F32, "ExternalInput")
    ext["gate_s"] = kb.dram("gate_s", [INTER], F32, "ExternalInput")
    ext["up_s"] = kb.dram("up_s", [INTER], F32, "ExternalInput")
    ext["down_s"] = kb.dram("down_s", [D], F32, "ExternalInput")
    ext["w_qkv"] = kb.dram("w_qkv", [D, 3 * D], F32, "ExternalInput")
    ext["w_c1"] = kb.dram("w_c1", [D, 512], F32, "ExternalInput")
    ext["w_c2"] = kb.dram("w_c2", [1, 512], F32, "ExternalInput")
    ext["w_out"] = kb.dram("w_out", [D, D], F32, "ExternalInput")
    ext["W_base"] = kb.dram("W_base", [D, D], F32, "ExternalInput")
    ext["b_base"] = kb.dram("b_base", [D], F32, "ExternalInput")
    ext["W_router"] = kb.dram("W_router", [D, 8], F32, "ExternalInput")
    ext["A_w"] = kb.dram("A_w", [D, 64], F32, "ExternalInput")
    ext["B_w"] = kb.dram("B_w", [8, D, 8], F32, "ExternalInput")
    ext["meta"] = kb.dram("meta", [1, 4], F32, "ExternalInput")
    ext["x2T"] = kb.dram("x2T", [D, T], F32, "ExternalOutput")
    for n in "qkv":
        ext[n + "T"] = kb.dram(n + "T", [D, T], BF16)
    ext["xnT"] = kb.dram("xnT", [D, T], BF16)
    ext["sT"] = kb.dram("sT", [D, T], BF16)
    ext["x1T"] = kb.dram("x1T", [D, T], F32)
    ext["q1T"] = kb.dram("q1T", [D, T], BF16)
    ext["k1T"] = kb.dram("k1T", [D, TH], BF16)
    ext["v1T"] = kb.dram("v1T", [D, TH], BF16)
    ext["xn1T"] = kb.dram("xn1T", [D, TH], BF16)
    ext["s1T"] = kb.dram("s1T", [D, T], BF16)
    ext["kTg"] = kb.dram("kTg", [NC * D, T], BF16)
    ext["vTg"] = kb.dram("vTg", [NC * D, T], BF16)
    ext["x1tail"] = kb.dram("x1tail", [D, HALO], F32)
    ext["x1tails_g"] = kb.dram("x1tails_g", [NC * D, HALO], F32)
    ext["x1halo"] = kb.dram("x1halo", [D, HALO], F32)
    ext["cc_stats_in"] = kb.dram("cc_stats_in", [128, 18], F32)
    ext["cc_stats_out"] = kb.dram("cc_stats_out", [128, 18], F32)
    ext["Wo_bf"] = kb.dram("Wo_bf", [D, D], BF16)
    ext["gate_bf"] = kb.dram("gate_bf", [D, INTER], BF16)
    ext["up_bf"] = kb.dram("up_bf", [D, INTER], BF16)
    ext["down_bf"] = kb.dram("down_bf", [INTER, D], BF16)
    ext["wq_bf"] = kb.dram("wq_bf", [D, 3 * D], BF16)
    ext["wc1_bf"] = kb.dram("wc1_bf", [D, 512], BF16)
    ext["wc2_bf"] = kb.dram("wc2_bf", [1, 512], BF16)
    ext["wout_bf"] = kb.dram("wout_bf", [D, D], BF16)
    ext["wbase_bf"] = kb.dram("wbase_bf", [D, D], BF16)
    ext["A_bf"] = kb.dram("A_bf", [D, 64], BF16)
    ext["B_bf"] = kb.dram("B_bf", [8, D, 8], BF16)

    blk = kb.block()
    F = {"kb": kb, "blk": blk, "ext": ext, "sems": {}}
    F["cast"] = {n: kb.sem_outer("cast_" + n)
                 for n in ["Wo", "Bo", "gate", "up", "down", "wq", "wc",
                           "wout", "wbase", "A", "B"]}

    cc_kv = kb.sem_outer("cc_kv")
    F["sems"]["cc_kv"] = cc_kv

    def _cc_hook():
        @blk.gpsimd
        def _(g):
            g.wait_ge(F["sems"]["l0pre_outp"][0].h, 512)
            g.wait_ge(F["sems"]["l0pre_outp"][1].h, 512)
            cc_kv.bump(g.collective_compute(
                "AllGather", ALU.bypass, replica_groups=[list(range(NC))],
                ins=[ext["kT"].opt()], outs=[ext["kTg"].opt()]))
            cc_kv.bump(g.collective_compute(
                "AllGather", ALU.bypass, replica_groups=[list(range(NC))],
                ins=[ext["vT"].opt()], outs=[ext["vTg"].opt()]))
    F["cc_hook"] = _cc_hook

    with kb.phase("p0"):
        build_l0pre(F=F)

        @blk.gpsimd
        def _(g):
            FC = F["cast"]
            FC["gate"].bump(g.dma_start(out=ext["gate_bf"], in_=ext["gate_w"]), 16)
            FC["up"].bump(g.dma_start(out=ext["up_bf"], in_=ext["up_w"]), 16)
            FC["Wo"].bump(g.dma_start(out=ext["Wo_bf"], in_=ext["Wo"]), 16)
            FC["down"].bump(g.dma_start(out=ext["down_bf"], in_=ext["down_w"]), 16)

        emit_phase_barrier(blk, kb, "b1", clear=kb.sem_track[-1])
    if upto <= 0:
        kb.ctx.close()
        return kb.nc
    F["sems"]["l0pre_outp"] = NullCtr()
    F["sems"]["l0pre_xn"] = NullCtr()
    with kb.phase("p1"):
        build_attn0(F=F)

        @blk.gpsimd
        def _(g):
            FC = F["cast"]
            FC["wq"].bump(g.dma_start(out=ext["wq_bf"], in_=ext["w_qkv"]), 16)
            FC["wc"].bump(g.dma_start(out=ext["wc1_bf"], in_=ext["w_c1"]), 16)
            FC["wc"].bump(g.dma_start(out=ext["wc2_bf"], in_=ext["w_c2"]), 16)
            FC["wout"].bump(g.dma_start(out=ext["wout_bf"], in_=ext["w_out"]), 16)
            FC["wbase"].bump(g.dma_start(out=ext["wbase_bf"], in_=ext["W_base"]), 16)
            FC["A"].bump(g.dma_start(out=ext["A_bf"], in_=ext["A_w"]), 16)
            FC["B"].bump(g.dma_start(out=ext["B_bf"], in_=ext["B_w"]), 16)
        emit_phase_barrier(blk, kb, "b2", clear=kb.sem_track[-1])
    if upto <= 1:
        kb.ctx.close()
        return kb.nc
    F["sems"]["attn0_outp"] = NullCtr()
    with kb.phase("p2"):
        build_l0post(F=F)
        emit_phase_barrier(blk, kb, "b2b", clear=kb.sem_track[-1])
    if upto <= 2:
        kb.ctx.close()
        return kb.nc
    F["sems"]["l0post_outp"] = NullCtr()

    cc_halo = kb.sem_outer("cc_halo")
    F["sems"]["cc_halo"] = cc_halo
    with kb.phase("halo"):
        hstg = kb.sb("hstg", [128, HALO], F32)
        hidxs = kb.sb("hidxs", [128, C], mybir.dt.int32)
        hidxf = kb.sb("hidxf", [128, C], F32)
        hbase = kb.sb("hbase", [128, 1], F32)
        hmsb = kb.sb("hmsb", [1, 4], F32)
        hmb = kb.sb("hmb", [1, 1], BF16)
        hones = kb.sb("hones", [1, 128], BF16)
        psH1 = kb.ps("psH1", [128, 1])
        s_hx = kb.sem("s_hx")
        s_hv2 = kb.sem("s_hv2")
        s_ht2 = kb.sem("s_ht2")

        @blk.gpsimd
        def _(g):
            s_hx.bump(g.memset(hones[:, :], 1.0))
            s_hx.bump(g.iota(hidxf[:, :], pattern=[[128, C]], base=0,
                             channel_multiplier=1,
                             allow_small_or_imprecise_dtypes=True))
            F["sems"]["l0post_outp"][0].wait(g)
            F["sems"]["l0post_outp"][1].wait(g)
            s_hx.bump(g.dma_start(out=ext["x1tail"], in_=ext["x1T"][:, T - HALO:T]), 16)
            g.wait_ge(s_hx.h, s_hx.n)
            cc_halo.bump(g.collective_compute(
                "AllGather", ALU.bypass, replica_groups=[list(range(NC))],
                ins=[ext["x1tail"].opt()], outs=[ext["x1tails_g"].opt()]))
            s_hx.bump(g.dma_start(out=hmsb[:, :], in_=ext["meta"]), 16)

        @blk.vector
        def _(v):
            v.wait_ge(s_hx.h, s_hx.n)
            s_hv2.bump(v.tensor_copy(out=hmb[:, :], in_=hmsb[0:1, 2:3]))

        @blk.tensor
        def _(t):
            s_hv2.wait(t)
            s_ht2.bump(t.matmul(psH1[:, 0:1], lhsT=hones[0:1, :], rhs=hmb[0:1, 0:1],
                                start=True, stop=True))

        @blk.vector
        def _(v):
            s_ht2.wait(v)
            s_hv2.bump(v.tensor_copy(out=hbase[:, :], in_=psH1[:, 0:1]))
            v.wait_ge(s_hv2.h, s_hv2.n)
            s_hv2.bump(v.tensor_scalar(hidxf[:, :], hidxf[:, :], hbase[:, :], None,
                                       ALU.add))
            v.wait_ge(s_hv2.h, s_hv2.n)
            s_hv2.bump(v.tensor_copy(out=hidxs[:, :], in_=hidxf[:, :]))
            v.wait_ge(s_hv2.h, s_hv2.n)

        @blk.gpsimd
        def _(g):
            g.wait_ge(s_hv2.h, s_hv2.n)
            g.wait_ge(cc_halo.h, 1)
            for c in range(C):
                s_hx.bump(g.memset(hstg[:, :], 0.0))
                g.wait_ge(s_hx.h, s_hx.n)
                s_hx.bump(g.indirect_dma_start(
                    out=hstg[:, :], out_offset=None,
                    in_=ext["x1tails_g"],
                    in_offset=bass.IndirectOffsetOnAxis(ap=hidxs[:, c:c + 1], axis=0)), 16)
                g.wait_ge(s_hx.h, s_hx.n)
                s_hx.bump(g.dma_start(out=ext["x1halo"][c * 128:(c + 1) * 128, :],
                                      in_=hstg[:, :]), 16)
                g.wait_ge(s_hx.h, s_hx.n)
            cc_halo.bump(g.nop())
            g.wait_ge(cc_halo.h, cc_halo.n)

        emit_phase_barrier(blk, kb, "b3", clear=kb.sem_track[-1])
    if upto <= 3:
        kb.ctx.close()
        return kb.nc

    F["sems"]["cc_halo"] = NullCtr()
    with kb.phase("p3"):
        build_l1pre(F=F)
        emit_phase_barrier(blk, kb, "b4", clear=kb.sem_track[-1])
    if upto <= 4:
        kb.ctx.close()
        return kb.nc
    F["sems"]["l1pre_outp"] = NullCtr()
    F["sems"]["l1pre_xn"] = NullCtr()
    with kb.phase("p4"):
        build_attn1(F=F)
        emit_phase_barrier(blk, kb, "b5", clear=kb.sem_track[-1])
    if upto <= 5:
        kb.ctx.close()
        return kb.nc
    F["sems"]["attn1_outp"] = NullCtr()
    with kb.phase("p5"):
        build_l1post(F=F)

        @blk.sync
        def _(sy):
            F["sems"]["l1post_outp"][0].wait(sy)
            F["sems"]["l1post_outp"][1].wait(sy)

    kb.ctx.close()
    return kb.nc


_builders = {}


def get_nc(name):
    if name not in _builders:
        _builders[name] = {"l0pre": build_l0pre, "attn0": build_attn0, "l0post": build_l0post, "l1pre": build_l1pre, "attn1": build_attn1, "l1post": build_l1post, "fused": build_fused}[name]()
    return _builders[name]


def kernel(**inputs):
    x = np.asarray(inputs["L0_attn_x"], np.float32)[0]
    nc = get_nc("fused")
    names_f32 = {
        "Wq": "L0_attn_Wq", "bq": "L0_attn_bq", "Bq": "L0_attn_Bq", "Aq": "L0_attn_Aq",
        "Wk": "L0_attn_Wk", "bk": "L0_attn_bk", "Bk": "L0_attn_Bk", "Ak": "L0_attn_Ak",
        "Wv": "L0_attn_Wv", "bv": "L0_attn_bv", "Bv": "L0_attn_Bv", "Av": "L0_attn_Av",
        "Wo": "L0_attn_Wo", "bo": "L0_attn_bo", "Bo": "L0_attn_Bo", "Ao": "L0_attn_Ao",
        "gate_w": "L0_ffn_gate_w", "up_w": "L0_ffn_up_w", "down_w": "L0_ffn_down_w",
        "gate_s": "L0_ffn_gate_s", "up_s": "L0_ffn_up_s", "down_s": "L0_ffn_down_s",
        "w_qkv": "L1_attn_w_qkv", "w_c1": "L1_attn_w_c1", "w_c2": "L1_attn_w_c2",
        "w_out": "L1_attn_w_out", "W_base": "L1_ffn_W_base", "b_base": "L1_ffn_b_base",
        "W_router": "L1_ffn_W_router", "A_w": "L1_ffn_A_w", "B_w": "L1_ffn_B_w",
    }
    transposed = {"Wq", "Wk", "Wv", "Wo", "W_base", "w_out", "w_qkv", "w_c1",
                  "gate_w", "up_w", "down_w",
                  "W_router", "Aq", "Ak", "Av", "Ao"}
    weights = {}
    for k, v in names_f32.items():
        a = np.asarray(inputs[v], np.float32)
        if k == "A_w":
            a = a.reshape(64, D).T
        elif k in transposed:
            a = a.T
        weights[k] = np.ascontiguousarray(a)
    maps = []
    for c in range(NC):
        m = dict(weights)
        m["xT"] = np.ascontiguousarray(x[c * T:(c + 1) * T].T)
        hoff = float((c - 1) * D) if c >= 1 else 0.0
        m["meta"] = np.array([[float(c * T - HALO), 0.0 if c == 0 else 1.0,
                               hoff, 0.0]], np.float32)
        maps.append(m)
    res = run_bass_kernel_spmd(nc, maps, core_ids=list(range(NC))).results
    x2 = np.concatenate([np.asarray(res[c]["x2T"], np.float32) for c in range(NC)],
                        axis=1)
    return np.ascontiguousarray(x2.T)[None, :, :]


def kernel_unfused(**inputs):

    x = np.asarray(inputs["L0_attn_x"], np.float32)[0]

    nc = get_nc("l0pre")
    maps = []
    for c in range(NC):
        m = {"xT": np.ascontiguousarray(x[c * T:(c + 1) * T].T)}
        for n in "qkv":
            m["W" + n] = np.asarray(inputs[f"L0_attn_W{n}"], np.float32)
            m["b" + n] = np.asarray(inputs[f"L0_attn_b{n}"], np.float32)
            m["B" + n] = np.asarray(inputs[f"L0_attn_B{n}"], np.float32)
            m["A" + n] = np.asarray(inputs[f"L0_attn_A{n}"], np.float32)
        maps.append(m)
    r1 = run_bass_kernel_spmd(nc, maps, core_ids=list(range(NC))).results
    qT = [np.asarray(r1[c]["qT"]) for c in range(NC)]
    kTf = np.ascontiguousarray(np.concatenate([np.asarray(r1[c]["kT"]) for c in range(NC)], axis=1))
    vTf = np.ascontiguousarray(np.concatenate([np.asarray(r1[c]["vT"]) for c in range(NC)], axis=1))
    xnT = [np.asarray(r1[c]["xnT"]) for c in range(NC)]

    nc = get_nc("attn0")
    maps = [{"qT": qT[c], "kTf": kTf, "vTf": vTf} for c in range(NC)]
    r2 = run_bass_kernel_spmd(nc, maps, core_ids=list(range(NC))).results
    sT = [np.asarray(r2[c]["sT"]) for c in range(NC)]

    nc = get_nc("l0post")
    maps = []
    for c in range(NC):
        maps.append({
            "xT": np.ascontiguousarray(x[c * T:(c + 1) * T].T),
            "xnT": xnT[c], "sT": sT[c],
            "Wo": np.asarray(inputs["L0_attn_Wo"], np.float32),
            "bo": np.asarray(inputs["L0_attn_bo"], np.float32),
            "Bo": np.asarray(inputs["L0_attn_Bo"], np.float32),
            "Ao": np.asarray(inputs["L0_attn_Ao"], np.float32),
            "gate_w": np.asarray(inputs["L0_ffn_gate_w"], np.float32),
            "up_w": np.asarray(inputs["L0_ffn_up_w"], np.float32),
            "down_w": np.asarray(inputs["L0_ffn_down_w"], np.float32),
            "gate_s": np.asarray(inputs["L0_ffn_gate_s"], np.float32),
            "up_s": np.asarray(inputs["L0_ffn_up_s"], np.float32),
            "down_s": np.asarray(inputs["L0_ffn_down_s"], np.float32),
        })
    r3 = run_bass_kernel_spmd(nc, maps, core_ids=list(range(NC))).results
    x1T = [np.ascontiguousarray(np.asarray(r3[c]["x1T"], np.float32)) for c in range(NC)]

    nc = get_nc("l1pre")
    maps = []
    metas = []
    for c in range(NC):
        xh = np.zeros((D, TH), np.float32)
        if c == 0:
            xh[:, HALO:] = x1T[0]
        else:
            xh[:, :HALO] = x1T[c - 1][:, T - HALO:]
            xh[:, HALO:] = x1T[c]
        meta = np.array([[float(c * T - HALO), 0.0 if c == 0 else 1.0, 0.0, 0.0]],
                        np.float32)
        metas.append(meta)
        maps.append({"x1Th": xh, "w_qkv": np.asarray(inputs["L1_attn_w_qkv"], np.float32),
                     "meta": meta})
    r4 = run_bass_kernel_spmd(nc, maps, core_ids=list(range(NC))).results
    q1T = [np.asarray(r4[c]["q1T"]) for c in range(NC)]
    k1T = [np.asarray(r4[c]["k1T"]) for c in range(NC)]
    v1T = [np.asarray(r4[c]["v1T"]) for c in range(NC)]
    xn1T = [np.asarray(r4[c]["xn1T"]) for c in range(NC)]
    xn1f = np.ascontiguousarray(np.concatenate([xn1T[c][:, HALO:] for c in range(NC)],
                                               axis=1))

    nc = get_nc("attn1")
    maps = [{"q1T": q1T[c], "k1T": k1T[c], "v1T": v1T[c], "xn1f": xn1f,
             "w_c1": np.asarray(inputs["L1_attn_w_c1"], np.float32),
             "w_c2": np.asarray(inputs["L1_attn_w_c2"], np.float32),
             "meta": metas[c]} for c in range(NC)]
    r5 = run_bass_kernel_spmd(nc, maps, core_ids=list(range(NC))).results
    s1T = [np.asarray(r5[c]["s1T"]) for c in range(NC)]

    nc = get_nc("l1post")
    maps = [{"x1T": x1T[c],
             "xnT": np.ascontiguousarray(xn1T[c][:, HALO:]),
             "s1T": s1T[c],
             "w_out": np.asarray(inputs["L1_attn_w_out"], np.float32),
             "W_base": np.asarray(inputs["L1_ffn_W_base"], np.float32),
             "b_base": np.asarray(inputs["L1_ffn_b_base"], np.float32),
             "W_router": np.asarray(inputs["L1_ffn_W_router"], np.float32),
             "A_w": np.asarray(inputs["L1_ffn_A_w"], np.float32),
             "B_w": np.asarray(inputs["L1_ffn_B_w"], np.float32)} for c in range(NC)]
    r6 = run_bass_kernel_spmd(nc, maps, core_ids=list(range(NC))).results
    x2 = np.concatenate([np.asarray(r6[c]["x2T"], np.float32) for c in range(NC)],
                        axis=1)
    return np.ascontiguousarray(x2.T)[None, :, :]


if __name__ == "__main__":
    print("kernel module ok")
